# version 09
# baseline (speedup 1.0000x reference)
import math
import numpy as np
import ml_dtypes
import concourse.bass as bass
import concourse.mybir as mybir
import bass_rust
from concourse.bass_utils import run_bass_kernel_spmd
from contextlib import ExitStack

F32 = mybir.dt.float32
BF16 = mybir.dt.bfloat16
AF = mybir.ActivationFunctionType
ALU = mybir.AluOpType
AX = mybir.AxisListType
BF = ml_dtypes.bfloat16

T = 4096
D = 1024
NH = 16
HD = 64
DL = 256
HID = 2048
NS = 4
QB = 512
NTO = NS * QB
CSTAB = -30.0
A_OFF = 3968
W2 = 8192
EPSR = 1.1920929e-07
ENGS = ("sync", "tensor", "vector", "scalar", "gpsimd")

_CW1 = 6.28125
_CW23 = float(np.float32(2.0 * math.pi - 6.28125))


class Sem:
    __slots__ = ("h", "n", "name")

    def __init__(self, h, name=""):
        self.h = h
        self.n = 0
        self.name = name


class P:
    def __init__(self, nc):
        self.nc = nc
        self.q = {e: [] for e in ENGS}
        self.es = ExitStack()

    def sb(self, name, shape, dt=F32, stack=None, side=None):
        kw = {"side": side} if side else {}
        return (stack or self.es).enter_context(
            self.nc.sbuf_tensor(name, shape, dt, **kw))

    def psum(self, name, shape, dt=F32, stack=None):
        return (stack or self.es).enter_context(self.nc.psum_tensor(name, shape, dt))

    def sem(self, name):
        return Sem(self.es.enter_context(self.nc.semaphore(name)), name)

    def op(self, eng, meth, *args, sem=None, d=1, **kw):
        if sem is not None:
            sem.n += d
            h, dd = sem.h, d

            def f(e, meth=meth, args=args, kw=kw, h=h, dd=dd):
                getattr(e, meth)(*args, **kw).then_inc(h, dd)
        else:
            def f(e, meth=meth, args=args, kw=kw):
                getattr(e, meth)(*args, **kw)
        self.q[eng].append(f)

    def w(self, eng, sem, val=None):
        v = sem.n if val is None else val
        h = sem.h
        self.q[eng].append(lambda e, h=h, v=v: e.wait_ge(h, v))

    def dma(self, eng, out, in_, sem=None, **kw):
        self.op(eng, "dma_start", sem=sem, d=16, out=out, in_=in_, **kw)

    def drain(self, eng="vector", sem=None):
        self.op(eng, "drain", sem=sem)

    def barrier(self, name, engs=ENGS):
        s = self.sem(name)
        for eng in engs:
            self.op(eng, "nop", sem=s)
        for eng in engs:
            self.w(eng, s)

    def build(self):
        with self.nc.Block() as block:
            for eng in ENGS:
                def body(e, eng=eng):
                    for fn in self.q[eng]:
                        fn(e)
                getattr(block, eng)(body)


def man_ap(base_ap, offset, pattern):
    return bass_rust.AP(tensor=base_ap.tensor, offset=offset, ap=pattern)


C_P, C_PI, C_PI2, C_NP, C_001, C_N001, C_N01, C_1E9, C_02, C_1E8, C_LN4 = range(11)
B_BQ, B_BKD, B_BKUK, B_BO, B_N1, B_QNW, B_KNW = 0, 8, 10, 18, 26, 34, 35


class MK:
    def __init__(self, debug=()):
        self.nc = bass.Bass("TRN2", target_bir_lowering=False)
        self.p = P(self.nc)
        self.debug = set(debug)
        self.DBG = {}
        self._declare_dram()
        self._declare_global_sbuf()
        p = self.p
        self.s_ld = p.sem("s_ld")
        self.s_gld = p.sem("s_gld")
        self.s_pe = p.sem("s_pe")
        self.s_act = p.sem("s_act")
        self.s_dve = p.sem("s_dve")
        self.s_pool = p.sem("s_pool")
        self.s_st = p.sem("s_st")

    def _declare_dram(self):
        nc = self.nc
        self.DI = {}

        def din(name, shape, dt=F32):
            ap = nc.dram_tensor(name, list(shape), dt, kind="ExternalInput").ap()
            self.DI[name] = ap
            return ap

        self.xfull = din("xfull", [T, D])
        self.xown = din("xown", [NTO, D])
        self.Wq = din("Wq", [D, D])
        self.Wkd = din("Wkd", [D, DL])
        self.Wku = din("Wku", [DL, 2 * D])
        self.Wo = din("Wo", [D, D])
        self.bq_pc = din("bq_pc", [128, 8])
        self.bkd_pc = din("bkd_pc", [128, 2])
        self.bkuk_pc = din("bkuk_pc", [128, 8])
        self.bkuv_row = din("bkuv_row", [1, D])
        self.bo_pc = din("bo_pc", [128, 8])
        self.Fwq = din("Fwq", [D, D])
        self.Fwk = din("Fwk", [D, D])
        self.Fwv = din("Fwv", [D, D])
        self.Fwo = din("Fwo", [D, D])
        self.n1w_pc = din("n1w_pc", [128, 8])
        self.n2w_row = din("n2w_row", [1, D])
        self.qnw_pc = din("qnw_pc", [128, 1])
        self.knw_pc = din("knw_pc", [128, 1])
        self.gwT = din("gwT", [D, 4])
        self.E0U = din("E0U", [D, HID])
        self.E0D = din("E0D", [HID, D])
        self.E1A = din("E1A", [D, HID])
        self.E1B = din("E1B", [D, HID])
        self.E1D = din("E1D", [HID, D])
        self.E2A = din("E2A", [D, HID])
        self.E2B = din("E2B", [HID, HID])
        self.E2C = din("E2C", [HID, HID])
        self.E2D = din("E2D", [HID, D])
        self.E3A = din("E3A", [D, D])
        self.E3B = din("E3B", [D, HID])
        self.E3C = din("E3C", [HID, D])
        self.thi = din("thi", [1, T])
        self.tlo = din("tlo", [1, T])
        self.thi_o = din("thi_o", [1, NTO])
        self.tlo_o = din("tlo_o", [1, NTO])
        self.i4r = din("i4r", [1, 128])
        self.kcol = din("kcol", [128, 32])
        self.qpos = din("qpos", [1, NTO])
        self.hsh = din("hsh", [128, 1])
        self.I16b = din("I16b", [128, 128], BF16)
        self.I32f = din("I32f", [128, 128], F32)
        self.PIb = din("PIb", [128, 128], BF16)
        self.sgn = din("sgn", [128, 1])
        self.sel2 = din("sel2", [2, 128])
        self.hsel = din("hsel", [128, 2])
        self.ltm16 = din("ltm16", [128, 16])
        self.sel4i = din("sel4", [4, 512])

        self.yout = nc.dram_tensor("yout", [NTO, D], F32, kind="ExternalOutput").ap()
        self.big2 = nc.dram_tensor("big2", [128, W2], BF16, kind="Internal").ap()
        self.Apark = nc.dram_tensor("Apark", [NS, 8, 128, QB], F32, kind="Internal").ap()
        self.qcapark = nc.dram_tensor("qcapark", [128, 8 * NTO], BF16, kind="Internal").ap()
        self.xnTpark = nc.dram_tensor("xnTpark", [128, 8 * T], BF16, kind="Internal").ap()
        self.xnopark = nc.dram_tensor("xnopark", [NTO, D], F32, kind="Internal").ap()
        self.x1park = nc.dram_tensor("x1park", [NTO, D], F32, kind="Internal").ap()
        self.aopark = nc.dram_tensor("aopark", [NS, 6, 128, QB], F32, kind="Internal").ap()

    def dbg(self, name, shape, dt=F32):
        if name not in self.debug:
            return None
        if name not in self.DBG:
            self.DBG[name] = self.nc.dram_tensor(
                name, list(shape), dt, kind="ExternalOutput").ap()
        return self.DBG[name]

    def dump(self, name, src_ap, shape, dt=F32):
        ap = self.dbg(name, shape, dt)
        if ap is not None:
            for s in (self.s_act, self.s_dve, self.s_pe, self.s_pool):
                self.p.w("sync", s)
            self.p.dma("sync", ap, src_ap, self.s_st)
            self.p.w("sync", self.s_st)

    def _declare_global_sbuf(self):
        p = self.p
        self.co = p.sb("co", [128, 16], F32)
        self.rstd1 = p.sb("rstd1", [128, 32], F32)
        self.ones1b = p.sb("ones1b", [128, 1], BF16)
        self.onesr_f = p.sb("onesr_f", [1, 128], F32)
        self.ones64_f = p.sb("ones64_f", [1, 64], F32)
        self.onesc_f = p.sb("onesc_f", [128, 1], F32)
        self.sel2b = p.sb("sel2b", [2, 128], F32)
        self.hselb = p.sb("hselb", [128, 2], F32)
        self.kcol_s = p.sb("kcol_s", [128, 32], F32)
        self.id16 = p.sb("id16", [128, 128], BF16)
        self.id32 = p.sb("id32", [128, 128], F32)
        self.pib = p.sb("pib", [128, 128], BF16)
        self.sgn_s = p.sb("sgn_s", [128, 1], F32)
        self.ltm_s = p.sb("ltm_s", [128, 16], F32)
        self.biases = p.sb("biases", [128, 40], F32)
        self.n2bc = p.sb("n2bc", [128, D], F32)
        self.i4row = p.sb("i4row", [1, 128], F32)
        self.ctmp1 = p.sb("c_tmp1", [1, 128], F32)
        self.ACTC_VALS = [1.0e-5, EPSR, 1.0, CSTAB, -1.0, 1.0 / 64.0,
                          1.0 / 1024.0, 0.5]
        self.actc = p.sb("actc", [128, len(self.ACTC_VALS)], F32)
        for i, v in enumerate(self.ACTC_VALS):
            self.nc.const_aps.register(F32, v, self.actc[:, i:i + 1])
        self.spcr = p.sb("spcr", [2, 64], F32)
        self.frow = p.sb("frow", [1, 128], F32)
        self.grow = p.sb("grow", [1, 128], F32)

    def cc(self, i):
        return self.co[:, i:i + 1]

    def cc1(self, i):
        return self.co[0:1, i:i + 1]

    def bcol(self, i):
        return self.biases[:, i:i + 1]

    def wsl(self, Wd, nout_total, j0, nj, nic):
        return man_ap(Wd, j0, [[nout_total, 128], [128 * nout_total, nic],
                               [1, nj]])

    def park_ap(self, base, s, C):
        return man_ap(base, s * C * 128 * QB,
                      [[QB, 128], [128 * QB, C], [1, QB]])

    def phase_consts(self):
        p, s_ld, s_dve = self.p, self.s_ld, self.s_dve
        co = self.co

        def consts(v):
            tp1 = float(np.float32(2 * np.pi))
            tp2 = float(np.float32(np.float64(2 * np.pi) - tp1))
            tp3 = float(np.float32(np.float64(2 * np.pi) - tp1 - tp2))
            v.memset(co[:, C_P:C_P + 1], tp1)
            v.drain()
            v.tensor_scalar_add(co[:, C_P:C_P + 1], co[:, C_P:C_P + 1], tp2)
            v.drain()
            v.tensor_scalar_add(co[:, C_P:C_P + 1], co[:, C_P:C_P + 1], tp3)
            v.drain()
            v.tensor_scalar_mul(co[:, C_PI:C_PI + 1], co[:, C_P:C_P + 1], 0.5)
            v.tensor_scalar_mul(co[:, C_PI2:C_PI2 + 1], co[:, C_P:C_P + 1], 0.25)
            v.tensor_scalar_mul(co[:, C_NP:C_NP + 1], co[:, C_P:C_P + 1], -1.0)
            v.memset(co[:, C_001:C_001 + 1], 100.0)
            v.memset(co[:, C_N01:C_N01 + 1], 10.0)
            v.memset(co[:, C_1E9:C_1E9 + 1], 1.0e9)
            v.memset(co[:, C_02:C_02 + 1], 5.0)
            v.memset(co[:, C_1E8:C_1E8 + 1], 1.0e8)
            v.drain()
            v.reciprocal(co[:, C_001:C_001 + 1], co[:, C_001:C_001 + 1])
            v.drain()
            v.reciprocal(co[:, C_N01:C_N01 + 1], co[:, C_N01:C_N01 + 1])
            v.drain()
            v.reciprocal(co[:, C_1E9:C_1E9 + 1], co[:, C_1E9:C_1E9 + 1])
            v.drain()
            v.reciprocal(co[:, C_02:C_02 + 1], co[:, C_02:C_02 + 1])
            v.drain()
            v.reciprocal(co[:, C_1E8:C_1E8 + 1], co[:, C_1E8:C_1E8 + 1])
            v.drain()
            v.tensor_scalar_mul(co[:, C_N001:C_N001 + 1], co[:, C_001:C_001 + 1], -1.0)
            v.tensor_scalar_mul(co[:, C_N01:C_N01 + 1], co[:, C_N01:C_N01 + 1], -1.0)
            L = np.float64(math.log(10000.0))
            L1 = float(np.float32(L))
            L2 = float(np.float32(L - np.float64(np.float32(L))))
            v.memset(co[:, C_LN4:C_LN4 + 1], L1)
            v.drain()
            v.tensor_scalar_add(co[:, C_LN4:C_LN4 + 1], co[:, C_LN4:C_LN4 + 1], L2)
            for i, val in enumerate(self.ACTC_VALS):
                v.memset(self.actc[:, i:i + 1], val)
            v.memset(self.spcr[:], 1.0)
            v.memset(self.ones1b[:], 1.0)
            v.memset(self.onesr_f[:], 1.0)
            v.memset(self.ones64_f[:], 1.0)
            v.memset(self.onesc_f[:], 1.0)
            v.drain()

        p.q["vector"].append(consts)
        p.op("vector", "nop", sem=s_dve)

        for dst, src in [(self.id16[:], self.I16b), (self.id32[:], self.I32f),
                         (self.pib[:], self.PIb), (self.sgn_s[:], self.sgn),
                         (self.sel2b[:], self.sel2), (self.hselb[:], self.hsel),
                         (self.kcol_s[:], self.kcol),
                         (self.ltm_s[:], self.ltm16), (self.i4row[:], self.i4r)]:
            p.dma("sync", dst, src, self.s_ld)
        b = self.biases
        p.dma("sync", b[:, 0:8], self.bq_pc, self.s_ld)
        p.dma("sync", b[:, 8:10], self.bkd_pc, self.s_ld)
        p.dma("sync", b[:, 10:18], self.bkuk_pc, self.s_ld)
        p.dma("sync", b[:, 18:26], self.bo_pc, self.s_ld)
        p.dma("sync", b[:, 26:34], self.n1w_pc, self.s_ld)
        p.dma("sync", b[:, 34:35], self.qnw_pc, self.s_ld)
        p.dma("sync", b[:, 35:36], self.knw_pc, self.s_ld)
        p.w("vector", self.s_ld)
        p.op("vector", "tensor_scalar_mul", self.bcol(B_QNW), self.bcol(B_QNW), 0.125)
        p.drain("vector", sem=s_dve)

        p.w("vector", s_dve)
        p.op("vector", "tensor_scalar",
             self.frow[:], self.i4row[:], self.cc1(C_LN4), -0.03125,
             op0=ALU.mult, op1=ALU.mult)
        p.drain("vector", sem=s_dve)
        p.w("scalar", s_dve)
        p.op("scalar", "activation", self.frow[:], self.frow[:], AF.Exp,
             sem=self.s_act)
        p.w("vector", self.s_act)
        p.op("vector", "tensor_scalar_mul", self.grow[:], self.frow[:], 64.0)
        p.drain("vector")
        tmp1 = self.ctmp1
        for j in range(3, -1, -1):
            thr = float(np.float32(2.0 * math.pi) * (1 << j))
            c1j = _CW1 * (1 << j)
            c23j = _CW23 * (1 << j)
            p.op("vector", "tensor_single_scalar", tmp1[:], self.grow[:], thr, ALU.is_ge)
            p.drain("vector")
            p.op("vector", "scalar_tensor_tensor", self.grow[:], tmp1[:], -c1j,
                 self.grow[:], op0=ALU.mult, op1=ALU.add)
            p.drain("vector")
            p.op("vector", "scalar_tensor_tensor", self.grow[:], tmp1[:], -c23j,
                 self.grow[:], op0=ALU.mult, op1=ALU.add)
            p.drain("vector")
        p.op("vector", "nop", sem=s_dve)
        self.dump("d_frow", self.frow[:], [1, 128], F32)
        self.dump("d_grow", self.grow[:], [1, 128], F32)
        self.dump("d_co", self.co[:], [128, 16], F32)

    def phase_big2(self):
        p = self.p
        st = ExitStack()
        dgb = p.sb("b2_dg", [128, 2048], F32, stack=st)
        t1b = p.sb("b2_t1", [128, 2048], F32, stack=st)
        t2b = p.sb("b2_t2", [128, 2048], F32, stack=st)
        wb = p.sb("b2_w", [128, 2048], BF16, stack=st)
        hshs = p.sb("b2_hsh", [128, 1], F32, stack=st)
        p.dma("sync", hshs[:], self.hsh, self.s_ld)
        for ch in range(4):
            base = -A_OFF + ch * 2048
            p.w("gpsimd", self.s_act)
            p.w("gpsimd", self.s_dve)
            p.op("gpsimd", "iota", dgb[:], [[1, 2048]], base=base,
                 channel_multiplier=-1, allow_small_or_imprecise_dtypes=True,
                 sem=self.s_pool)
            p.w("vector", self.s_pool)
            p.w("vector", self.s_ld)
            p.op("vector", "tensor_scalar_add", dgb[:], dgb[:], hshs[:], sem=self.s_dve)
            p.w("scalar", self.s_dve)
            p.op("scalar", "activation", t1b[:], dgb[:], AF.Relu)
            p.op("scalar", "activation", t2b[:], t1b[:], AF.Exp,
                 scale=self.cc(C_N001), sem=self.s_act)
            p.op("scalar", "activation", t1b[:], dgb[:], AF.Abs)
            p.op("scalar", "activation", t1b[:], t1b[:], AF.Ln, bias=1.0)
            p.op("scalar", "activation", t1b[:], t1b[:], AF.Exp,
                 scale=self.cc(C_N01), sem=self.s_act)
            p.w("vector", self.s_act)
            p.w("vector", self.s_st)
            p.op("vector", "tensor_scalar_add", t2b[:], t2b[:], self.cc(C_1E9))
            p.op("vector", "tensor_mul", wb[:], t1b[:], t2b[:], sem=self.s_dve)
            p.w("sync", self.s_dve)
            p.dma("sync", self.big2[:, ch * 2048:(ch + 1) * 2048], wb[:], self.s_st)
        self.dump("d_big2d", wb[:], [128, 2048], BF16)
        p.w("sync", self.s_st)
        st.close()

    def _cs_chunk(self, st_bufs, dst_cos, dst_sin, thi_src, tlo_src):
        p = self.p
        rbuf, rc, tmp, thr_, tlr_, psA = st_bufs
        p.w("sync", self.s_pe)
        p.dma("sync", thr_[:], thi_src, self.s_ld)
        p.dma("sync", tlr_[:], tlo_src, self.s_ld)
        p.w("tensor", self.s_ld)
        p.w("tensor", self.s_dve)
        p.op("tensor", "matmul", psA[:], self.grow[:], thr_[:], start=True, stop=False)
        p.op("tensor", "matmul", psA[:], self.frow[:], tlr_[:], start=False, stop=True,
             sem=self.s_pe)
        p.w("vector", self.s_pe)
        p.w("vector", self.s_act)
        p.op("vector", "tensor_copy", rbuf[:], psA[:], sem=self.s_dve)
        for j in range(6, -1, -1):
            thr = float(np.float32(2.0 * math.pi) * (1 << j))
            c1j = _CW1 * (1 << j)
            c23j = _CW23 * (1 << j)
            p.op("vector", "tensor_single_scalar", tmp[:], rbuf[:], thr, ALU.is_ge)
            p.op("vector", "scalar_tensor_tensor", rbuf[:], tmp[:], -c1j,
                 rbuf[:], op0=ALU.mult, op1=ALU.add)
            p.op("vector", "scalar_tensor_tensor", rbuf[:], tmp[:], -c23j,
                 rbuf[:], op0=ALU.mult, op1=ALU.add)
        p.op("vector", "tensor_single_scalar", tmp[:], rbuf[:], self.cc(C_PI), ALU.is_gt)
        p.op("vector", "scalar_tensor_tensor", rbuf[:], tmp[:], self.cc(C_NP),
             rbuf[:], op0=ALU.mult, op1=ALU.add)
        p.op("vector", "tensor_scalar_add", rc[:], rbuf[:], self.cc(C_PI2))
        p.op("vector", "tensor_single_scalar", tmp[:], rc[:], self.cc(C_PI), ALU.is_gt)
        p.op("vector", "scalar_tensor_tensor", rc[:], tmp[:], self.cc(C_NP),
             rc[:], op0=ALU.mult, op1=ALU.add, sem=self.s_dve)
        p.w("scalar", self.s_dve)
        p.op("scalar", "activation", dst_sin, rbuf[:], AF.Sin)
        p.op("scalar", "activation", dst_cos, rc[:], AF.Sin, sem=self.s_act)
        if getattr(self, "_cs_first", True):
            self._cs_first = False
            self.dump("d_rb0", rbuf[:], [128, 512], F32)
            self.dump("d_rc0", rc[:], [128, 512], F32)

    def phase_cs(self, CSo):
        p = self.p
        p.barrier("cs_start")
        st = ExitStack()
        rbuf = p.sb("cs_r", [128, 512], F32, stack=st)
        rc = p.sb("cs_rc", [128, 512], F32, stack=st)
        tmp = p.sb("cs_tmp", [128, 512], F32, stack=st)
        thr_ = p.sb("cs_thi", [1, 512], F32, stack=st)
        tlr_ = p.sb("cs_tlo", [1, 512], F32, stack=st)
        psA = p.psum("cs_ps", [128, 512], F32, stack=st)
        bufs = (rbuf, rc, tmp, thr_, tlr_, psA)
        for ch in range(8):
            c0 = ch * 512
            self._cs_chunk(bufs,
                           self.CS[:, c0:c0 + 512],
                           self.CS[:, T + c0:T + c0 + 512],
                           self.thi[:, c0:c0 + 512],
                           self.tlo[:, c0:c0 + 512])
        for ch in range(4):
            c0 = ch * 512
            self._cs_chunk(bufs,
                           CSo[:, c0:c0 + 512],
                           CSo[:, NTO + c0:NTO + c0 + 512],
                           self.thi_o[:, c0:c0 + 512],
                           self.tlo_o[:, c0:c0 + 512])
        self.dump("d_cs", self.CS[:, 0:4096], [128, 4096], BF16)
        self.dump("d_css", self.CS[:, T:T + 4096], [128, 4096], BF16)
        self.dump("d_rbuf", rbuf[:], [128, 512], F32)
        st.close()

    def phase_p0(self, st0, stXNT):
        p = self.p
        p.barrier("p0_start")
        self.xnT = p.sb("xnT", [128, 8 * T], BF16, stack=stXNT, side="right")
        self.xnTo = p.sb("xnTo", [128, 8 * NTO], BF16, stack=st0)
        st = ExitStack()
        xb = [p.sb(f"p0_x{i}", [128, D], F32, stack=st) for i in range(2)]
        sqt = p.sb("p0_sq", [128, D], F32, stack=st)
        xnf = p.sb("p0_xnf", [128, D], F32, stack=st)
        xnb = p.sb("p0_xnb", [128, D], BF16, stack=st)
        rows = p.sb("p0_rows", [128, 16], F32, stack=st)
        ptr = [p.psum(f"p0_pt{i}", [128, 128], BF16, stack=st) for i in range(2)]
        R_S, R_Q, R_M, R_V, R_NM, R_SD, R_RR, R_T1, R_S2 = range(9)

        apply_cnt = {}
        evac_cnt = {}

        def rcol(i):
            return rows[:, i:i + 1]

        def ln_tile(src_rows, gi, own, idx):
            bi = gi % 2
            if gi - 2 in apply_cnt:
                p.w("sync", self.s_dve, apply_cnt[gi - 2])
            p.dma("sync", xb[bi][:], src_rows, self.s_ld)
            p.w("scalar", self.s_ld)
            p.op("scalar", "activation", sqt[:], xb[bi][:], AF.Copy,
                 accum_out=rcol(R_S))
            p.op("scalar", "activation", sqt[:], xb[bi][:], AF.Square,
                 accum_out=rcol(R_Q), sem=self.s_act)
            p.w("vector", self.s_act)
            p.op("vector", "tensor_single_scalar", rcol(R_M), rcol(R_S),
                 1.0 / 1024.0, ALU.mult)
            p.op("vector", "tensor_single_scalar", rcol(R_V), rcol(R_Q),
                 1.0 / 1024.0, ALU.mult)
            p.drain("vector")
            p.op("vector", "tensor_scalar", rcol(R_NM), rcol(R_M), rcol(R_M),
                 -1.0, op0=ALU.mult, op1=ALU.mult)
            p.drain("vector")
            p.op("vector", "tensor_add", rcol(R_V), rcol(R_V), rcol(R_NM))
            p.drain("vector", sem=self.s_dve)
            p.w("scalar", self.s_dve)
            p.op("scalar", "activation", rcol(R_SD), rcol(R_V), AF.Ln,
                 bias=1.0e-5)
            p.op("scalar", "activation", rcol(R_T1), rcol(R_V), AF.Square)
            p.op("scalar", "activation", rcol(R_RR), rcol(R_SD), AF.Exp,
                 scale=-0.5, sem=self.s_act)
            p.w("vector", self.s_act)
            if not own:
                p.op("vector", "tensor_mul", rcol(R_T1), rcol(R_V), rcol(R_RR))
                p.drain("vector")
                p.op("vector", "tensor_mul", rcol(R_T1), rcol(R_T1), rcol(R_RR))
                p.drain("vector", sem=self.s_dve)
                p.w("scalar", self.s_dve)
                p.op("scalar", "activation", rcol(R_S2), rcol(R_T1), AF.Ln,
                     bias=EPSR)
                p.op("scalar", "activation", rcol(R_NM), rcol(R_T1), AF.Square)
                p.op("scalar", "activation", self.rstd1[:, idx:idx + 1],
                     rcol(R_S2), AF.Exp, scale=-0.5, sem=self.s_act)
                p.w("vector", self.s_act)
            if own:
                p.w("vector", self.s_st)
            p.op("vector", "tensor_scalar", xnf[:], xb[bi][:], rcol(R_M),
                 rcol(R_RR), op0=ALU.subtract, op1=ALU.mult, sem=self.s_dve)
            apply_cnt[gi] = self.s_dve.n
            p.op("vector", "tensor_copy", xnb[:], xnf[:], sem=self.s_dve)
            if own:
                p.w("sync", self.s_dve, apply_cnt[gi])
                p.dma("sync", self.xnopark[idx * 128:(idx + 1) * 128, :], xnf[:],
                      self.s_st)
            dstT = self.xnTo if own else self.xnT
            span = NTO if own else T
            p.w("tensor", self.s_dve)
            for c in range(8):
                k = gi * 8 + c
                if k >= 2:
                    p.w("tensor", self.s_dve, evac_cnt[k - 2])
                p.op("tensor", "transpose", ptr[k % 2][:],
                     xnb[:, c * 128:(c + 1) * 128], self.id16[:], sem=self.s_pe)
                p.w("vector", self.s_pe)
                p.op("vector", "tensor_copy",
                     dstT[:, c * span + idx * 128: c * span + idx * 128 + 128],
                     ptr[k % 2][:], sem=self.s_dve)
                evac_cnt[k] = self.s_dve.n

        gi = 0
        for i in range(32):
            ln_tile(self.xfull[i * 128:(i + 1) * 128, :], gi, False, i)
            gi += 1
        for i in range(16):
            ln_tile(self.xown[i * 128:(i + 1) * 128, :], gi, True, i)
            gi += 1
        self.dump("d_xnT", self.xnT[:, 0:4096], [128, 4096], BF16)
        self.dump("d_xnTo", self.xnTo[:, 0:2048], [128, 2048], BF16)
        self.dump("d_rstd1", self.rstd1[:], [128, 32], F32)
        st.close()

    def phase_bcasts(self):
        p = self.p
        p.barrier("bcast_start")
        st = ExitStack()
        row = p.sb("bc_row", [1, D], F32, stack=st)
        psb = p.psum("bc_ps", [128, 512], F32, stack=st)
        for nm, srow, dst in [("bv", self.bkuv_row, self.bvbc),
                              ("n2", self.n2w_row, self.n2bc)]:
            p.w("sync", self.s_pe)
            p.dma("sync", row[:], srow, self.s_ld)
            for h in range(2):
                p.w("tensor", self.s_ld)
                p.w("tensor", self.s_act)
                p.op("tensor", "matmul", psb[:], self.onesr_f[:],
                     row[:, h * 512:(h + 1) * 512], start=True, stop=True,
                     sem=self.s_pe)
                p.w("scalar", self.s_pe)
                p.op("scalar", "activation", dst[:, h * 512:(h + 1) * 512],
                     psb[:], AF.Copy, sem=self.s_act)
        p.barrier("bc_done")
        st.close()

    def phase_latent(self):
        p = self.p
        p.barrier("lat_start")
        st = ExitStack()
        wkdb = p.sb("wkdb", [128, 8 * DL], BF16, stack=st)
        pp = [p.psum(f"lat_ps{i}", [128, 512], F32, stack=st) for i in range(2)]
        p.dma("gpsimd", wkdb[:],
              self.wsl(self.Wkd, DL, 0, DL, 8), self.s_gld)
        p.w("tensor", self.s_gld)
        p.w("tensor", self.s_dve)
        evac = {}
        k = 0
        for jc in range(2):
            for tb in range(8):
                b = k % 2
                if k >= 2:
                    p.w("tensor", self.s_dve, evac[k - 2])
                for ic in range(8):
                    p.op("tensor", "matmul", pp[b][:],
                         wkdb[:, ic * DL + jc * 128: ic * DL + (jc + 1) * 128],
                         self.xnT[:, ic * T + tb * 512: ic * T + tb * 512 + 512],
                         start=(ic == 0), stop=(ic == 7),
                         sem=self.s_pe if ic == 7 else None)
                p.w("vector", self.s_pe)
                p.op("vector", "tensor_scalar_add",
                     self.latT[:, jc * T + tb * 512: jc * T + tb * 512 + 512],
                     pp[b][:], self.bcol(B_BKD + jc), sem=self.s_dve)
                evac[k] = self.s_dve.n
                k += 1
        self.dump("d_latT", self.latT[:, 0:4096], [128, 4096], BF16)
        self.dump("d_wkdb", wkdb[:], [128, 2048], BF16)
        self.dump("d_xnT2", self.xnT[:, 0:4096], [128, 4096], BF16)
        p.w("sync", self.s_dve)
        p.dma("sync", self.xnTpark, self.xnT[:], self.s_st)
        p.w("sync", self.s_st)
        p.barrier("lat_done")
        st.close()

    def phase_qprod(self, CSo, st01):
        p = self.p
        p.barrier("qprod_start")
        st = ExitStack()
        wqb = p.sb("wqb", [128, 8 * D], BF16, stack=st)
        fchunk = p.sb("fwq_c", [128, D], F32, stack=st)
        fwqb = p.sb("fwqb", [128, 8 * D], BF16, stack=st)
        qtile = [p.sb(f"qtile{i}", [128, 512], BF16, stack=st) for i in range(2)]
        q1f = p.sb("q1f", [128, 512], F32, stack=st)
        qb1 = p.sb("qb1", [128, 512], BF16, stack=st)
        t1f = p.sb("t1f", [128, 512], F32, stack=st)
        t2f = p.sb("t2f", [128, 512], F32, stack=st)
        sqf = p.sb("sqf", [128, 512], F32, stack=st)
        sqfB = p.sb("sqfB", [128, 512], F32, stack=st)
        ssrow = p.sb("ssrow", [2, 512], F32, stack=st)
        ssrB = p.sb("qp_ssrB", [2, 512], F32, stack=st)
        ssl = p.sb("qp_ssl", [2, 512], F32, stack=st)
        sslB = p.sb("qp_sslB", [2, 512], F32, stack=st)
        bsb = p.sb("bsb", [128, 512], F32, stack=st)
        bsbB = p.sb("bsbB", [128, 512], F32, stack=st)
        q1f2 = [q1f, p.sb("q1fB", [128, 512], F32, stack=st)]
        qb12 = [qb1, p.sb("qb1B", [128, 512], BF16, stack=st)]
        pbank = [p.psum(f"qp_pb{i}", [128, 512], F32, stack=st) for i in range(7)]
        pa = pbank[0:2]
        pPq = pbank[3:5]

        p.dma("gpsimd", wqb[:],
              self.wsl(self.Wq, D, 0, D, 8), self.s_gld)
        for ic in range(8):
            p.w("sync", self.s_dve)
            p.dma("sync", fchunk[:], self.Fwq[ic * 128:(ic + 1) * 128, :], self.s_ld)
            p.w("vector", self.s_ld)
            p.op("vector", "tensor_scalar_mul", fwqb[:, ic * D:(ic + 1) * D],
                 fchunk[:], self.bcol(B_N1 + ic), sem=self.s_dve)
        p.w("tensor", self.s_gld)
        p.w("tensor", self.s_dve)

        evacq, dve_cp, pe_mm, pe_pi = {}, {}, {}, {}
        for k in range(33):
            if k >= 1:
                j = k - 1
                p.w("tensor", self.s_dve, dve_cp[j])
                p.op("tensor", "matmul", pPq[j % 2][:], self.pib[:], qb12[j % 2][:],
                     start=True, stop=True, sem=self.s_pe)
                pe_pi[j] = self.s_pe.n
            if k < 32:
                jc, tb = divmod(k, 4)
                if k >= 2:
                    p.w("tensor", self.s_dve, evacq[k - 2])
                for ic in range(8):
                    p.op("tensor", "matmul", pa[k % 2][:],
                         wqb[:, ic * D + jc * 128: ic * D + (jc + 1) * 128],
                         self.xnTo[:, ic * NTO + tb * 512: ic * NTO + tb * 512 + 512],
                         start=(ic == 0), stop=(ic == 7),
                         sem=self.s_pe if ic == 7 else None)
                pe_mm[k] = self.s_pe.n
            if k >= 1:
                j = k - 1
                jcj, tbj = divmod(j, 4)
                p.op("vector", "tensor_mul", t1f[:], q1f2[j % 2][:],
                     CSo[:, tbj * 512: tbj * 512 + 512])
            if k < 32:
                jc, tb = divmod(k, 4)
                p.w("vector", self.s_pe, pe_mm[k])
                p.op("vector", "tensor_scalar", q1f2[k % 2][:], pa[k % 2][:],
                     self.bcol(B_BQ + jc), 0.125, op0=ALU.add, op1=ALU.mult,
                     sem=self.s_dve)
                evacq[k] = self.s_dve.n
                p.op("vector", "tensor_copy", qb12[k % 2][:], q1f2[k % 2][:],
                     sem=self.s_dve)
                dve_cp[k] = self.s_dve.n
            if k >= 1:
                j = k - 1
                jcj, tbj = divmod(j, 4)
                p.w("vector", self.s_pe, pe_pi[j])
                p.op("vector", "scalar_tensor_tensor", t2f[:], pPq[j % 2][:],
                     self.sgn_s[:], CSo[:, NTO + tbj * 512: NTO + tbj * 512 + 512],
                     op0=ALU.mult, op1=ALU.mult)
                p.op("vector", "tensor_add",
                     self.qsp[:, jcj * NTO + tbj * 512: jcj * NTO + tbj * 512 + 512],
                     t1f[:], t2f[:], sem=self.s_dve)

        ring = [pbank[0], pbank[1], pbank[2]]
        sqf2 = [sqf, sqfB]
        bsb2 = [bsb, bsbB]
        ssl2 = [ssl, sslB]
        ssrow2 = [ssrow, ssrB]
        evac, stcnt = {}, {}
        pe_mm, pe_hs, pe_bc = {}, {}, {}
        act_sq, act_ex, act_cp = {}, {}, {}
        p.w("tensor", self.s_dve)
        for k in range(32 + 2):
            if k < 32:
                jc, tb = divmod(k, 4)
                if k >= 3:
                    p.w("tensor", self.s_dve, evac[k - 3])
                for ic in range(8):
                    p.op("tensor", "matmul", ring[k % 3][:],
                         fwqb[:, ic * D + jc * 128: ic * D + (jc + 1) * 128],
                         self.xnTo[:, ic * NTO + tb * 512: ic * NTO + tb * 512 + 512],
                         start=(ic == 0), stop=(ic == 7),
                         sem=self.s_pe if ic == 7 else None)
                pe_mm[k] = self.s_pe.n
            if 1 <= k <= 32:
                j = k - 1
                p.w("tensor", self.s_act, act_sq[j])
                p.op("tensor", "matmul", pbank[3 + j % 2][0:2, :], self.hselb[:],
                     sqf2[j % 2][:], start=True, stop=True, sem=self.s_pe)
                pe_hs[j] = self.s_pe.n
            if k >= 2:
                i2 = k - 2
                p.w("tensor", self.s_act, act_ex[i2])
                p.op("tensor", "matmul", pbank[5 + i2 % 2][:],
                     self.sel2b[:], ssrow2[i2 % 2][:],
                     start=True, stop=True, sem=self.s_pe)
                pe_bc[i2] = self.s_pe.n
            if k < 32:
                p.w("scalar", self.s_pe, pe_mm[k])
                p.op("scalar", "activation", sqf2[k % 2][:],
                     ring[k % 3][:], AF.Square, sem=self.s_act)
                act_sq[k] = self.s_act.n
            if 1 <= k <= 32:
                j = k - 1
                p.w("scalar", self.s_pe, pe_hs[j])
                p.op("scalar", "activation", ssl2[j % 2][:],
                     pbank[3 + j % 2][0:2, :], AF.Ln,
                     scale=1.0 / 64.0, bias=EPSR)
                p.op("scalar", "activation", self.spcr[:], self.spcr[:],
                     AF.Square)
                p.op("scalar", "activation", ssrow2[j % 2][:],
                     ssl2[j % 2][:], AF.Exp, scale=-0.5, sem=self.s_act)
                act_ex[j] = self.s_act.n
            if k >= 2:
                i2 = k - 2
                jc2, tb2 = divmod(i2, 4)
                p.w("scalar", self.s_pe, pe_bc[i2])
                if i2 >= 2:
                    p.w("scalar", self.s_dve, evac[i2 - 2])
                p.op("scalar", "activation", bsb2[i2 % 2][:],
                     pbank[5 + i2 % 2][:], AF.Copy, sem=self.s_act)
                act_cp[i2] = self.s_act.n
                p.w("vector", self.s_act, act_cp[i2])
                if i2 >= 2:
                    p.w("vector", self.s_st, stcnt[i2 - 2])
                p.op("vector", "scalar_tensor_tensor", qtile[i2 % 2][:],
                     ring[i2 % 3][:], self.bcol(B_QNW), bsb2[i2 % 2][:],
                     op0=ALU.mult, op1=ALU.mult, sem=self.s_dve)
                evac[i2] = self.s_dve.n
                p.w("sync", self.s_dve, evac[i2])
                p.dma("sync",
                      self.qcapark[:, jc2 * NTO + tb2 * 512: jc2 * NTO + tb2 * 512 + 512],
                      qtile[i2 % 2][:], self.s_st)
                stcnt[i2] = self.s_st.n
        self.dump("d_qsp", self.qsp[:, 0:2048], [128, 2048], BF16)
        p.w("sync", self.s_st)
        p.barrier("qprod_done")
        st.close()

    def phase_p1(self):
        import os
        NG = int(os.environ.get("MK_P1_G", "2"))
        NSL = int(os.environ.get("MK_P1_S", "4"))
        SKIP = int(os.environ.get("MK_P1_SKIP", "0"))
        p = self.p
        p.barrier("p1_start")
        st = ExitStack()
        KTg = p.sb("KTg", [128, 4 * T], BF16, stack=st)
        Vg = p.sb("Vg", [128, 32 * 520], BF16, stack=st)
        wq_ = [p.sb(f"wq_{i}", [128, 2 * 512], BF16, stack=st) for i in range(2)]
        Astage = p.sb("Astage", [128, 8 * 512], F32, stack=st)
        OT = p.sb("OT", [128, 4 * 512], BF16, stack=st)
        pT = [p.sb(f"pT{i}", [128, 1024], BF16, stack=st) for i in range(3)]
        ktmp = pT[0][:, 0:512]
        scr = p.sb("p1_scr", [128, 1024], F32, stack=st)
        t1f = scr[:, 0:512]
        t2f = scr[:, 512:1024]
        mkf4 = p.sb("p1_mkf", [128, 4], F32, stack=st)
        linv2 = p.sb("p1_linv", [1, 1024], F32, stack=st)
        sig2 = p.sb("p1_sig", [1, 1024], F32, stack=st)
        lrow2 = p.sb("p1_lrow", [1, 1024], F32, stack=st)
        sig2s = lrow2
        ouf2 = scr[0:64, :]
        wkuk = p.sb("wkuk", [128, 2 * 512], BF16, stack=st)
        wkuv = p.sb("wkuv", [128, 2 * 512], BF16, stack=st)
        wob = p.sb("wob", [128, 4 * D], BF16, stack=st)
        pS01 = p.psum("pS01", [128, 1024], F32, stack=st)
        pS23 = p.psum("pS23", [128, 1024], F32, stack=st)
        pF = p.psum("pF", [128, 1024], F32, stack=st)
        pSb = [pS01, pS23, pF]
        pO = [p.psum(f"pO{i}", [128, 512], F32, stack=st) for i in range(2)]

        s_ld, s_gld, s_pe, s_act, s_dve, s_st = (self.s_ld, self.s_gld, self.s_pe,
                                                 self.s_act, self.s_dve, self.s_st)
        CSc = self.CS[:, 0:T]
        CSs = self.CS[:, T:2 * T]

        for g in range(NG):
            p.w("gpsimd", s_pe)
            p.dma("gpsimd", wkuk[:],
                  self.wsl(self.Wku, 2 * D, g * 512, 512, 2), s_gld)
            p.dma("gpsimd", wkuv[:],
                  self.wsl(self.Wku, 2 * D, D + g * 512, 512, 2), s_gld)
            p.dma("gpsimd", wob[:],
                  man_ap(self.Wo, g * 512 * D, [[D, 128], [128 * D, 4], [1, D]]), s_gld)
            p.w("tensor", s_gld)
            ktmp2 = [pT[0][:, 0:512], pT[0][:, 512:1024]]
            pPi2 = [pF[:, 0:512], pF[:, 512:1024]]
            evac, dve_kt, pe_mm, pe_pi = {}, {}, {}, {}
            for k in range(33):
                if k >= 1:
                    j = k - 1
                    p.w("tensor", s_dve, dve_kt[j])
                    p.op("tensor", "matmul", pPi2[j % 2], self.pib[:],
                         ktmp2[j % 2], start=True, stop=True, sem=s_pe)
                    pe_pi[j] = s_pe.n
                if k < 32:
                    ci, tb = divmod(k, 8)
                    if k >= 2:
                        p.w("tensor", s_dve, evac[k - 2])
                    for lc in range(2):
                        p.op("tensor", "matmul", pO[k % 2][:],
                             wkuk[:, lc * 512 + ci * 128: lc * 512 + (ci + 1) * 128],
                             self.latT[:, lc * T + tb * 512: lc * T + tb * 512 + 512],
                             start=(lc == 0), stop=(lc == 1),
                             sem=s_pe if lc == 1 else None)
                    pe_mm[k] = s_pe.n
                if k >= 1:
                    j = k - 1
                    cj, tbj = divmod(j, 8)
                    bcolj = self.bcol(B_BKUK + g * 4 + cj)
                    p.op("vector", "scalar_tensor_tensor", t1f, pO[j % 2][:],
                         bcolj, CSc[:, tbj * 512: tbj * 512 + 512],
                         op0=ALU.add, op1=ALU.mult)
                if k < 32:
                    ci, tb = divmod(k, 8)
                    p.w("vector", s_pe, pe_mm[k])
                    p.op("vector", "tensor_scalar_add", ktmp2[k % 2], pO[k % 2][:],
                         self.bcol(B_BKUK + g * 4 + ci), sem=s_dve)
                    dve_kt[k] = s_dve.n
                if k >= 1:
                    j = k - 1
                    cj, tbj = divmod(j, 8)
                    p.w("vector", s_pe, pe_pi[j])
                    p.op("vector", "scalar_tensor_tensor", t2f, pPi2[j % 2],
                         self.sgn_s[:], CSs[:, tbj * 512: tbj * 512 + 512],
                         op0=ALU.mult, op1=ALU.mult)
                    p.op("vector", "tensor_add",
                         KTg[:, cj * T + tbj * 512: cj * T + tbj * 512 + 512],
                         t1f, t2f, sem=s_dve)
                    evac[j] = s_dve.n
            for ci in range(4):
                p.op("vector", "reduce_sum", mkf4[:, ci:ci + 1],
                     KTg[:, ci * T:(ci + 1) * T], AX.X)
            p.drain("vector")
            p.op("vector", "tensor_single_scalar", self.mk8[:, 0:4],
                 mkf4[:], 8.0 / 4096.0, ALU.mult)
            p.drain("vector", sem=s_dve)
            p.op("vector", "memset",
                 man_ap(Vg[:, :], 64, [[32 * 520, 128], [520, 32], [65, 8]]), 1.0)
            p.drain("vector", sem=s_dve)
            p.w("tensor", s_dve)
            evac, pe_mm = {}, {}
            for tt in range(33):
                if tt < 32:
                    if tt >= 2:
                        p.w("tensor", s_dve, evac[tt - 2])
                    for lc in range(2):
                        p.op("tensor", "matmul", pPi2[tt % 2],
                             self.latT[:, lc * T + tt * 128: lc * T + (tt + 1) * 128],
                             wkuv[:, lc * 512:(lc + 1) * 512],
                             start=(lc == 0), stop=(lc == 1),
                             sem=s_pe if lc == 1 else None)
                    pe_mm[tt] = s_pe.n
                if tt >= 1:
                    j = tt - 1
                    p.w("vector", s_pe, pe_mm[j])
                    p.op("vector", "tensor_add",
                         man_ap(Vg[:, :], j * 520, [[32 * 520, 128], [65, 8], [1, 64]]),
                         man_ap(pF[:, :], (j % 2) * 512, [[1024, 128], [64, 8], [1, 64]]),
                         man_ap(self.bvbc[:, :], g * 512, [[D, 128], [64, 8], [1, 64]]),
                         sem=s_dve)
                    evac[j] = s_dve.n
            for hl in range(8):
                p.w("tensor", s_dve)
                p.w("tensor", s_act)
                for tt in range(32):
                    p.op("tensor", "matmul", pS01[0:1, 0:65], self.ones1b[:],
                         Vg[:, tt * 520 + hl * 65: tt * 520 + hl * 65 + 65],
                         start=(tt == 0), stop=(tt == 31),
                         sem=s_pe if tt == 31 else None)
                p.w("scalar", s_pe)
                p.op("scalar", "activation",
                     self.mv02[0:1, (g * 8 + hl) * 64:(g * 8 + hl) * 64 + 64],
                     pS01[0:1, 0:64], AF.Copy, sem=s_act)
            p.w("vector", s_act)
            p.op("vector", "tensor_scalar",
                 self.mv02[0:1, g * 512:(g + 1) * 512],
                 self.mv02[0:1, g * 512:(g + 1) * 512],
                 self.co[0:1, C_02:C_02 + 1], 1.0 / 4096.0,
                 op0=ALU.mult, op1=ALU.mult)
            p.drain("vector", sem=s_dve)

            for s in range(NSL):
                segq = list(range(16)) * 4
                ldcnt = {}
                segwm = {}

                def load_seg(seg, s=s, segq=segq, ldcnt=ldcnt, segwm=segwm):
                    qtr = segq[seg]
                    base = A_OFF + 1024 * s - 128 * (2 * qtr + 1)
                    if seg >= 2:
                        p.w("sync", s_dve, segwm[seg - 2])
                    else:
                        p.w("sync", s_dve)
                    p.dma("sync", wq_[seg % 2][:],
                          man_ap(self.big2, base, [[W2, 128], [128, 2], [1, 512]]),
                          s_ld)
                    ldcnt[seg] = s_ld.n

                load_seg(0)
                ou_act = 0
                fin_dve = 0
                prev_expd = None
                for pq in range(4):
                    if SKIP & 1:
                        break
                    ci = pq
                    cq = g * 4 + ci
                    h0 = pq * 2
                    expd = {}
                    wmulcnt = {}
                    vcnt = {}

                    def emit_v(kt, pq=pq, h0=h0, vcnt=vcnt, wmulcnt=wmulcnt):
                        p.w("tensor", s_dve, wmulcnt[kt])
                        if kt == 0:
                            p.w("tensor", s_dve, fin_dve)
                            p.w("tensor", s_act, ou_act)
                        b3 = kt % 3
                        p.op("tensor", "matmul", pO[0][0:65, :],
                             Vg[:, kt * 520 + h0 * 65: kt * 520 + h0 * 65 + 65],
                             pT[b3][:, 0:512],
                             start=(kt == 0), stop=(kt == 31))
                        p.op("tensor", "matmul", pO[1][0:65, :],
                             Vg[:, kt * 520 + (h0 + 1) * 65: kt * 520 + (h0 + 1) * 65 + 65],
                             pT[b3][:, 512:1024],
                             start=(kt == 0), stop=(kt == 31), sem=s_pe)
                        vcnt[kt] = s_pe.n

                    for kt in range(32):
                        seg = pq * 16 + kt // 2
                        if kt % 2 == 0 and seg + 1 < 64:
                            load_seg(seg + 1)
                        r = 1 - (kt % 2)
                        wtile = wq_[seg % 2][:, r * 512:(r + 1) * 512]
                        pS = pSb[kt % 3]
                        if kt >= 3:
                            p.w("tensor", s_act, expd[kt - 3])
                        elif pq == 0:
                            p.w("tensor", s_act)
                            p.w("tensor", s_dve)
                        else:
                            p.w("tensor", s_act, ou_act)
                        p.op("tensor", "matmul", pS[:, 0:512],
                             KTg[0:64, ci * T + kt * 128: ci * T + kt * 128 + 128],
                             self.qsp[0:64, cq * NTO + s * 512: cq * NTO + s * 512 + 512],
                             start=True, stop=True, tile_position=(0, 0))
                        p.op("tensor", "matmul", pS[:, 512:1024],
                             KTg[64:128, ci * T + kt * 128: ci * T + kt * 128 + 128],
                             self.qsp[64:128, cq * NTO + s * 512: cq * NTO + s * 512 + 512],
                             start=True, stop=True, tile_position=(64, 0),
                             sem=s_pe)
                        p.w("scalar", s_pe)
                        if kt >= 3:
                            p.w("scalar", s_pe, vcnt[kt - 3])
                        p.op("scalar", "activation", pT[kt % 3][:], pS[:],
                             AF.Exp, bias=CSTAB, sem=s_act)
                        expd[kt] = s_act.n
                        p.w("vector", s_act)
                        p.w("vector", s_ld, ldcnt[seg])
                        p.op("vector", "tensor_mul", pT[kt % 3][:, 0:512],
                             pT[kt % 3][:, 0:512], wtile)
                        p.op("vector", "tensor_mul", pT[kt % 3][:, 512:1024],
                             pT[kt % 3][:, 512:1024], wtile, sem=s_dve)
                        wmulcnt[kt] = s_dve.n
                        if kt % 2 == 1:
                            segwm[seg] = s_dve.n
                        if kt >= 2:
                            emit_v(kt - 2)
                    emit_v(30)
                    emit_v(31)
                    prev_expd = expd
                    if SKIP & 2:
                        continue
                    p.w("scalar", s_pe, vcnt[31])
                    p.w("scalar", s_dve, fin_dve)
                    p.op("scalar", "activation", lrow2[0:1, 0:512],
                         pO[0][64:65, :], AF.Copy)
                    p.op("scalar", "activation", lrow2[0:1, 512:1024],
                         pO[1][64:65, :], AF.Copy)
                    p.op("scalar", "activation", ouf2[:, 0:512], pO[0][0:64, :],
                         AF.Copy)
                    p.op("scalar", "activation", lrow2[0:1, 0:1024],
                         lrow2[0:1, 0:1024], AF.Ln)
                    p.op("scalar", "activation", ouf2[:, 512:1024],
                         pO[1][0:64, :], AF.Copy)
                    p.op("scalar", "activation", linv2[0:1, 0:1024],
                         lrow2[0:1, 0:1024], AF.Exp, scale=-1.0, sem=s_act)
                    p.w("tensor", s_act)
                    p.op("tensor", "matmul", pO[0][0:1, :],
                         self.mk8[0:64, ci:ci + 1],
                         self.qsp[0:64, cq * NTO + s * 512: cq * NTO + s * 512 + 512],
                         start=True, stop=True, tile_position=(0, 0))
                    p.op("tensor", "matmul", pO[1][0:1, :],
                         self.mk8[64:128, ci:ci + 1],
                         self.qsp[64:128, cq * NTO + s * 512: cq * NTO + s * 512 + 512],
                         start=True, stop=True, tile_position=(64, 0), sem=s_pe)
                    p.w("scalar", s_pe)
                    p.op("scalar", "activation", sig2[0:1, 0:512],
                         pO[0][0:1, :], AF.Exp, scale=-1.0)
                    p.op("scalar", "activation", sig2[0:1, 512:1024],
                         pO[1][0:1, :], AF.Exp, scale=-1.0)
                    p.op("scalar", "activation", sig2s[0:1, 0:512],
                         sig2[0:1, 0:512], AF.Ln, bias=1.0)
                    p.op("scalar", "activation", sig2s[0:1, 512:1024],
                         sig2[0:1, 512:1024], AF.Ln, bias=1.0)
                    p.op("scalar", "activation", sig2[0:1, 0:512],
                         sig2s[0:1, 0:512], AF.Exp, scale=-1.0)
                    p.op("scalar", "activation", sig2[0:1, 512:1024],
                         sig2s[0:1, 512:1024], AF.Exp, scale=-1.0, sem=s_act)
                    p.w("tensor", s_act)
                    gh = g * 8 + h0
                    p.op("tensor", "matmul", pO[0][0:64, :], self.ones64_f[:],
                         linv2[0:1, 0:512], start=True, stop=True)
                    p.op("tensor", "matmul", pO[0][64:128, :], self.ones64_f[:],
                         linv2[0:1, 512:1024], start=True, stop=True,
                         tile_position=(0, 64))
                    p.op("tensor", "matmul", pO[1][0:64, :],
                         self.mv02[0:1, gh * 64: gh * 64 + 64],
                         sig2[0:1, 0:512], start=True, stop=True)
                    p.op("tensor", "matmul", pO[1][64:128, :],
                         self.mv02[0:1, (gh + 1) * 64: (gh + 1) * 64 + 64],
                         sig2[0:1, 512:1024], start=True, stop=True,
                         tile_position=(0, 64), sem=s_pe)
                    ou_act = s_act.n
                    p.w("vector", s_pe)
                    p.op("vector", "tensor_mul", ouf2[:, 0:512], ouf2[:, 0:512],
                         pO[0][0:64, :])
                    p.op("vector", "tensor_mul", ouf2[:, 512:1024],
                         ouf2[:, 512:1024], pO[0][64:128, :])
                    p.op("vector", "tensor_add",
                         OT[0:64, ci * 512:(ci + 1) * 512],
                         ouf2[:, 0:512], pO[1][0:64, :])
                    p.op("vector", "tensor_add",
                         OT[64:128, ci * 512:(ci + 1) * 512],
                         ouf2[:, 512:1024], pO[1][64:128, :], sem=s_dve)
                    fin_dve = s_dve.n
                if g == 1:
                    p.w("sync", s_st)
                    p.dma("sync", Astage[:],
                          self.park_ap(self.Apark, s, 8), s_ld)
                    p.w("vector", s_ld)
                else:
                    p.w("scalar", s_st)
                evw = {}
                for jc in range(0 if SKIP & 4 else 8):
                    ps = pO[jc % 2]
                    if jc >= 2:
                        p.w("tensor", s_act if g == 0 else s_dve, evw[jc - 2])
                    else:
                        p.w("tensor", s_dve)
                    for ic in range(4):
                        p.op("tensor", "matmul", ps[:],
                             wob[:, ic * D + jc * 128: ic * D + (jc + 1) * 128],
                             OT[:, ic * 512:(ic + 1) * 512],
                             start=(ic == 0), stop=(ic == 3),
                             sem=s_pe if ic == 3 else None)
                    if g == 0:
                        p.w("scalar", s_pe)
                        p.op("scalar", "activation",
                             Astage[:, jc * 512:(jc + 1) * 512], ps[:], AF.Copy,
                             sem=s_act)
                        evw[jc] = s_act.n
                    else:
                        p.w("vector", s_pe)
                        p.op("vector", "scalar_tensor_tensor",
                             Astage[:, jc * 512:(jc + 1) * 512], ps[:],
                             self.bcol(B_BO + jc),
                             Astage[:, jc * 512:(jc + 1) * 512],
                             op0=ALU.add, op1=ALU.add, sem=s_dve)
                        evw[jc] = s_dve.n
                p.w("sync", s_act if g == 0 else s_dve)
                p.dma("sync", self.park_ap(self.Apark, s, 8),
                      Astage[:], s_st)
        self.dump("d_KTg", KTg[:, 0:4096], [128, 4096], BF16)
        self.dump("d_Vg", Vg[:, 0:4160], [128, 4160], BF16)
        self.dump("d_A0", Astage[:], [128, 4096], F32)
        p.w("sync", self.s_st)
        p.barrier("p1_done")
        st.close()

    def phase_p2(self):
        p = self.p
        p.barrier("p2_start")
        st = ExitStack()
        aostage = p.sb("aostage", [128, 6 * QB], F32, stack=st)
        xnT2 = p.sb("xnT2", [128, 8 * T], BF16, stack=st)
        qca = p.sb("qca", [128, 8 * NTO], BF16, stack=st)
        KT2 = p.sb("KT2", [128, 2 * T], BF16, stack=st)
        V2 = p.sb("V2", [128, 32 * 260], BF16, stack=st)
        mslab = p.sb("mslab", [128, 8 * 512], BF16, stack=st)
        qposr = p.sb("qposr", [1, 512], F32, stack=st)
        fchunk = p.sb("p2_fc", [128, 256], F32, stack=st)
        kslab = p.sb("kslab", [128, 8 * 256], BF16, stack=st)
        vslab = p.sb("vslab", [128, 8 * 256], BF16, stack=st)
        fwogb = p.sb("fwogb", [128, 2 * 768], BF16, stack=st)
        OT2 = p.sb("OT2", [128, 2 * 512], BF16, stack=st)
        pT = [p.sb(f"p2_pT{i}", [128, 1024], BF16, stack=st) for i in range(3)]
        sqf = p.sb("p2_sqf", [128, 512], F32, stack=st)
        sqfB = p.sb("p2_sqfB", [128, 512], F32, stack=st)
        ssrow = p.sb("p2_ss", [2, 512], F32, stack=st)
        ssrB = p.sb("p2_ssrB", [2, 512], F32, stack=st)
        ssl = p.sb("p2_ssl", [2, 512], F32, stack=st)
        sslB = p.sb("p2_sslB", [2, 512], F32, stack=st)
        bsb = p.sb("p2_bsb", [128, 512], F32, stack=st)
        bsbB = p.sb("p2_bsbB", [128, 512], F32, stack=st)
        lrow2 = p.sb("p2_lrow", [1, 1024], F32, stack=st)
        linv2 = p.sb("p2_linv", [1, 1024], F32, stack=st)
        ouf2b = p.sb("p2_ou2", [64, 1024], F32, stack=st)
        ouf2 = ouf2b[:, :]
        pS01 = p.psum("p2_pS01", [128, 1024], F32, stack=st)
        pS23 = p.psum("p2_pS23", [128, 1024], F32, stack=st)
        pF = p.psum("p2_pF", [128, 1024], F32, stack=st)
        pSb = [pS01, pS23, pF]
        pO = [p.psum(f"p2_pO{i}", [128, 512], F32, stack=st) for i in range(2)]
        s_ld, s_gld, s_pe, s_act, s_dve, s_st = (self.s_ld, self.s_gld, self.s_pe,
                                                 self.s_act, self.s_dve, self.s_st)

        p.dma("sync", xnT2[:], self.xnTpark, s_ld)
        p.dma("sync", qca[:], self.qcapark, s_ld)

        for g in range(4):
            p.w("vector", s_pe)
            for srcw, dstb in ((self.Fwk, kslab), (self.Fwv, vslab)):
                for ic in range(8):
                    p.w("sync", s_dve)
                    p.dma("sync", fchunk[:],
                          srcw[ic * 128:(ic + 1) * 128, g * 256:(g + 1) * 256],
                          s_ld)
                    p.w("vector", s_ld)
                    p.op("vector", "tensor_scalar_mul",
                         dstb[:, ic * 256:(ic + 1) * 256], fchunk[:],
                         self.bcol(B_N1 + ic), sem=s_dve)
            p.w("gpsimd", s_pe)
            p.dma("gpsimd", fwogb[:],
                  man_ap(self.Fwo, g * 256 * D + 256, [[D, 128], [128 * D, 2], [1, 768]]), s_gld)
            p.w("tensor", s_ld)
            p.w("tensor", s_dve)
            kps = [pO[0], pO[1], pF[:, 0:512]]
            sqf2 = [sqf, sqfB]
            bsb2 = [bsb, bsbB]
            ssl2 = [ssl, sslB]
            ssrow2 = [ssrow, ssrB]
            evac = {}
            pe_mm, pe_hs, pe_bc = {}, {}, {}
            act_sq, act_ex, act_cp = {}, {}, {}
            for k in range(16 + 2):
                if k < 16:
                    ci, tb = divmod(k, 8)
                    if k >= 3:
                        p.w("tensor", s_dve, evac[k - 3])
                    for ic in range(8):
                        p.op("tensor", "matmul", kps[k % 3][:],
                             kslab[:, ic * 256 + ci * 128: ic * 256 + (ci + 1) * 128],
                             xnT2[:, ic * T + tb * 512: ic * T + tb * 512 + 512],
                             start=(ic == 0), stop=(ic == 7),
                             sem=s_pe if ic == 7 else None)
                    pe_mm[k] = s_pe.n
                if 1 <= k <= 16:
                    j = k - 1
                    p.w("tensor", s_act, act_sq[j])
                    p.op("tensor", "matmul", pSb[j % 2][0:2, 0:512], self.hselb[:],
                         sqf2[j % 2][:], start=True, stop=True, sem=s_pe)
                    pe_hs[j] = s_pe.n
                if k >= 2:
                    i2 = k - 2
                    p.w("tensor", s_act, act_ex[i2])
                    p.op("tensor", "matmul", pSb[i2 % 2][:, 512:1024],
                         self.sel2b[:], ssrow2[i2 % 2][:],
                         start=True, stop=True, sem=s_pe)
                    pe_bc[i2] = s_pe.n
                if k < 16:
                    p.w("scalar", s_pe, pe_mm[k])
                    p.op("scalar", "activation", sqf2[k % 2][:],
                         kps[k % 3][:], AF.Square, sem=s_act)
                    act_sq[k] = s_act.n
                if 1 <= k <= 16:
                    j = k - 1
                    p.w("scalar", s_pe, pe_hs[j])
                    p.op("scalar", "activation", ssl2[j % 2][:],
                         pSb[j % 2][0:2, 0:512], AF.Ln,
                         scale=1.0 / 64.0, bias=EPSR)
                    p.op("scalar", "activation", self.spcr[:], self.spcr[:],
                         AF.Square)
                    p.op("scalar", "activation", ssrow2[j % 2][:],
                         ssl2[j % 2][:], AF.Exp, scale=-0.5, sem=s_act)
                    act_ex[j] = s_act.n
                if k >= 2:
                    i2 = k - 2
                    p.w("scalar", s_pe, pe_bc[i2])
                    if i2 >= 2:
                        p.w("scalar", s_dve, evac[i2 - 2])
                    p.op("scalar", "activation", bsb2[i2 % 2][:],
                         pSb[i2 % 2][:, 512:1024], AF.Copy, sem=s_act)
                    act_cp[i2] = s_act.n
                    ci2, tb2 = divmod(i2, 8)
                    p.w("vector", s_act, act_cp[i2])
                    p.op("vector", "scalar_tensor_tensor",
                         KT2[:, ci2 * T + tb2 * 512: ci2 * T + tb2 * 512 + 512],
                         kps[i2 % 3][:], self.bcol(B_KNW), bsb2[i2 % 2][:],
                         op0=ALU.mult, op1=ALU.mult, sem=s_dve)
                    evac[i2] = s_dve.n
            p.w("tensor", s_dve, evac[15])
            p.op("vector", "memset",
                 man_ap(V2[:, :], 64, [[32 * 260, 128], [260, 32], [65, 4]]), 1.0)
            p.drain("vector", sem=s_dve)
            evac = {}
            for tt in range(32):
                ps = pF
                if tt >= 1:
                    p.w("tensor", s_dve, evac[tt - 1])
                for ic in range(8):
                    p.op("tensor", "matmul", ps[:, 0:256],
                         xnT2[:, ic * T + tt * 128: ic * T + (tt + 1) * 128],
                         vslab[:, ic * 256:(ic + 1) * 256],
                         start=(ic == 0), stop=(ic == 7),
                         sem=s_pe if ic == 7 else None)
                p.w("vector", s_pe)
                p.op("vector", "tensor_single_scalar",
                     man_ap(V2[:, :], tt * 260, [[32 * 260, 128], [65, 4], [1, 64]]),
                     man_ap(pF[:, :], 0, [[1024, 128], [64, 4], [1, 64]]),
                     self.rstd1[:, tt:tt + 1], ALU.mult, sem=s_dve)
                evac[tt] = s_dve.n

            for s in range(4):
                KTN = 8 * s + 8
                p.w("sync", s_pe)
                p.dma("sync", qposr[:], self.qpos[:, s * 512:(s + 1) * 512], s_ld)
                p.w("tensor", s_ld)
                p.w("tensor", s_dve)
                p.op("tensor", "matmul", pO[0][:], self.onesr_f[:], qposr[:],
                     start=True, stop=True, sem=s_pe)
                p.w("vector", s_pe)
                for m in range(8):
                    p.op("vector", "tensor_single_scalar",
                         mslab[:, m * 512:(m + 1) * 512], pO[0][:],
                         self.kcol_s[:, 8 * s + m: 8 * s + m + 1], ALU.is_ge,
                         sem=s_dve if m == 7 else None)
                ou_act = 0
                fin_dve = 0
                prev_expd = None
                for pq in range(2):
                    ci = pq
                    cq = g * 2 + ci
                    h0 = pq * 2
                    expd = {}
                    concnt = {}
                    vcnt = {}

                    def emit_v(kt, pq=pq, h0=h0, vcnt=vcnt, concnt=concnt,
                               KTN=KTN):
                        sm, vv = concnt[kt]
                        if sm is not None:
                            p.w("tensor", sm, vv)
                        if kt == 0:
                            p.w("tensor", s_dve, fin_dve)
                            p.w("tensor", s_act, ou_act)
                            p.w("tensor", s_dve)
                        b3 = kt % 3
                        p.op("tensor", "matmul", pO[0][0:65, :],
                             V2[:, kt * 260 + h0 * 65: kt * 260 + h0 * 65 + 65],
                             pT[b3][:, 0:512],
                             start=(kt == 0), stop=(kt == KTN - 1))
                        p.op("tensor", "matmul", pO[1][0:65, :],
                             V2[:, kt * 260 + (h0 + 1) * 65: kt * 260 + (h0 + 1) * 65 + 65],
                             pT[b3][:, 512:1024],
                             start=(kt == 0), stop=(kt == KTN - 1), sem=s_pe)
                        vcnt[kt] = s_pe.n

                    for kt in range(KTN):
                        masked = kt >= 8 * s
                        pS = pSb[kt % 3]
                        if kt >= 3:
                            p.w("tensor", s_act, expd[kt - 3])
                        elif prev_expd is not None and KTN >= 3:
                            p.w("tensor", s_act, prev_expd[KTN - 3 + kt])
                        else:
                            p.w("tensor", s_act)
                            p.w("tensor", s_dve)
                        p.op("tensor", "matmul", pS[:, 0:512],
                             KT2[0:64, ci * T + kt * 128: ci * T + kt * 128 + 128],
                             qca[0:64, cq * NTO + s * 512: cq * NTO + s * 512 + 512],
                             start=True, stop=True, tile_position=(0, 0))
                        p.op("tensor", "matmul", pS[:, 512:1024],
                             KT2[64:128, ci * T + kt * 128: ci * T + kt * 128 + 128],
                             qca[64:128, cq * NTO + s * 512: cq * NTO + s * 512 + 512],
                             start=True, stop=True, tile_position=(64, 0),
                             sem=s_pe)
                        p.w("scalar", s_pe)
                        if kt >= 3:
                            sm, vv = concnt[kt - 3]
                            if sm is not None:
                                p.w("scalar", sm, vv)
                            p.w("scalar", s_pe, vcnt[kt - 3])
                        p.op("scalar", "activation", pT[kt % 3][:], pS[:],
                             AF.Exp, bias=CSTAB, sem=s_act)
                        expd[kt] = s_act.n
                        if masked:
                            m = kt - 8 * s
                            mt = mslab[:, m * 512:(m + 1) * 512]
                            p.w("vector", s_act)
                            p.op("vector", "tensor_mul", pT[kt % 3][:, 0:512],
                                 pT[kt % 3][:, 0:512], mt)
                            p.op("vector", "tensor_mul", pT[kt % 3][:, 512:1024],
                                 pT[kt % 3][:, 512:1024], mt, sem=s_dve)
                            concnt[kt] = (s_dve, s_dve.n)
                        else:
                            concnt[kt] = (None, None)
                        if kt >= 2:
                            emit_v(kt - 2)
                    if KTN >= 2:
                        emit_v(KTN - 2)
                    emit_v(KTN - 1)
                    prev_expd = expd
                    p.w("scalar", s_pe, vcnt[KTN - 1])
                    p.w("scalar", s_dve, fin_dve)
                    p.op("scalar", "activation", lrow2[0:1, 0:512],
                         pO[0][64:65, :], AF.Copy)
                    p.op("scalar", "activation", lrow2[0:1, 512:1024],
                         pO[1][64:65, :], AF.Copy)
                    p.op("scalar", "activation", ouf2[:, 0:512], pO[0][0:64, :],
                         AF.Copy)
                    p.op("scalar", "activation", lrow2[0:1, 0:1024],
                         lrow2[0:1, 0:1024], AF.Ln)
                    p.op("scalar", "activation", ouf2[:, 512:1024],
                         pO[1][0:64, :], AF.Copy)
                    p.op("scalar", "activation", linv2[0:1, 0:1024],
                         lrow2[0:1, 0:1024], AF.Exp, scale=-1.0, sem=s_act)
                    ou_act = s_act.n
                    p.w("tensor", s_act)
                    p.op("tensor", "matmul", pO[0][0:64, :], self.ones64_f[:],
                         linv2[0:1, 0:512], start=True, stop=True)
                    p.op("tensor", "matmul", pO[0][64:128, :], self.ones64_f[:],
                         linv2[0:1, 512:1024], start=True, stop=True,
                         tile_position=(0, 64), sem=s_pe)
                    p.w("vector", s_act)
                    p.w("vector", s_pe)
                    p.op("vector", "tensor_mul",
                         OT2[0:64, ci * 512:(ci + 1) * 512],
                         ouf2[:, 0:512], pO[0][0:64, :])
                    p.op("vector", "tensor_mul",
                         OT2[64:128, ci * 512:(ci + 1) * 512],
                         ouf2[:, 512:1024], pO[0][64:128, :], sem=s_dve)
                    fin_dve = s_dve.n
                if g > 0:
                    p.w("sync", s_st)
                    p.dma("sync", aostage[:],
                          self.park_ap(self.aopark, s, 6),
                          s_ld)
                    p.w("vector", s_ld)
                else:
                    p.w("scalar", s_st)
                evw = {}
                for jx in range(6):
                    ps = pO[jx % 2]
                    if jx >= 2:
                        p.w("tensor", s_act if g == 0 else s_dve, evw[jx - 2])
                    else:
                        p.w("tensor", s_dve)
                    for ic in range(2):
                        p.op("tensor", "matmul", ps[:],
                             fwogb[:, ic * 768 + jx * 128: ic * 768 + (jx + 1) * 128],
                             OT2[:, ic * 512:(ic + 1) * 512],
                             start=(ic == 0), stop=(ic == 1),
                             sem=s_pe if ic == 1 else None)
                    dst = aostage[:, jx * 512:(jx + 1) * 512]
                    if g == 0:
                        p.w("scalar", s_pe)
                        p.op("scalar", "activation", dst, ps[:], AF.Copy,
                             sem=s_act)
                        evw[jx] = s_act.n
                    else:
                        p.w("vector", s_pe)
                        p.op("vector", "tensor_add", dst, ps[:], dst, sem=s_dve)
                        evw[jx] = s_dve.n
                p.w("sync", s_act if g == 0 else s_dve)
                p.dma("sync", self.park_ap(self.aopark, s, 6),
                      aostage[:], s_st)
        self.dump("d_KT2", KT2[:, 0:4096], [128, 4096], BF16)
        self.dump("d_ao", aostage[:], [128, 3072], F32)
        p.barrier("p2_done")
        st.close()

    def phase_p3(self):
        p = self.p
        s_ld, s_gld, s_pe, s_act, s_dve, s_st = (self.s_ld, self.s_gld, self.s_pe,
                                                 self.s_act, self.s_dve, self.s_st)
        st = ExitStack()
        ptrA = [p.psum(f"p3_tr{i}", [128, 512], F32, stack=st) for i in range(2)]
        xfT = p.sb("xfT", [128, 8 * NTO], BF16, stack=st)
        wtsT = p.sb("wtsT", [4, NTO], F32, stack=st)
        moe = p.sb("moe", [128, 6 * NTO], F32, stack=st)

        stA = ExitStack()
        tok48 = p.sb("tok48", [128, 16 * 768], F32, stack=stA)
        aobuf = p.sb("p3_aob", [128, 6 * QB], F32, stack=stA)
        xnb = p.sb("p3_xnb", [128, D], F32, stack=stA)
        x1b = p.sb("p3_x1b", [128, D], F32, stack=stA)
        xfb = p.sb("p3_xfb", [128, D], BF16, stack=stA)
        sqd = p.sb("p3_sqd", [128, D], F32, stack=stA)
        rws = p.sb("p3_rws", [128, 4], F32, stack=stA)
        ptrB = [p.psum(f"p3_trb{i}", [128, 128], BF16, stack=stA) for i in range(2)]
        evac = {}
        k = 0
        for s in range(4):
            p.w("sync", s_pe)
            p.dma("sync", aobuf[:],
                  self.park_ap(self.aopark, s, 6), s_ld)
            p.w("tensor", s_ld)
            for u in range(4):
                tt = s * 4 + u
                for jx in range(6):
                    b = k % 2
                    if k >= 2:
                        p.w("tensor", s_dve, evac[k - 2])
                    p.op("tensor", "transpose", ptrA[b][:, 0:128],
                         aobuf[:, jx * 512 + u * 128: jx * 512 + u * 128 + 128],
                         self.id32[:], sem=s_pe)
                    p.w("vector", s_pe)
                    p.op("vector", "tensor_copy",
                         tok48[:, tt * 768 + jx * 128: tt * 768 + jx * 128 + 128],
                         ptrA[b][:, 0:128], sem=s_dve)
                    evac[k] = s_dve.n
                    k += 1
        evacX = {}
        evacT = {}
        kT = 0
        for tt in range(16):
            if tt >= 1:
                p.w("sync", s_dve, evacX[tt - 1])
            p.dma("sync", xnb[:], self.xnopark[tt * 128:(tt + 1) * 128, :], s_ld)
            p.w("vector", s_ld)
            p.w("vector", s_st)
            p.op("vector", "tensor_copy", x1b[:], xnb[:])
            p.op("vector", "tensor_add", x1b[:, 256:D], x1b[:, 256:D],
                 tok48[:, tt * 768:(tt + 1) * 768], sem=s_dve)
            evacX[tt] = s_dve.n
            p.w("sync", s_dve)
            p.dma("sync", self.x1park[tt * 128:(tt + 1) * 128, :], x1b[:], s_st)
            p.w("scalar", s_dve)
            p.op("scalar", "activation", sqd[:], x1b[:], AF.Square,
                 accum_out=rws[:, 0:1])
            p.op("scalar", "activation", rws[:, 1:2], rws[:, 0:1], AF.Ln,
                 scale=1.0 / 1024.0, bias=EPSR)
            p.op("scalar", "activation", sqd[:, 0:64], sqd[:, 0:64], AF.Square)
            p.op("scalar", "activation", rws[:, 2:3], rws[:, 1:2], AF.Exp,
                 scale=-0.5, sem=s_act)
            p.w("vector", s_act)
            p.op("vector", "scalar_tensor_tensor", xfb[:], x1b[:], rws[:, 2:3],
                 self.n2bc[:], op0=ALU.mult, op1=ALU.mult, sem=s_dve)
            p.w("tensor", s_dve)
            for c in range(8):
                b = kT % 2
                if kT >= 2:
                    p.w("tensor", s_dve, evacT[kT - 2])
                p.op("tensor", "transpose", ptrB[b][:],
                     xfb[:, c * 128:(c + 1) * 128], self.id16[:], sem=s_pe)
                p.w("vector", s_pe)
                p.op("vector", "tensor_copy",
                     xfT[:, c * NTO + tt * 128: c * NTO + tt * 128 + 128],
                     ptrB[b][:], sem=s_dve)
                evacT[kT] = s_dve.n
                kT += 1
        self.dump("d_xfT", xfT[:, 0:2048], [128, 2048], BF16)
        self.dump("d_x1", x1b[:], [128, 1024], F32)

        gslab = p.sb("gslab", [128, 8 * 4], BF16, stack=stA)
        lg = p.sb("p3_lg", [4, 512], F32, stack=stA)
        g4 = p.sb("p3_g4", [128, 4], F32, stack=stA)
        e4 = p.sb("p3_e4", [128, 4], F32, stack=stA)
        cbuf = p.sb("p3_cb", [128, 16], F32, stack=stA)
        ebuf = p.sb("p3_eb", [128, 16], F32, stack=stA)
        rk = p.sb("p3_rk", [128, 4], F32, stack=stA)
        rk2 = p.sb("p3_rk2", [128, 4], F32, stack=stA)
        w4 = p.sb("p3_w4", [128, 4], F32, stack=stA)
        grow_ = p.sb("p3_gr", [128, 6], F32, stack=stA)
        p.dma("gpsimd", gslab[:],
              self.wsl(self.gwT, 4, 0, 4, 8), s_gld)
        p.w("tensor", s_gld)
        GR_MAX, GR_NM, GR_S, GR_SI, GR_W, GR_WI = range(6)
        for s in range(4):
            p.w("tensor", s_dve)
            for ic in range(8):
                p.op("tensor", "matmul", ptrA[0][0:4, :],
                     gslab[:, ic * 4:(ic + 1) * 4],
                     xfT[:, ic * NTO + s * 512: ic * NTO + s * 512 + 512],
                     start=(ic == 0), stop=(ic == 7),
                     sem=s_pe if ic == 7 else None)
            p.w("scalar", s_pe)
            p.op("scalar", "activation", lg[:], ptrA[0][0:4, :], AF.Copy,
                 sem=s_act)
            for u in range(4):
                tt = s * 4 + u
                p.w("tensor", s_act)
                p.w("tensor", s_dve)
                p.op("tensor", "transpose", ptrA[1][:, 0:4],
                     lg[0:4, u * 128:(u + 1) * 128], self.id32[0:4, 0:4],
                     sem=s_pe)
                p.w("vector", s_pe)
                p.op("vector", "tensor_copy", g4[:], ptrA[1][:, 0:4])
                p.drain("vector")
                p.op("vector", "reduce_max", grow_[:, GR_MAX:GR_MAX + 1], g4[:],
                     AX.X)
                p.drain("vector")
                p.op("vector", "tensor_scalar_mul", grow_[:, GR_NM:GR_NM + 1],
                     grow_[:, GR_MAX:GR_MAX + 1], -1.0)
                p.drain("vector", sem=s_dve)
                p.w("scalar", s_dve)
                p.op("scalar", "activation", e4[:], g4[:], AF.Exp,
                     bias=grow_[:, GR_NM:GR_NM + 1], sem=s_act)
                p.w("vector", s_act)
                p.op("vector", "reduce_sum", grow_[:, GR_S:GR_S + 1], e4[:], AX.X)
                p.drain("vector", sem=s_dve)
                p.w("scalar", s_dve)
                p.op("scalar", "activation", grow_[:, GR_SI:GR_SI + 1],
                     grow_[:, GR_S:GR_S + 1], AF.Ln)
                p.op("scalar", "activation", grow_[:, GR_NM:GR_NM + 1],
                     grow_[:, GR_NM:GR_NM + 1], AF.Copy)
                p.op("scalar", "activation", grow_[:, GR_SI:GR_SI + 1],
                     grow_[:, GR_SI:GR_SI + 1], AF.Exp, scale=-1.0, sem=s_act)
                p.w("vector", s_act)
                p.op("vector", "tensor_scalar_mul", g4[:], e4[:],
                     grow_[:, GR_SI:GR_SI + 1])
                p.drain("vector")
                pa = g4[:].unsqueeze(1).broadcast_to([128, 4, 4])
                pb = g4[:].unsqueeze(2).broadcast_to([128, 4, 4])
                c3 = cbuf[:].rearrange("p (a b) -> p a b", b=4)
                e3_ = ebuf[:].rearrange("p (a b) -> p a b", b=4)
                p.op("vector", "tensor_tensor", c3, pa, pb, ALU.is_gt)
                p.op("vector", "tensor_tensor", e3_, pa, pb, ALU.is_equal)
                p.drain("vector")
                p.op("vector", "tensor_mul", ebuf[:], ebuf[:], self.ltm_s[:])
                p.drain("vector")
                for e in range(4):
                    p.op("vector", "reduce_sum", rk[:, e:e + 1],
                         cbuf[:, 4 * e:4 * e + 4], AX.X)
                    p.op("vector", "reduce_sum", rk2[:, e:e + 1],
                         ebuf[:, 4 * e:4 * e + 4], AX.X)
                p.drain("vector")
                p.op("vector", "tensor_add", rk[:], rk[:], rk2[:])
                p.drain("vector")
                p.op("vector", "tensor_single_scalar", rk[:], rk[:], 1.5,
                     ALU.is_le)
                p.drain("vector")
                p.op("vector", "tensor_mul", w4[:], g4[:], rk[:])
                p.drain("vector")
                p.op("vector", "reduce_sum", grow_[:, GR_W:GR_W + 1], w4[:], AX.X)
                p.drain("vector")
                p.op("vector", "tensor_scalar_add", grow_[:, GR_W:GR_W + 1],
                     grow_[:, GR_W:GR_W + 1], self.cc(C_1E8))
                p.drain("vector", sem=s_dve)
                p.w("scalar", s_dve)
                p.op("scalar", "activation", grow_[:, GR_WI:GR_WI + 1],
                     grow_[:, GR_W:GR_W + 1], AF.Ln)
                p.op("scalar", "activation", grow_[:, GR_NM:GR_NM + 1],
                     grow_[:, GR_NM:GR_NM + 1], AF.Copy)
                p.op("scalar", "activation", grow_[:, GR_WI:GR_WI + 1],
                     grow_[:, GR_WI:GR_WI + 1], AF.Exp, scale=-1.0, sem=s_act)
                p.w("vector", s_act)
                p.op("vector", "tensor_scalar_mul", w4[:], w4[:],
                     grow_[:, GR_WI:GR_WI + 1])
                p.drain("vector", sem=s_dve)
                p.w("tensor", s_dve)
                p.op("tensor", "transpose", ptrA[1][0:4, 0:128], w4[:],
                     self.id32[:], sem=s_pe)
                p.w("vector", s_pe)
                p.op("vector", "tensor_copy", wtsT[0:4, tt * 128:(tt + 1) * 128],
                     ptrA[1][0:4, 0:128], sem=s_dve)
        self.dump("d_wtsT", wtsT[:], [4, NTO], F32)
        p.barrier("p3_gate_done")
        stA.close()

        stB = ExitStack()
        Ht = p.sb("Ht", [128, 16 * 1024], BF16, stack=stB)
        Ht2 = p.sb("Ht2", [128, 16 * 1024], BF16, stack=stB)
        upsl = [p.sb(f"upsl{i}", [128, 16 * 256], BF16, stack=stB) for i in range(2)]
        dnsl = [p.sb(f"dnsl{i}", [128, 16 * 128], BF16, stack=stB) for i in range(2)]
        wbc = p.sb("wbc", [128, NTO], F32, stack=stB)
        tmpd = p.sb("p3_tmpd", [128, 512], F32, stack=stB)
        sel4 = p.sb("sel4s", [4, 512], F32, stack=stB)
        pH = [p.psum(f"p3_pH{i}", [128, 512], F32, stack=stB) for i in range(2)]
        p.dma("sync", sel4[:], self.sel4i, s_ld)
        upcnt = [0, 0]
        dncnt = [0, 0]

        def up_layer(tb, srcT, spitch, sbase, Wd, dst, act, nin, nout=HID):
            nic = nin // 128
            p.w("tensor", s_act)
            p.w("tensor", s_dve)
            ev = {}
            kk = 0
            for jb in range(nout // 256):
                p.w("gpsimd", s_pe, upcnt[jb % 2])
                p.dma("gpsimd", upsl[jb % 2][:, 0:nic * 256],
                      self.wsl(Wd, nout, jb * 256, 256, nic), s_gld)
                nw = s_gld.n
                for jc2 in range(2):
                    jc = jb * 2 + jc2
                    for ts in range(2):
                        b = kk % 2
                        if kk >= 2:
                            p.w("tensor", s_act, ev[kk - 2])
                        if jc2 == 0 and ts == 0:
                            p.w("tensor", s_gld, nw)
                        for ic in range(nic):
                            p.op("tensor", "matmul", pH[b][:],
                                 upsl[jb % 2][:, ic * 256 + jc2 * 128:
                                              ic * 256 + jc2 * 128 + 128],
                                 srcT[:, ic * spitch + sbase + ts * 512:
                                      ic * spitch + sbase + ts * 512 + 512],
                                 start=(ic == 0), stop=(ic == nic - 1),
                                 sem=s_pe if ic == nic - 1 else None)
                        if jc2 == 1 and ts == 1:
                            upcnt[jb % 2] = s_pe.n
                        p.w("scalar", s_pe)
                        p.op("scalar", "activation",
                             dst[:, jc * 1024 + ts * 512: jc * 1024 + ts * 512 + 512],
                             pH[b][:], act, sem=s_act)
                        ev[kk] = s_act.n
                        kk += 1

        def down_layer(tb, e, srcT, Wd, nin):
            nic = nin // 128
            ev = {}
            kk = 0
            for jx in range(6):
                p.w("gpsimd", s_pe, dncnt[jx % 2])
                p.dma("gpsimd", dnsl[jx % 2][:, 0:nic * 128],
                      self.wsl(Wd, D, (jx + 2) * 128, 128, nic), s_gld)
                nw = s_gld.n
                for ts in range(2):
                    b = kk % 2
                    if kk >= 2:
                        p.w("tensor", s_dve, ev[kk - 2])
                    else:
                        p.w("tensor", s_act)
                        p.w("tensor", s_dve)
                    if ts == 0:
                        p.w("tensor", s_gld, nw)
                    for ic in range(nic):
                        p.op("tensor", "matmul", pH[b][:],
                             dnsl[jx % 2][:, ic * 128:(ic + 1) * 128],
                             srcT[:, ic * 1024 + ts * 512:
                                  ic * 1024 + ts * 512 + 512],
                             start=(ic == 0), stop=(ic == nic - 1),
                             sem=s_pe if ic == nic - 1 else None)
                    if ts == 1:
                        dncnt[jx % 2] = s_pe.n
                    p.w("vector", s_pe)
                    col = jx * NTO + tb * 1024 + ts * 512
                    wcol = tb * 1024 + ts * 512
                    dstm = moe[:, col:col + 512]
                    if e == 0:
                        p.op("vector", "tensor_mul", dstm, pH[b][:],
                             wbc[:, wcol:wcol + 512], sem=s_dve)
                    else:
                        p.op("vector", "tensor_mul", tmpd[:], pH[b][:],
                             wbc[:, wcol:wcol + 512])
                        p.op("vector", "tensor_add", dstm, dstm, tmpd[:],
                             sem=s_dve)
                    ev[kk] = s_dve.n
                    kk += 1

        for e, layers in enumerate((None,) * 4):
            for ts4 in range(4):
                p.w("tensor", s_act)
                p.w("tensor", s_dve)
                p.w("tensor", s_ld)
                p.op("tensor", "matmul", pH[0][:], sel4[:, e * 128:(e + 1) * 128],
                     wtsT[0:4, ts4 * 512:(ts4 + 1) * 512], start=True, stop=True,
                     sem=s_pe)
                p.w("scalar", s_pe)
                p.op("scalar", "activation", wbc[:, ts4 * 512:(ts4 + 1) * 512],
                     pH[0][:], AF.Copy, sem=s_act)
            for tb in range(2):
                base = tb * 1024
                if e == 0:
                    up_layer(tb, xfT, NTO, base, self.E0U, Ht, AF.Gelu, D)
                    down_layer(tb, 0, Ht, self.E0D, HID)
                elif e == 1:
                    up_layer(tb, xfT, NTO, base, self.E1A, Ht, AF.Silu, D)
                    up_layer(tb, xfT, NTO, base, self.E1B, Ht2, AF.Copy, D)
                    p.w("vector", s_act)
                    for jc in range(16):
                        for ts in range(2):
                            sl = slice(jc * 1024 + ts * 512,
                                       jc * 1024 + ts * 512 + 512)
                            p.op("vector", "tensor_mul", Ht[:, sl], Ht[:, sl],
                                 Ht2[:, sl],
                                 sem=s_dve if (jc == 15 and ts == 1) else None)
                    down_layer(tb, 1, Ht, self.E1D, HID)
                elif e == 2:
                    up_layer(tb, xfT, NTO, base, self.E2A, Ht, AF.Gelu, D)
                    up_layer(tb, Ht, 1024, 0, self.E2B, Ht2, AF.Gelu, HID)
                    up_layer(tb, Ht2, 1024, 0, self.E2C, Ht, AF.Gelu, HID)
                    down_layer(tb, 2, Ht, self.E2D, HID)
                else:
                    up_layer(tb, xfT, NTO, base, self.E3A, Ht, AF.Gelu, D, nout=D)
                    up_layer(tb, Ht, 1024, 0, self.E3B, Ht2, AF.Gelu, D)
                    down_layer(tb, 3, Ht2, self.E3C, HID)
        self.dump("d_moe", moe[:, 0:2048], [128, 2048], F32)
        p.barrier("p3_moe_done")
        stB.close()

        stC = ExitStack()
        tokm = p.sb("tokm", [128, 16 * 768], F32, stack=stC)
        Abuf = p.sb("p3_Ab", [128, 8 * 512], F32, stack=stC)
        Atok = p.sb("p3_At", [128, 4 * 1024], F32, stack=stC)
        xob = p.sb("p3_xo", [128, D], F32, stack=stC)
        x1b2 = p.sb("p3_x1r", [128, D], F32, stack=stC)
        outb = p.sb("p3_out", [128, D], F32, stack=stC)
        evac = {}
        k = 0
        for jx in range(6):
            for tt in range(16):
                b = k % 2
                if k >= 2:
                    p.w("tensor", s_dve, evac[k - 2])
                p.op("tensor", "transpose", ptrA[b][:, 0:128],
                     moe[:, jx * NTO + tt * 128: jx * NTO + tt * 128 + 128],
                     self.id32[:], sem=s_pe)
                p.w("vector", s_pe)
                p.op("vector", "tensor_copy",
                     tokm[:, tt * 768 + jx * 128: tt * 768 + jx * 128 + 128],
                     ptrA[b][:, 0:128], sem=s_dve)
                evac[k] = s_dve.n
                k += 1
        for s in range(4):
            p.w("sync", s_dve)
            p.dma("sync", Abuf[:],
                  self.park_ap(self.Apark, s, 8), s_ld)
            p.w("tensor", s_ld)
            evac = {}
            k = 0
            for u in range(4):
                for c in range(8):
                    b = k % 2
                    if k >= 2:
                        p.w("tensor", s_dve, evac[k - 2])
                    p.op("tensor", "transpose", ptrA[b][:, 0:128],
                         Abuf[:, c * 512 + u * 128: c * 512 + u * 128 + 128],
                         self.id32[:], sem=s_pe)
                    p.w("vector", s_pe)
                    p.op("vector", "tensor_copy",
                         Atok[:, u * 1024 + c * 128: u * 1024 + c * 128 + 128],
                         ptrA[b][:, 0:128], sem=s_dve)
                    evac[k] = s_dve.n
                    k += 1
            for u in range(4):
                tt = s * 4 + u
                p.w("sync", s_dve)
                p.dma("sync", xob[:], self.xown[tt * 128:(tt + 1) * 128, :], s_ld)
                p.dma("sync", x1b2[:], self.x1park[tt * 128:(tt + 1) * 128, :],
                      s_ld)
                p.w("vector", s_ld)
                p.w("vector", s_st)
                p.op("vector", "tensor_add", outb[:], xob[:], x1b2[:])
                p.op("vector", "tensor_add", outb[:], outb[:],
                     Atok[:, u * 1024:(u + 1) * 1024])
                p.op("vector", "tensor_add", outb[:, 256:D], outb[:, 256:D],
                     tokm[:, tt * 768:(tt + 1) * 768], sem=s_dve)
                p.w("sync", s_dve)
                p.dma("sync", self.yout[tt * 128:(tt + 1) * 128, :], outb[:],
                      s_st)
        p.w("sync", self.s_st)
        stC.close()
        st.close()

    def build_all(self, phases=99):
        p = self.p
        import os
        phases = int(os.environ.get("MK_PHASES", phases))
        st01 = ExitStack()
        self.latT = p.sb("latT", [128, 2 * T], BF16, stack=st01)
        self.qsp = p.sb("qsp", [128, 8 * NTO], BF16, stack=st01)
        self.CS = p.sb("CS", [128, 2 * T], BF16, stack=st01)
        self.bvbc = p.sb("bvbc", [128, D], F32, stack=st01)
        self.mk8 = p.sb("mk8", [128, 8], BF16, stack=st01)
        self.mv02 = p.sb("mv02", [1, NH * 64], F32, stack=st01)
        stCSo = ExitStack()
        CSo = p.sb("CSo", [128, 2 * NTO], BF16, stack=stCSo)
        self.phase_consts()
        if phases >= 1:
            self.phase_big2()
        if phases >= 2:
            self.phase_cs(CSo)
            self.phase_bcasts()
        if phases >= 3:
            st0 = ExitStack()
            stXNT = ExitStack()
            self.phase_p0(st0, stXNT)
        if phases >= 4:
            self.phase_latent()
            p.barrier("xnt_close")
            stXNT.close()
        if phases >= 5:
            self.phase_qprod(CSo, st01)
        if phases >= 3:
            p.barrier("p0_close")
            st0.close()
        stCSo.close()
        if phases >= 6:
            self.phase_p1()
        p.barrier("p01_close")
        st01.close()
        if phases >= 7:
            self.phase_p2()
        if phases >= 8:
            self.phase_p3()
        p.w("sync", self.s_st)
        p.build()
        return self.nc



def _prep_shared(inputs):
    f32 = np.float32
    g = lambda k: np.asarray(inputs[k], dtype=f32)
    out = {}
    out["Wq"] = np.ascontiguousarray(g("L0_attn_Wq"))
    out["Wkd"] = np.ascontiguousarray(g("L0_attn_Wkv_down"))
    kvperm = np.concatenate([np.arange(16) * 128 + j for j in [0]])
    kvperm = np.concatenate(
        [np.concatenate([np.arange(h * 128, h * 128 + 64) for h in range(16)]),
         np.concatenate([np.arange(h * 128 + 64, h * 128 + 128) for h in range(16)])])
    out["Wku"] = np.ascontiguousarray(g("L0_attn_Wkv_up")[:, kvperm])
    out["Wo"] = np.ascontiguousarray(g("L0_attn_Wo"))
    out["bq_pc"] = np.ascontiguousarray(g("L0_attn_bq").reshape(8, 128).T)
    out["bkd_pc"] = np.ascontiguousarray(g("L0_attn_bkv_down").reshape(2, 128).T)
    bku = g("L0_attn_bkv_up")[kvperm]
    out["bkuk_pc"] = np.ascontiguousarray(bku[0:1024].reshape(8, 128).T)
    out["bkuv_row"] = np.ascontiguousarray(bku[1024:2048].reshape(1, 1024))
    out["bo_pc"] = np.ascontiguousarray(g("L0_attn_bo").reshape(8, 128).T)
    out["Fwq"] = np.ascontiguousarray(g("L0_ffn_wq").T)
    out["Fwk"] = np.ascontiguousarray(g("L0_ffn_wk").T)
    out["Fwv"] = np.ascontiguousarray(g("L0_ffn_wv").T)
    out["Fwo"] = np.ascontiguousarray(g("L0_ffn_wo").T)
    out["n1w_pc"] = np.ascontiguousarray(g("L0_ffn_n1_w").reshape(8, 128).T)
    out["n2w_row"] = np.ascontiguousarray(g("L0_ffn_n2_w").reshape(1, 1024))
    out["qnw_pc"] = np.ascontiguousarray(np.tile(g("L0_ffn_qn_w"), 2).reshape(128, 1))
    out["knw_pc"] = np.ascontiguousarray(np.tile(g("L0_ffn_kn_w"), 2).reshape(128, 1))
    out["gwT"] = np.ascontiguousarray(g("L0_ffn_gate_w").T)
    out["E0U"] = np.ascontiguousarray(g("L0_ffn_e0_up").T)
    out["E0D"] = np.ascontiguousarray(g("L0_ffn_e0_down").T)
    out["E1A"] = np.ascontiguousarray(g("L0_ffn_e1_w1").T)
    out["E1B"] = np.ascontiguousarray(g("L0_ffn_e1_w2").T)
    out["E1D"] = np.ascontiguousarray(g("L0_ffn_e1_down").T)
    out["E2A"] = np.ascontiguousarray(g("L0_ffn_e2_l1").T)
    out["E2B"] = np.ascontiguousarray(g("L0_ffn_e2_l2").T)
    out["E2C"] = np.ascontiguousarray(g("L0_ffn_e2_l3").T)
    out["E2D"] = np.ascontiguousarray(g("L0_ffn_e2_l4").T)
    out["E3A"] = np.ascontiguousarray(g("L0_ffn_e3_down").T)
    out["E3B"] = np.ascontiguousarray(g("L0_ffn_e3_up").T)
    out["E3C"] = np.ascontiguousarray(g("L0_ffn_e3_out").T)
    t = np.arange(T, dtype=np.float64)
    out["thi"] = (t // 64).astype(f32).reshape(1, T)
    out["tlo"] = (t % 64).astype(f32).reshape(1, T)
    out["i4r"] = (np.arange(128) % 32).astype(f32).reshape(1, 128)
    out["kcol"] = (np.arange(32)[None, :] * 128
                   + np.arange(128)[:, None]).astype(f32)
    out["I16b"] = np.eye(128, dtype=BF)
    out["I32f"] = np.eye(128, dtype=f32)
    pi = np.zeros((128, 128), dtype=BF)
    for p_ in range(128):
        q = p_ + 32 if (p_ % 64) < 32 else p_ - 32
        pi[p_, q] = 1.0
    out["PIb"] = pi
    sgn = np.where((np.arange(128) % 64) < 32, -1.0, 1.0).astype(f32)
    out["sgn"] = sgn.reshape(128, 1)
    sel2 = np.zeros((2, 128), dtype=f32)
    sel2[0, 0:64] = 1.0
    sel2[1, 64:128] = 1.0
    out["sel2"] = sel2
    out["hsel"] = np.ascontiguousarray(sel2.T)
    ltm = np.zeros((128, 16), dtype=f32)
    for e in range(4):
        for ff in range(4):
            if ff < e:
                ltm[:, 4 * e + ff] = 1.0
    out["ltm16"] = ltm
    sel4 = np.zeros((4, 512), dtype=f32)
    for e in range(4):
        sel4[e, e * 128:(e + 1) * 128] = 1.0
    out["sel4"] = sel4
    return out


def make_in_maps(inputs, cores=8):
    shared = _prep_shared(inputs)
    x = np.asarray(inputs["L0_attn_x"], dtype=np.float32)
    in_maps = []
    for c in range(cores):
        b, half = c // 2, c % 2
        qs = [1024 * s + 512 * half for s in range(NS)]
        own = np.concatenate([np.arange(q, q + QB) for q in qs])
        m = dict(shared)
        m["xfull"] = np.ascontiguousarray(x[b])
        m["xown"] = np.ascontiguousarray(x[b][own])
        tposo = own.astype(np.float64)
        m["thi_o"] = (tposo // 64).astype(np.float32).reshape(1, NTO)
        m["tlo_o"] = (tposo % 64).astype(np.float32).reshape(1, NTO)
        m["qpos"] = tposo.astype(np.float32).reshape(1, NTO)
        m["hsh"] = np.full((128, 1), 512.0 * half, dtype=np.float32)
        in_maps.append(m)
    return in_maps


def kernel(**inputs):
    x = np.asarray(inputs["L0_attn_x"], dtype=np.float32)
    B = x.shape[0]
    mk = MK()
    nc = mk.build_all()
    in_maps = make_in_maps(inputs)
    res = run_bass_kernel_spmd(nc, in_maps, core_ids=list(range(8)))
    out = np.empty((B, T, D), dtype=np.float32)
    for c in range(8):
        b, half = c // 2, c % 2
        y = np.asarray(res.results[c]["yout"])
        for s in range(NS):
            q = 1024 * s + 512 * half
            out[b, q:q + QB] = y[s * QB:(s + 1) * QB]
    return out


# version 10
# speedup vs baseline: 1.0478x; 1.0478x over previous
import math
import numpy as np
import ml_dtypes
import concourse.bass as bass
import concourse.mybir as mybir
import bass_rust
from concourse.bass_utils import run_bass_kernel_spmd
from contextlib import ExitStack

F32 = mybir.dt.float32
BF16 = mybir.dt.bfloat16
AF = mybir.ActivationFunctionType
ALU = mybir.AluOpType
AX = mybir.AxisListType
BF = ml_dtypes.bfloat16

T = 4096
D = 1024
NH = 16
HD = 64
DL = 256
HID = 2048
NS = 4
QB = 512
NTO = NS * QB
CSTAB = -30.0
A_OFF = 3968
W2 = 8192
EPSR = 1.1920929e-07
ENGS = ("sync", "tensor", "vector", "scalar", "gpsimd")

_CW1 = 6.28125
_CW23 = float(np.float32(2.0 * math.pi - 6.28125))


class Sem:
    __slots__ = ("h", "n", "name")

    def __init__(self, h, name=""):
        self.h = h
        self.n = 0
        self.name = name


class P:
    def __init__(self, nc):
        self.nc = nc
        self.q = {e: [] for e in ENGS}
        self.es = ExitStack()

    def sb(self, name, shape, dt=F32, stack=None, side=None):
        kw = {"side": side} if side else {}
        return (stack or self.es).enter_context(
            self.nc.sbuf_tensor(name, shape, dt, **kw))

    def psum(self, name, shape, dt=F32, stack=None):
        return (stack or self.es).enter_context(self.nc.psum_tensor(name, shape, dt))

    def sem(self, name):
        return Sem(self.es.enter_context(self.nc.semaphore(name)), name)

    def op(self, eng, meth, *args, sem=None, d=1, **kw):
        if sem is not None:
            sem.n += d
            h, dd = sem.h, d

            def f(e, meth=meth, args=args, kw=kw, h=h, dd=dd):
                getattr(e, meth)(*args, **kw).then_inc(h, dd)
        else:
            def f(e, meth=meth, args=args, kw=kw):
                getattr(e, meth)(*args, **kw)
        self.q[eng].append(f)

    def w(self, eng, sem, val=None):
        v = sem.n if val is None else val
        h = sem.h
        self.q[eng].append(lambda e, h=h, v=v: e.wait_ge(h, v))

    def dma(self, eng, out, in_, sem=None, **kw):
        self.op(eng, "dma_start", sem=sem, d=16, out=out, in_=in_, **kw)

    def drain(self, eng="vector", sem=None):
        self.op(eng, "drain", sem=sem)

    def barrier(self, name, engs=ENGS):
        s = self.sem(name)
        for eng in engs:
            self.op(eng, "nop", sem=s)
        for eng in engs:
            self.w(eng, s)

    def build(self):
        with self.nc.Block() as block:
            for eng in ENGS:
                def body(e, eng=eng):
                    for fn in self.q[eng]:
                        fn(e)
                getattr(block, eng)(body)


def man_ap(base_ap, offset, pattern):
    return bass_rust.AP(tensor=base_ap.tensor, offset=offset, ap=pattern)


C_P, C_PI, C_PI2, C_NP, C_001, C_N001, C_N01, C_1E9, C_02, C_1E8, C_LN4 = range(11)
B_BQ, B_BKD, B_BKUK, B_BO, B_N1, B_QNW, B_KNW = 0, 8, 10, 18, 26, 34, 35


class MK:
    def __init__(self, debug=()):
        self.nc = bass.Bass("TRN2", target_bir_lowering=False)
        self.p = P(self.nc)
        self.debug = set(debug)
        self.DBG = {}
        self._declare_dram()
        self._declare_global_sbuf()
        p = self.p
        self.s_ld = p.sem("s_ld")
        self.s_gld = p.sem("s_gld")
        self.s_pe = p.sem("s_pe")
        self.s_act = p.sem("s_act")
        self.s_dve = p.sem("s_dve")
        self.s_pool = p.sem("s_pool")
        self.s_st = p.sem("s_st")

    def _declare_dram(self):
        nc = self.nc
        self.DI = {}

        def din(name, shape, dt=F32):
            ap = nc.dram_tensor(name, list(shape), dt, kind="ExternalInput").ap()
            self.DI[name] = ap
            return ap

        self.xfull = din("xfull", [T, D])
        self.xown = din("xown", [NTO, D])
        self.Wq = din("Wq", [D, D])
        self.Wkd = din("Wkd", [D, DL])
        self.Wku = din("Wku", [DL, 2 * D])
        self.Wo = din("Wo", [D, D])
        self.bq_pc = din("bq_pc", [128, 8])
        self.bkd_pc = din("bkd_pc", [128, 2])
        self.bkuk_pc = din("bkuk_pc", [128, 8])
        self.bkuv_row = din("bkuv_row", [1, D])
        self.bo_pc = din("bo_pc", [128, 8])
        self.Fwq = din("Fwq", [D, D])
        self.Fwk = din("Fwk", [D, D])
        self.Fwv = din("Fwv", [D, D])
        self.Fwo = din("Fwo", [D, D])
        self.n1w_pc = din("n1w_pc", [128, 8])
        self.n2w_row = din("n2w_row", [1, D])
        self.qnw_pc = din("qnw_pc", [128, 1])
        self.knw_pc = din("knw_pc", [128, 1])
        self.gwT = din("gwT", [D, 4])
        self.E0U = din("E0U", [D, HID])
        self.E0D = din("E0D", [HID, D])
        self.E1A = din("E1A", [D, HID])
        self.E1B = din("E1B", [D, HID])
        self.E1D = din("E1D", [HID, D])
        self.E2A = din("E2A", [D, HID])
        self.E2B = din("E2B", [HID, HID])
        self.E2C = din("E2C", [HID, HID])
        self.E2D = din("E2D", [HID, D])
        self.E3A = din("E3A", [D, D])
        self.E3B = din("E3B", [D, HID])
        self.E3C = din("E3C", [HID, D])
        self.thi = din("thi", [1, T])
        self.tlo = din("tlo", [1, T])
        self.thi_o = din("thi_o", [1, NTO])
        self.tlo_o = din("tlo_o", [1, NTO])
        self.i4r = din("i4r", [1, 128])
        self.kcol = din("kcol", [128, 32])
        self.qpos = din("qpos", [1, NTO])
        self.hsh = din("hsh", [128, 1])
        self.I16b = din("I16b", [128, 128], BF16)
        self.I32f = din("I32f", [128, 128], F32)
        self.PIb = din("PIb", [128, 128], BF16)
        self.sgn = din("sgn", [128, 1])
        self.sel2 = din("sel2", [2, 128])
        self.hsel = din("hsel", [128, 2])
        self.ltm16 = din("ltm16", [128, 16])
        self.sel4i = din("sel4", [4, 512])

        self.yout = nc.dram_tensor("yout", [NTO, D], F32, kind="ExternalOutput").ap()
        self.big2 = nc.dram_tensor("big2", [128, W2], BF16, kind="Internal").ap()
        self.Apark = nc.dram_tensor("Apark", [NS, 8, 128, QB], F32, kind="Internal").ap()
        self.qcapark = nc.dram_tensor("qcapark", [128, 8 * NTO], BF16, kind="Internal").ap()
        self.xnTpark = nc.dram_tensor("xnTpark", [128, 8 * T], BF16, kind="Internal").ap()
        self.xnopark = nc.dram_tensor("xnopark", [NTO, D], F32, kind="Internal").ap()
        self.x1park = nc.dram_tensor("x1park", [NTO, D], F32, kind="Internal").ap()
        self.aopark = nc.dram_tensor("aopark", [NS, 6, 128, QB], F32, kind="Internal").ap()

    def dbg(self, name, shape, dt=F32):
        if name not in self.debug:
            return None
        if name not in self.DBG:
            self.DBG[name] = self.nc.dram_tensor(
                name, list(shape), dt, kind="ExternalOutput").ap()
        return self.DBG[name]

    def dump(self, name, src_ap, shape, dt=F32):
        ap = self.dbg(name, shape, dt)
        if ap is not None:
            for s in (self.s_act, self.s_dve, self.s_pe, self.s_pool):
                self.p.w("sync", s)
            self.p.dma("sync", ap, src_ap, self.s_st)
            self.p.w("sync", self.s_st)

    def _declare_global_sbuf(self):
        p = self.p
        self.co = p.sb("co", [128, 16], F32)
        self.rstd1 = p.sb("rstd1", [128, 32], F32)
        self.ones1b = p.sb("ones1b", [128, 1], BF16)
        self.onesr_f = p.sb("onesr_f", [1, 128], F32)
        self.ones64_f = p.sb("ones64_f", [1, 64], F32)
        self.onesc_f = p.sb("onesc_f", [128, 1], F32)
        self.sel2b = p.sb("sel2b", [2, 128], F32)
        self.hselb = p.sb("hselb", [128, 2], F32)
        self.kcol_s = p.sb("kcol_s", [128, 32], F32)
        self.id16 = p.sb("id16", [128, 128], BF16)
        self.id32 = p.sb("id32", [128, 128], F32)
        self.pib = p.sb("pib", [128, 128], BF16)
        self.sgn_s = p.sb("sgn_s", [128, 1], F32)
        self.ltm_s = p.sb("ltm_s", [128, 16], F32)
        self.biases = p.sb("biases", [128, 40], F32)
        self.n2bc = p.sb("n2bc", [128, D], F32)
        self.i4row = p.sb("i4row", [1, 128], F32)
        self.ctmp1 = p.sb("c_tmp1", [1, 128], F32)
        self.ACTC_VALS = [1.0e-5, EPSR, 1.0, CSTAB, -1.0, 1.0 / 64.0,
                          1.0 / 1024.0, 0.5]
        self.actc = p.sb("actc", [128, len(self.ACTC_VALS)], F32)
        for i, v in enumerate(self.ACTC_VALS):
            self.nc.const_aps.register(F32, v, self.actc[:, i:i + 1])
        self.spcr = p.sb("spcr", [2, 64], F32)
        self.frow = p.sb("frow", [1, 128], F32)
        self.grow = p.sb("grow", [1, 128], F32)

    def cc(self, i):
        return self.co[:, i:i + 1]

    def cc1(self, i):
        return self.co[0:1, i:i + 1]

    def bcol(self, i):
        return self.biases[:, i:i + 1]

    def wsl(self, Wd, nout_total, j0, nj, nic):
        return man_ap(Wd, j0, [[nout_total, 128], [128 * nout_total, nic],
                               [1, nj]])

    def park_ap(self, base, s, C):
        return man_ap(base, s * C * 128 * QB,
                      [[QB, 128], [128 * QB, C], [1, QB]])

    def phase_consts(self):
        p, s_ld, s_dve = self.p, self.s_ld, self.s_dve
        co = self.co

        def consts(v):
            tp1 = float(np.float32(2 * np.pi))
            tp2 = float(np.float32(np.float64(2 * np.pi) - tp1))
            tp3 = float(np.float32(np.float64(2 * np.pi) - tp1 - tp2))
            v.memset(co[:, C_P:C_P + 1], tp1)
            v.drain()
            v.tensor_scalar_add(co[:, C_P:C_P + 1], co[:, C_P:C_P + 1], tp2)
            v.drain()
            v.tensor_scalar_add(co[:, C_P:C_P + 1], co[:, C_P:C_P + 1], tp3)
            v.drain()
            v.tensor_scalar_mul(co[:, C_PI:C_PI + 1], co[:, C_P:C_P + 1], 0.5)
            v.tensor_scalar_mul(co[:, C_PI2:C_PI2 + 1], co[:, C_P:C_P + 1], 0.25)
            v.tensor_scalar_mul(co[:, C_NP:C_NP + 1], co[:, C_P:C_P + 1], -1.0)
            v.memset(co[:, C_001:C_001 + 1], 100.0)
            v.memset(co[:, C_N01:C_N01 + 1], 10.0)
            v.memset(co[:, C_1E9:C_1E9 + 1], 1.0e9)
            v.memset(co[:, C_02:C_02 + 1], 5.0)
            v.memset(co[:, C_1E8:C_1E8 + 1], 1.0e8)
            v.drain()
            v.reciprocal(co[:, C_001:C_001 + 1], co[:, C_001:C_001 + 1])
            v.drain()
            v.reciprocal(co[:, C_N01:C_N01 + 1], co[:, C_N01:C_N01 + 1])
            v.drain()
            v.reciprocal(co[:, C_1E9:C_1E9 + 1], co[:, C_1E9:C_1E9 + 1])
            v.drain()
            v.reciprocal(co[:, C_02:C_02 + 1], co[:, C_02:C_02 + 1])
            v.drain()
            v.reciprocal(co[:, C_1E8:C_1E8 + 1], co[:, C_1E8:C_1E8 + 1])
            v.drain()
            v.tensor_scalar_mul(co[:, C_N001:C_N001 + 1], co[:, C_001:C_001 + 1], -1.0)
            v.tensor_scalar_mul(co[:, C_N01:C_N01 + 1], co[:, C_N01:C_N01 + 1], -1.0)
            L = np.float64(math.log(10000.0))
            L1 = float(np.float32(L))
            L2 = float(np.float32(L - np.float64(np.float32(L))))
            v.memset(co[:, C_LN4:C_LN4 + 1], L1)
            v.drain()
            v.tensor_scalar_add(co[:, C_LN4:C_LN4 + 1], co[:, C_LN4:C_LN4 + 1], L2)
            for i, val in enumerate(self.ACTC_VALS):
                v.memset(self.actc[:, i:i + 1], val)
            v.memset(self.spcr[:], 1.0)
            v.memset(self.ones1b[:], 1.0)
            v.memset(self.onesr_f[:], 1.0)
            v.memset(self.ones64_f[:], 1.0)
            v.memset(self.onesc_f[:], 1.0)
            v.drain()

        p.q["vector"].append(consts)
        p.op("vector", "nop", sem=s_dve)

        for dst, src in [(self.id16[:], self.I16b), (self.id32[:], self.I32f),
                         (self.pib[:], self.PIb), (self.sgn_s[:], self.sgn),
                         (self.sel2b[:], self.sel2), (self.hselb[:], self.hsel),
                         (self.kcol_s[:], self.kcol),
                         (self.ltm_s[:], self.ltm16), (self.i4row[:], self.i4r)]:
            p.dma("sync", dst, src, self.s_ld)
        b = self.biases
        p.dma("sync", b[:, 0:8], self.bq_pc, self.s_ld)
        p.dma("sync", b[:, 8:10], self.bkd_pc, self.s_ld)
        p.dma("sync", b[:, 10:18], self.bkuk_pc, self.s_ld)
        p.dma("sync", b[:, 18:26], self.bo_pc, self.s_ld)
        p.dma("sync", b[:, 26:34], self.n1w_pc, self.s_ld)
        p.dma("sync", b[:, 34:35], self.qnw_pc, self.s_ld)
        p.dma("sync", b[:, 35:36], self.knw_pc, self.s_ld)
        p.w("vector", self.s_ld)
        p.op("vector", "tensor_scalar_mul", self.bcol(B_QNW), self.bcol(B_QNW), 0.125)
        p.drain("vector", sem=s_dve)

        p.w("vector", s_dve)
        p.op("vector", "tensor_scalar",
             self.frow[:], self.i4row[:], self.cc1(C_LN4), -0.03125,
             op0=ALU.mult, op1=ALU.mult)
        p.drain("vector", sem=s_dve)
        p.w("scalar", s_dve)
        p.op("scalar", "activation", self.frow[:], self.frow[:], AF.Exp,
             sem=self.s_act)
        p.w("vector", self.s_act)
        p.op("vector", "tensor_scalar_mul", self.grow[:], self.frow[:], 64.0)
        p.drain("vector")
        tmp1 = self.ctmp1
        for j in range(3, -1, -1):
            thr = float(np.float32(2.0 * math.pi) * (1 << j))
            c1j = _CW1 * (1 << j)
            c23j = _CW23 * (1 << j)
            p.op("vector", "tensor_single_scalar", tmp1[:], self.grow[:], thr, ALU.is_ge)
            p.drain("vector")
            p.op("vector", "scalar_tensor_tensor", self.grow[:], tmp1[:], -c1j,
                 self.grow[:], op0=ALU.mult, op1=ALU.add)
            p.drain("vector")
            p.op("vector", "scalar_tensor_tensor", self.grow[:], tmp1[:], -c23j,
                 self.grow[:], op0=ALU.mult, op1=ALU.add)
            p.drain("vector")
        p.op("vector", "nop", sem=s_dve)
        self.dump("d_frow", self.frow[:], [1, 128], F32)
        self.dump("d_grow", self.grow[:], [1, 128], F32)
        self.dump("d_co", self.co[:], [128, 16], F32)

    def phase_big2(self):
        p = self.p
        st = ExitStack()
        dgb = p.sb("b2_dg", [128, 2048], F32, stack=st)
        t1b = p.sb("b2_t1", [128, 2048], F32, stack=st)
        t2b = p.sb("b2_t2", [128, 2048], F32, stack=st)
        wb = p.sb("b2_w", [128, 2048], BF16, stack=st)
        hshs = p.sb("b2_hsh", [128, 1], F32, stack=st)
        p.dma("sync", hshs[:], self.hsh, self.s_ld)
        for ch in range(4):
            base = -A_OFF + ch * 2048
            p.w("gpsimd", self.s_act)
            p.w("gpsimd", self.s_dve)
            p.op("gpsimd", "iota", dgb[:], [[1, 2048]], base=base,
                 channel_multiplier=-1, allow_small_or_imprecise_dtypes=True,
                 sem=self.s_pool)
            p.w("vector", self.s_pool)
            p.w("vector", self.s_ld)
            p.op("vector", "tensor_scalar_add", dgb[:], dgb[:], hshs[:], sem=self.s_dve)
            p.w("scalar", self.s_dve)
            p.op("scalar", "activation", t1b[:], dgb[:], AF.Relu)
            p.op("scalar", "activation", t2b[:], t1b[:], AF.Exp,
                 scale=self.cc(C_N001), sem=self.s_act)
            p.op("scalar", "activation", t1b[:], dgb[:], AF.Abs)
            p.op("scalar", "activation", t1b[:], t1b[:], AF.Ln, bias=1.0)
            p.op("scalar", "activation", t1b[:], t1b[:], AF.Exp,
                 scale=self.cc(C_N01), sem=self.s_act)
            p.w("vector", self.s_act)
            p.w("vector", self.s_st)
            p.op("vector", "tensor_scalar_add", t2b[:], t2b[:], self.cc(C_1E9))
            p.op("vector", "tensor_mul", wb[:], t1b[:], t2b[:], sem=self.s_dve)
            p.w("sync", self.s_dve)
            p.dma("sync", self.big2[:, ch * 2048:(ch + 1) * 2048], wb[:], self.s_st)
        self.dump("d_big2d", wb[:], [128, 2048], BF16)
        p.w("sync", self.s_st)
        st.close()

    def _cs_chunk(self, st_bufs, dst_cos, dst_sin, thi_src, tlo_src):
        p = self.p
        rbuf, rc, tmp, thr_, tlr_, psA = st_bufs
        p.w("sync", self.s_pe)
        p.dma("sync", thr_[:], thi_src, self.s_ld)
        p.dma("sync", tlr_[:], tlo_src, self.s_ld)
        p.w("tensor", self.s_ld)
        p.w("tensor", self.s_dve)
        p.op("tensor", "matmul", psA[:], self.grow[:], thr_[:], start=True, stop=False)
        p.op("tensor", "matmul", psA[:], self.frow[:], tlr_[:], start=False, stop=True,
             sem=self.s_pe)
        p.w("vector", self.s_pe)
        p.w("vector", self.s_act)
        p.op("vector", "tensor_copy", rbuf[:], psA[:], sem=self.s_dve)
        for j in range(6, -1, -1):
            thr = float(np.float32(2.0 * math.pi) * (1 << j))
            c1j = _CW1 * (1 << j)
            c23j = _CW23 * (1 << j)
            p.op("vector", "tensor_single_scalar", tmp[:], rbuf[:], thr, ALU.is_ge)
            p.op("vector", "scalar_tensor_tensor", rbuf[:], tmp[:], -c1j,
                 rbuf[:], op0=ALU.mult, op1=ALU.add)
            p.op("vector", "scalar_tensor_tensor", rbuf[:], tmp[:], -c23j,
                 rbuf[:], op0=ALU.mult, op1=ALU.add)
        p.op("vector", "tensor_single_scalar", tmp[:], rbuf[:], self.cc(C_PI), ALU.is_gt)
        p.op("vector", "scalar_tensor_tensor", rbuf[:], tmp[:], self.cc(C_NP),
             rbuf[:], op0=ALU.mult, op1=ALU.add)
        p.op("vector", "tensor_scalar_add", rc[:], rbuf[:], self.cc(C_PI2))
        p.op("vector", "tensor_single_scalar", tmp[:], rc[:], self.cc(C_PI), ALU.is_gt)
        p.op("vector", "scalar_tensor_tensor", rc[:], tmp[:], self.cc(C_NP),
             rc[:], op0=ALU.mult, op1=ALU.add, sem=self.s_dve)
        p.w("scalar", self.s_dve)
        p.op("scalar", "activation", dst_sin, rbuf[:], AF.Sin)
        p.op("scalar", "activation", dst_cos, rc[:], AF.Sin, sem=self.s_act)
        if getattr(self, "_cs_first", True):
            self._cs_first = False
            self.dump("d_rb0", rbuf[:], [128, 512], F32)
            self.dump("d_rc0", rc[:], [128, 512], F32)

    def phase_cs(self, CSo):
        p = self.p
        p.barrier("cs_start")
        st = ExitStack()
        rbuf = p.sb("cs_r", [128, 512], F32, stack=st)
        rc = p.sb("cs_rc", [128, 512], F32, stack=st)
        tmp = p.sb("cs_tmp", [128, 512], F32, stack=st)
        thr_ = p.sb("cs_thi", [1, 512], F32, stack=st)
        tlr_ = p.sb("cs_tlo", [1, 512], F32, stack=st)
        psA = p.psum("cs_ps", [128, 512], F32, stack=st)
        bufs = (rbuf, rc, tmp, thr_, tlr_, psA)
        for ch in range(8):
            c0 = ch * 512
            self._cs_chunk(bufs,
                           self.CS[:, c0:c0 + 512],
                           self.CS[:, T + c0:T + c0 + 512],
                           self.thi[:, c0:c0 + 512],
                           self.tlo[:, c0:c0 + 512])
        for ch in range(4):
            c0 = ch * 512
            self._cs_chunk(bufs,
                           CSo[:, c0:c0 + 512],
                           CSo[:, NTO + c0:NTO + c0 + 512],
                           self.thi_o[:, c0:c0 + 512],
                           self.tlo_o[:, c0:c0 + 512])
        self.dump("d_cs", self.CS[:, 0:4096], [128, 4096], BF16)
        self.dump("d_css", self.CS[:, T:T + 4096], [128, 4096], BF16)
        self.dump("d_rbuf", rbuf[:], [128, 512], F32)
        st.close()

    def phase_p0(self, st0, stXNT):
        p = self.p
        p.barrier("p0_start")
        self.xnT = p.sb("xnT", [128, 8 * T], BF16, stack=stXNT, side="right")
        self.xnTo = p.sb("xnTo", [128, 8 * NTO], BF16, stack=st0)
        st = ExitStack()
        xb = [p.sb(f"p0_x{i}", [128, D], F32, stack=st) for i in range(2)]
        sqt = p.sb("p0_sq", [128, D], F32, stack=st)
        xnf = p.sb("p0_xnf", [128, D], F32, stack=st)
        xnb = p.sb("p0_xnb", [128, D], BF16, stack=st)
        rows = p.sb("p0_rows", [128, 16], F32, stack=st)
        ptr = [p.psum(f"p0_pt{i}", [128, 128], BF16, stack=st) for i in range(2)]
        R_S, R_Q, R_M, R_V, R_NM, R_SD, R_RR, R_T1, R_S2 = range(9)

        apply_cnt = {}
        evac_cnt = {}

        def rcol(i):
            return rows[:, i:i + 1]

        def ln_tile(src_rows, gi, own, idx):
            bi = gi % 2
            if gi - 2 in apply_cnt:
                p.w("sync", self.s_dve, apply_cnt[gi - 2])
            p.dma("sync", xb[bi][:], src_rows, self.s_ld)
            p.w("scalar", self.s_ld)
            p.op("scalar", "activation", sqt[:], xb[bi][:], AF.Copy,
                 accum_out=rcol(R_S))
            p.op("scalar", "activation", sqt[:], xb[bi][:], AF.Square,
                 accum_out=rcol(R_Q), sem=self.s_act)
            p.w("vector", self.s_act)
            p.op("vector", "tensor_single_scalar", rcol(R_M), rcol(R_S),
                 1.0 / 1024.0, ALU.mult)
            p.op("vector", "tensor_single_scalar", rcol(R_V), rcol(R_Q),
                 1.0 / 1024.0, ALU.mult)
            p.drain("vector")
            p.op("vector", "tensor_scalar", rcol(R_NM), rcol(R_M), rcol(R_M),
                 -1.0, op0=ALU.mult, op1=ALU.mult)
            p.drain("vector")
            p.op("vector", "tensor_add", rcol(R_V), rcol(R_V), rcol(R_NM))
            p.drain("vector", sem=self.s_dve)
            p.w("scalar", self.s_dve)
            p.op("scalar", "activation", rcol(R_SD), rcol(R_V), AF.Ln,
                 bias=1.0e-5)
            p.op("scalar", "activation", rcol(R_T1), rcol(R_V), AF.Square)
            p.op("scalar", "activation", rcol(R_RR), rcol(R_SD), AF.Exp,
                 scale=-0.5, sem=self.s_act)
            p.w("vector", self.s_act)
            if not own:
                p.op("vector", "tensor_mul", rcol(R_T1), rcol(R_V), rcol(R_RR))
                p.drain("vector")
                p.op("vector", "tensor_mul", rcol(R_T1), rcol(R_T1), rcol(R_RR))
                p.drain("vector", sem=self.s_dve)
                p.w("scalar", self.s_dve)
                p.op("scalar", "activation", rcol(R_S2), rcol(R_T1), AF.Ln,
                     bias=EPSR)
                p.op("scalar", "activation", rcol(R_NM), rcol(R_T1), AF.Square)
                p.op("scalar", "activation", self.rstd1[:, idx:idx + 1],
                     rcol(R_S2), AF.Exp, scale=-0.5, sem=self.s_act)
                p.w("vector", self.s_act)
            if own:
                p.w("vector", self.s_st)
            p.op("vector", "tensor_scalar", xnf[:], xb[bi][:], rcol(R_M),
                 rcol(R_RR), op0=ALU.subtract, op1=ALU.mult, sem=self.s_dve)
            apply_cnt[gi] = self.s_dve.n
            p.op("vector", "tensor_copy", xnb[:], xnf[:], sem=self.s_dve)
            if own:
                p.w("sync", self.s_dve, apply_cnt[gi])
                p.dma("sync", self.xnopark[idx * 128:(idx + 1) * 128, :], xnf[:],
                      self.s_st)
            dstT = self.xnTo if own else self.xnT
            span = NTO if own else T
            p.w("tensor", self.s_dve)
            for c in range(8):
                k = gi * 8 + c
                if k >= 2:
                    p.w("tensor", self.s_dve, evac_cnt[k - 2])
                p.op("tensor", "transpose", ptr[k % 2][:],
                     xnb[:, c * 128:(c + 1) * 128], self.id16[:], sem=self.s_pe)
                p.w("vector", self.s_pe)
                p.op("vector", "tensor_copy",
                     dstT[:, c * span + idx * 128: c * span + idx * 128 + 128],
                     ptr[k % 2][:], sem=self.s_dve)
                evac_cnt[k] = self.s_dve.n

        gi = 0
        for i in range(32):
            ln_tile(self.xfull[i * 128:(i + 1) * 128, :], gi, False, i)
            gi += 1
        for i in range(16):
            ln_tile(self.xown[i * 128:(i + 1) * 128, :], gi, True, i)
            gi += 1
        self.dump("d_xnT", self.xnT[:, 0:4096], [128, 4096], BF16)
        self.dump("d_xnTo", self.xnTo[:, 0:2048], [128, 2048], BF16)
        self.dump("d_rstd1", self.rstd1[:], [128, 32], F32)
        st.close()

    def phase_bcasts(self):
        p = self.p
        p.barrier("bcast_start")
        st = ExitStack()
        row = p.sb("bc_row", [1, D], F32, stack=st)
        psb = p.psum("bc_ps", [128, 512], F32, stack=st)
        for nm, srow, dst in [("bv", self.bkuv_row, self.bvbc),
                              ("n2", self.n2w_row, self.n2bc)]:
            p.w("sync", self.s_pe)
            p.dma("sync", row[:], srow, self.s_ld)
            for h in range(2):
                p.w("tensor", self.s_ld)
                p.w("tensor", self.s_act)
                p.op("tensor", "matmul", psb[:], self.onesr_f[:],
                     row[:, h * 512:(h + 1) * 512], start=True, stop=True,
                     sem=self.s_pe)
                p.w("scalar", self.s_pe)
                p.op("scalar", "activation", dst[:, h * 512:(h + 1) * 512],
                     psb[:], AF.Copy, sem=self.s_act)
        p.barrier("bc_done")
        st.close()

    def phase_latent(self):
        p = self.p
        p.barrier("lat_start")
        st = ExitStack()
        wkdb = p.sb("wkdb", [128, 8 * DL], BF16, stack=st)
        pp = [p.psum(f"lat_ps{i}", [128, 512], F32, stack=st) for i in range(2)]
        p.dma("gpsimd", wkdb[:],
              self.wsl(self.Wkd, DL, 0, DL, 8), self.s_gld)
        p.w("tensor", self.s_gld)
        p.w("tensor", self.s_dve)
        evac = {}
        k = 0
        for jc in range(2):
            for tb in range(8):
                b = k % 2
                if k >= 2:
                    p.w("tensor", self.s_dve, evac[k - 2])
                for ic in range(8):
                    p.op("tensor", "matmul", pp[b][:],
                         wkdb[:, ic * DL + jc * 128: ic * DL + (jc + 1) * 128],
                         self.xnT[:, ic * T + tb * 512: ic * T + tb * 512 + 512],
                         start=(ic == 0), stop=(ic == 7),
                         sem=self.s_pe if ic == 7 else None)
                p.w("vector", self.s_pe)
                p.op("vector", "tensor_scalar_add",
                     self.latT[:, jc * T + tb * 512: jc * T + tb * 512 + 512],
                     pp[b][:], self.bcol(B_BKD + jc), sem=self.s_dve)
                evac[k] = self.s_dve.n
                k += 1
        self.dump("d_latT", self.latT[:, 0:4096], [128, 4096], BF16)
        self.dump("d_wkdb", wkdb[:], [128, 2048], BF16)
        self.dump("d_xnT2", self.xnT[:, 0:4096], [128, 4096], BF16)
        p.w("sync", self.s_dve)
        p.dma("sync", self.xnTpark, self.xnT[:], self.s_st)
        p.w("sync", self.s_st)
        p.barrier("lat_done")
        st.close()

    def phase_qprod(self, CSo, st01):
        p = self.p
        p.barrier("qprod_start")
        st = ExitStack()
        wqb = p.sb("wqb", [128, 8 * D], BF16, stack=st)
        fchunk = p.sb("fwq_c", [128, D], F32, stack=st)
        fwqb = p.sb("fwqb", [128, 8 * D], BF16, stack=st)
        qtile = [p.sb(f"qtile{i}", [128, 512], BF16, stack=st) for i in range(2)]
        q1f = p.sb("q1f", [128, 512], F32, stack=st)
        qb1 = p.sb("qb1", [128, 512], BF16, stack=st)
        t1f = p.sb("t1f", [128, 512], F32, stack=st)
        t2f = p.sb("t2f", [128, 512], F32, stack=st)
        sqf = p.sb("sqf", [128, 512], F32, stack=st)
        sqfB = p.sb("sqfB", [128, 512], F32, stack=st)
        ssrow = p.sb("ssrow", [2, 512], F32, stack=st)
        ssrB = p.sb("qp_ssrB", [2, 512], F32, stack=st)
        ssl = p.sb("qp_ssl", [2, 512], F32, stack=st)
        sslB = p.sb("qp_sslB", [2, 512], F32, stack=st)
        bsb = p.sb("bsb", [128, 512], F32, stack=st)
        bsbB = p.sb("bsbB", [128, 512], F32, stack=st)
        q1f2 = [q1f, p.sb("q1fB", [128, 512], F32, stack=st)]
        qb12 = [qb1, p.sb("qb1B", [128, 512], BF16, stack=st)]
        pbank = [p.psum(f"qp_pb{i}", [128, 512], F32, stack=st) for i in range(7)]
        pa = pbank[0:2]
        pPq = pbank[3:5]

        p.dma("gpsimd", wqb[:],
              self.wsl(self.Wq, D, 0, D, 8), self.s_gld)
        for ic in range(8):
            p.w("sync", self.s_dve)
            p.dma("sync", fchunk[:], self.Fwq[ic * 128:(ic + 1) * 128, :], self.s_ld)
            p.w("vector", self.s_ld)
            p.op("vector", "tensor_scalar_mul", fwqb[:, ic * D:(ic + 1) * D],
                 fchunk[:], self.bcol(B_N1 + ic), sem=self.s_dve)
        p.w("tensor", self.s_gld)
        p.w("tensor", self.s_dve)

        evacq, dve_cp, pe_mm, pe_pi = {}, {}, {}, {}
        for k in range(33):
            if k >= 1:
                j = k - 1
                p.w("tensor", self.s_dve, dve_cp[j])
                p.op("tensor", "matmul", pPq[j % 2][:], self.pib[:], qb12[j % 2][:],
                     start=True, stop=True, sem=self.s_pe)
                pe_pi[j] = self.s_pe.n
            if k < 32:
                jc, tb = divmod(k, 4)
                if k >= 2:
                    p.w("tensor", self.s_dve, evacq[k - 2])
                for ic in range(8):
                    p.op("tensor", "matmul", pa[k % 2][:],
                         wqb[:, ic * D + jc * 128: ic * D + (jc + 1) * 128],
                         self.xnTo[:, ic * NTO + tb * 512: ic * NTO + tb * 512 + 512],
                         start=(ic == 0), stop=(ic == 7),
                         sem=self.s_pe if ic == 7 else None)
                pe_mm[k] = self.s_pe.n
            if k >= 1:
                j = k - 1
                jcj, tbj = divmod(j, 4)
                p.op("vector", "tensor_mul", t1f[:], q1f2[j % 2][:],
                     CSo[:, tbj * 512: tbj * 512 + 512])
            if k < 32:
                jc, tb = divmod(k, 4)
                p.w("vector", self.s_pe, pe_mm[k])
                p.op("vector", "tensor_scalar", q1f2[k % 2][:], pa[k % 2][:],
                     self.bcol(B_BQ + jc), 0.125, op0=ALU.add, op1=ALU.mult,
                     sem=self.s_dve)
                evacq[k] = self.s_dve.n
                p.op("vector", "tensor_copy", qb12[k % 2][:], q1f2[k % 2][:],
                     sem=self.s_dve)
                dve_cp[k] = self.s_dve.n
            if k >= 1:
                j = k - 1
                jcj, tbj = divmod(j, 4)
                p.w("vector", self.s_pe, pe_pi[j])
                p.op("vector", "scalar_tensor_tensor", t2f[:], pPq[j % 2][:],
                     self.sgn_s[:], CSo[:, NTO + tbj * 512: NTO + tbj * 512 + 512],
                     op0=ALU.mult, op1=ALU.mult)
                p.op("vector", "tensor_add",
                     self.qsp[:, jcj * NTO + tbj * 512: jcj * NTO + tbj * 512 + 512],
                     t1f[:], t2f[:], sem=self.s_dve)

        ring = [pbank[0], pbank[1], pbank[2]]
        sqf2 = [sqf, sqfB]
        bsb2 = [bsb, bsbB]
        ssl2 = [ssl, sslB]
        ssrow2 = [ssrow, ssrB]
        evac, stcnt = {}, {}
        pe_mm, pe_hs, pe_bc = {}, {}, {}
        act_sq, act_ex, act_cp = {}, {}, {}
        p.w("tensor", self.s_dve)
        for k in range(32 + 2):
            if k < 32:
                jc, tb = divmod(k, 4)
                if k >= 3:
                    p.w("tensor", self.s_dve, evac[k - 3])
                for ic in range(8):
                    p.op("tensor", "matmul", ring[k % 3][:],
                         fwqb[:, ic * D + jc * 128: ic * D + (jc + 1) * 128],
                         self.xnTo[:, ic * NTO + tb * 512: ic * NTO + tb * 512 + 512],
                         start=(ic == 0), stop=(ic == 7),
                         sem=self.s_pe if ic == 7 else None)
                pe_mm[k] = self.s_pe.n
            if 1 <= k <= 32:
                j = k - 1
                p.w("tensor", self.s_act, act_sq[j])
                p.op("tensor", "matmul", pbank[3 + j % 2][0:2, :], self.hselb[:],
                     sqf2[j % 2][:], start=True, stop=True, sem=self.s_pe)
                pe_hs[j] = self.s_pe.n
            if k >= 2:
                i2 = k - 2
                p.w("tensor", self.s_act, act_ex[i2])
                p.op("tensor", "matmul", pbank[5 + i2 % 2][:],
                     self.sel2b[:], ssrow2[i2 % 2][:],
                     start=True, stop=True, sem=self.s_pe)
                pe_bc[i2] = self.s_pe.n
            if k < 32:
                p.w("scalar", self.s_pe, pe_mm[k])
                p.op("scalar", "activation", sqf2[k % 2][:],
                     ring[k % 3][:], AF.Square, sem=self.s_act)
                act_sq[k] = self.s_act.n
            if 1 <= k <= 32:
                j = k - 1
                p.w("scalar", self.s_pe, pe_hs[j])
                p.op("scalar", "activation", ssl2[j % 2][:],
                     pbank[3 + j % 2][0:2, :], AF.Ln,
                     scale=1.0 / 64.0, bias=EPSR)
                p.op("scalar", "activation", self.spcr[:], self.spcr[:],
                     AF.Square)
                p.op("scalar", "activation", ssrow2[j % 2][:],
                     ssl2[j % 2][:], AF.Exp, scale=-0.5, sem=self.s_act)
                act_ex[j] = self.s_act.n
            if k >= 2:
                i2 = k - 2
                jc2, tb2 = divmod(i2, 4)
                p.w("scalar", self.s_pe, pe_bc[i2])
                if i2 >= 2:
                    p.w("scalar", self.s_dve, evac[i2 - 2])
                p.op("scalar", "activation", bsb2[i2 % 2][:],
                     pbank[5 + i2 % 2][:], AF.Copy, sem=self.s_act)
                act_cp[i2] = self.s_act.n
                p.w("vector", self.s_act, act_cp[i2])
                if i2 >= 2:
                    p.w("vector", self.s_st, stcnt[i2 - 2])
                p.op("vector", "scalar_tensor_tensor", qtile[i2 % 2][:],
                     ring[i2 % 3][:], self.bcol(B_QNW), bsb2[i2 % 2][:],
                     op0=ALU.mult, op1=ALU.mult, sem=self.s_dve)
                evac[i2] = self.s_dve.n
                p.w("sync", self.s_dve, evac[i2])
                p.dma("sync",
                      self.qcapark[:, jc2 * NTO + tb2 * 512: jc2 * NTO + tb2 * 512 + 512],
                      qtile[i2 % 2][:], self.s_st)
                stcnt[i2] = self.s_st.n
        self.dump("d_qsp", self.qsp[:, 0:2048], [128, 2048], BF16)
        p.w("sync", self.s_st)
        p.barrier("qprod_done")
        st.close()

    def phase_p1(self):
        import os
        NG = int(os.environ.get("MK_P1_G", "2"))
        NSL = int(os.environ.get("MK_P1_S", "4"))
        SKIP = int(os.environ.get("MK_P1_SKIP", "0"))
        p = self.p
        p.barrier("p1_start")
        st = ExitStack()
        KTg = p.sb("KTg", [128, 4 * T], BF16, stack=st)
        Vg = p.sb("Vg", [128, 32 * 520], BF16, stack=st)
        wq_ = [p.sb(f"wq_{i}", [128, 2 * 512], BF16, stack=st) for i in range(2)]
        Astage = p.sb("Astage", [128, 8 * 512], F32, stack=st)
        OT = p.sb("OT", [128, 4 * 512], BF16, stack=st)
        pT = [p.sb(f"pT{i}", [128, 1024], BF16, stack=st) for i in range(3)]
        ktmp = pT[0][:, 0:512]
        scr = p.sb("p1_scr", [128, 1024], F32, stack=st)
        t1f = scr[:, 0:512]
        t2f = scr[:, 512:1024]
        mkf4 = p.sb("p1_mkf", [128, 4], F32, stack=st)
        linv2 = p.sb("p1_linv", [1, 1024], F32, stack=st)
        sig2 = p.sb("p1_sig", [1, 1024], F32, stack=st)
        lrow2 = p.sb("p1_lrow", [1, 1024], F32, stack=st)
        sig2s = lrow2
        ouf2 = scr[0:64, :]
        wkuk = p.sb("wkuk", [128, 2 * 512], BF16, stack=st)
        wkuv = p.sb("wkuv", [128, 2 * 512], BF16, stack=st)
        wob = p.sb("wob", [128, 4 * D], BF16, stack=st)
        pS01 = p.psum("pS01", [128, 1024], F32, stack=st)
        pS23 = p.psum("pS23", [128, 1024], F32, stack=st)
        pF = p.psum("pF", [128, 1024], F32, stack=st)
        pSb = [pS01, pS23, pF]
        pO = [p.psum(f"pO{i}", [128, 512], F32, stack=st) for i in range(2)]

        s_ld, s_gld, s_pe, s_act, s_dve, s_st = (self.s_ld, self.s_gld, self.s_pe,
                                                 self.s_act, self.s_dve, self.s_st)
        CSc = self.CS[:, 0:T]
        CSs = self.CS[:, T:2 * T]

        for g in range(NG):
            p.w("gpsimd", s_pe)
            p.dma("gpsimd", wkuk[:],
                  self.wsl(self.Wku, 2 * D, g * 512, 512, 2), s_gld)
            p.dma("gpsimd", wkuv[:],
                  self.wsl(self.Wku, 2 * D, D + g * 512, 512, 2), s_gld)
            p.dma("gpsimd", wob[:],
                  man_ap(self.Wo, g * 512 * D, [[D, 128], [128 * D, 4], [1, D]]), s_gld)
            p.w("tensor", s_gld)
            ktmp2 = [pT[0][:, 0:512], pT[0][:, 512:1024]]
            pPi2 = [pF[:, 0:512], pF[:, 512:1024]]
            evac, dve_kt, pe_mm, pe_pi = {}, {}, {}, {}
            for k in range(33):
                if k >= 1:
                    j = k - 1
                    p.w("tensor", s_dve, dve_kt[j])
                    p.op("tensor", "matmul", pPi2[j % 2], self.pib[:],
                         ktmp2[j % 2], start=True, stop=True, sem=s_pe)
                    pe_pi[j] = s_pe.n
                if k < 32:
                    ci, tb = divmod(k, 8)
                    if k >= 2:
                        p.w("tensor", s_dve, evac[k - 2])
                    for lc in range(2):
                        p.op("tensor", "matmul", pO[k % 2][:],
                             wkuk[:, lc * 512 + ci * 128: lc * 512 + (ci + 1) * 128],
                             self.latT[:, lc * T + tb * 512: lc * T + tb * 512 + 512],
                             start=(lc == 0), stop=(lc == 1),
                             sem=s_pe if lc == 1 else None)
                    pe_mm[k] = s_pe.n
                if k >= 1:
                    j = k - 1
                    cj, tbj = divmod(j, 8)
                    bcolj = self.bcol(B_BKUK + g * 4 + cj)
                    p.op("vector", "scalar_tensor_tensor", t1f, pO[j % 2][:],
                         bcolj, CSc[:, tbj * 512: tbj * 512 + 512],
                         op0=ALU.add, op1=ALU.mult)
                if k < 32:
                    ci, tb = divmod(k, 8)
                    p.w("vector", s_pe, pe_mm[k])
                    p.op("vector", "tensor_scalar_add", ktmp2[k % 2], pO[k % 2][:],
                         self.bcol(B_BKUK + g * 4 + ci), sem=s_dve)
                    dve_kt[k] = s_dve.n
                if k >= 1:
                    j = k - 1
                    cj, tbj = divmod(j, 8)
                    p.w("vector", s_pe, pe_pi[j])
                    p.op("vector", "scalar_tensor_tensor", t2f, pPi2[j % 2],
                         self.sgn_s[:], CSs[:, tbj * 512: tbj * 512 + 512],
                         op0=ALU.mult, op1=ALU.mult)
                    p.op("vector", "tensor_add",
                         KTg[:, cj * T + tbj * 512: cj * T + tbj * 512 + 512],
                         t1f, t2f, sem=s_dve)
                    evac[j] = s_dve.n
            for ci in range(4):
                p.op("vector", "reduce_sum", mkf4[:, ci:ci + 1],
                     KTg[:, ci * T:(ci + 1) * T], AX.X)
            p.drain("vector")
            p.op("vector", "tensor_single_scalar", self.mk8[:, 0:4],
                 mkf4[:], 8.0 / 4096.0, ALU.mult)
            p.drain("vector", sem=s_dve)
            p.op("vector", "memset",
                 man_ap(Vg[:, :], 64, [[32 * 520, 128], [520, 32], [65, 8]]), 1.0)
            p.drain("vector", sem=s_dve)
            p.w("tensor", s_dve)
            evac, pe_mm = {}, {}
            for tt in range(33):
                if tt < 32:
                    if tt >= 2:
                        p.w("tensor", s_dve, evac[tt - 2])
                    for lc in range(2):
                        p.op("tensor", "matmul", pPi2[tt % 2],
                             self.latT[:, lc * T + tt * 128: lc * T + (tt + 1) * 128],
                             wkuv[:, lc * 512:(lc + 1) * 512],
                             start=(lc == 0), stop=(lc == 1),
                             sem=s_pe if lc == 1 else None)
                    pe_mm[tt] = s_pe.n
                if tt >= 1:
                    j = tt - 1
                    p.w("vector", s_pe, pe_mm[j])
                    p.op("vector", "tensor_add",
                         man_ap(Vg[:, :], j * 520, [[32 * 520, 128], [65, 8], [1, 64]]),
                         man_ap(pF[:, :], (j % 2) * 512, [[1024, 128], [64, 8], [1, 64]]),
                         man_ap(self.bvbc[:, :], g * 512, [[D, 128], [64, 8], [1, 64]]),
                         sem=s_dve)
                    evac[j] = s_dve.n
            for hl in range(8):
                p.w("tensor", s_dve)
                p.w("tensor", s_act)
                for tt in range(32):
                    p.op("tensor", "matmul", pS01[0:1, 0:65], self.ones1b[:],
                         Vg[:, tt * 520 + hl * 65: tt * 520 + hl * 65 + 65],
                         start=(tt == 0), stop=(tt == 31),
                         sem=s_pe if tt == 31 else None)
                p.w("scalar", s_pe)
                p.op("scalar", "activation",
                     self.mv02[0:1, (g * 8 + hl) * 64:(g * 8 + hl) * 64 + 64],
                     pS01[0:1, 0:64], AF.Copy, sem=s_act)
            p.w("vector", s_act)
            p.op("vector", "tensor_scalar",
                 self.mv02[0:1, g * 512:(g + 1) * 512],
                 self.mv02[0:1, g * 512:(g + 1) * 512],
                 self.co[0:1, C_02:C_02 + 1], 1.0 / 4096.0,
                 op0=ALU.mult, op1=ALU.mult)
            p.drain("vector", sem=s_dve)

            for s in range(NSL):
                segq = list(range(16)) * 4
                ldcnt = {}
                segwm = {}

                def load_seg(seg, s=s, segq=segq, ldcnt=ldcnt, segwm=segwm):
                    qtr = segq[seg]
                    base = A_OFF + 1024 * s - 128 * (2 * qtr + 1)
                    if seg >= 2:
                        p.w("sync", s_dve, segwm[seg - 2])
                    else:
                        p.w("sync", s_dve)
                    p.dma("sync", wq_[seg % 2][:],
                          man_ap(self.big2, base, [[W2, 128], [128, 2], [1, 512]]),
                          s_ld)
                    ldcnt[seg] = s_ld.n

                load_seg(0)
                ou_act = 0
                fin_dve = 0
                prev_expd = None
                for pq in range(4):
                    if SKIP & 1:
                        break
                    ci = pq
                    cq = g * 4 + ci
                    h0 = pq * 2
                    expd = {}
                    wmulcnt = {}
                    vcnt = {}

                    def emit_v(kt, pq=pq, h0=h0, vcnt=vcnt, wmulcnt=wmulcnt):
                        p.w("tensor", s_dve, wmulcnt[kt])
                        if kt == 0:
                            p.w("tensor", s_dve, fin_dve)
                            p.w("tensor", s_act, ou_act)
                        b3 = kt % 3
                        p.op("tensor", "matmul", pO[0][0:65, :],
                             Vg[:, kt * 520 + h0 * 65: kt * 520 + h0 * 65 + 65],
                             pT[b3][:, 0:512],
                             start=(kt == 0), stop=(kt == 31))
                        p.op("tensor", "matmul", pO[1][0:65, :],
                             Vg[:, kt * 520 + (h0 + 1) * 65: kt * 520 + (h0 + 1) * 65 + 65],
                             pT[b3][:, 512:1024],
                             start=(kt == 0), stop=(kt == 31), sem=s_pe)
                        vcnt[kt] = s_pe.n

                    for kt in range(32):
                        seg = pq * 16 + kt // 2
                        if kt % 2 == 0 and seg + 1 < 64:
                            load_seg(seg + 1)
                        r = 1 - (kt % 2)
                        wtile = wq_[seg % 2][:, r * 512:(r + 1) * 512]
                        pS = pSb[kt % 3]
                        if kt >= 3:
                            p.w("tensor", s_act, expd[kt - 3])
                        elif pq == 0:
                            p.w("tensor", s_act)
                            p.w("tensor", s_dve)
                        else:
                            p.w("tensor", s_act, ou_act)
                        p.op("tensor", "matmul", pS[:, 0:512],
                             KTg[0:64, ci * T + kt * 128: ci * T + kt * 128 + 128],
                             self.qsp[0:64, cq * NTO + s * 512: cq * NTO + s * 512 + 512],
                             start=True, stop=True, tile_position=(0, 0))
                        p.op("tensor", "matmul", pS[:, 512:1024],
                             KTg[64:128, ci * T + kt * 128: ci * T + kt * 128 + 128],
                             self.qsp[64:128, cq * NTO + s * 512: cq * NTO + s * 512 + 512],
                             start=True, stop=True, tile_position=(64, 0),
                             sem=s_pe)
                        p.w("scalar", s_pe)
                        if kt >= 3:
                            p.w("scalar", s_pe, vcnt[kt - 3])
                        p.op("scalar", "activation", pT[kt % 3][:], pS[:],
                             AF.Exp, bias=CSTAB, sem=s_act)
                        expd[kt] = s_act.n
                        p.w("vector", s_act)
                        p.w("vector", s_ld, ldcnt[seg])
                        p.op("vector", "tensor_mul", pT[kt % 3][:, 0:512],
                             pT[kt % 3][:, 0:512], wtile)
                        p.op("vector", "tensor_mul", pT[kt % 3][:, 512:1024],
                             pT[kt % 3][:, 512:1024], wtile, sem=s_dve)
                        wmulcnt[kt] = s_dve.n
                        if kt % 2 == 1:
                            segwm[seg] = s_dve.n
                        if kt >= 2:
                            emit_v(kt - 2)
                    emit_v(30)
                    emit_v(31)
                    prev_expd = expd
                    if SKIP & 2:
                        continue
                    p.w("scalar", s_pe, vcnt[31])
                    p.w("scalar", s_dve, fin_dve)
                    p.op("scalar", "activation", lrow2[0:1, 0:512],
                         pO[0][64:65, :], AF.Copy)
                    p.op("scalar", "activation", lrow2[0:1, 512:1024],
                         pO[1][64:65, :], AF.Copy)
                    p.op("scalar", "activation", ouf2[:, 0:512], pO[0][0:64, :],
                         AF.Copy)
                    p.op("scalar", "activation", lrow2[0:1, 0:1024],
                         lrow2[0:1, 0:1024], AF.Ln)
                    p.op("scalar", "activation", ouf2[:, 512:1024],
                         pO[1][0:64, :], AF.Copy)
                    p.op("scalar", "activation", linv2[0:1, 0:1024],
                         lrow2[0:1, 0:1024], AF.Exp, scale=-1.0, sem=s_act)
                    p.w("tensor", s_act)
                    p.op("tensor", "matmul", pO[0][0:1, :],
                         self.mk8[0:64, ci:ci + 1],
                         self.qsp[0:64, cq * NTO + s * 512: cq * NTO + s * 512 + 512],
                         start=True, stop=True, tile_position=(0, 0))
                    p.op("tensor", "matmul", pO[1][0:1, :],
                         self.mk8[64:128, ci:ci + 1],
                         self.qsp[64:128, cq * NTO + s * 512: cq * NTO + s * 512 + 512],
                         start=True, stop=True, tile_position=(64, 0), sem=s_pe)
                    p.w("scalar", s_pe)
                    p.op("scalar", "activation", sig2[0:1, 0:512],
                         pO[0][0:1, :], AF.Exp, scale=-1.0)
                    p.op("scalar", "activation", sig2[0:1, 512:1024],
                         pO[1][0:1, :], AF.Exp, scale=-1.0)
                    p.op("scalar", "activation", sig2s[0:1, 0:512],
                         sig2[0:1, 0:512], AF.Ln, bias=1.0)
                    p.op("scalar", "activation", sig2s[0:1, 512:1024],
                         sig2[0:1, 512:1024], AF.Ln, bias=1.0)
                    p.op("scalar", "activation", sig2[0:1, 0:512],
                         sig2s[0:1, 0:512], AF.Exp, scale=-1.0)
                    p.op("scalar", "activation", sig2[0:1, 512:1024],
                         sig2s[0:1, 512:1024], AF.Exp, scale=-1.0, sem=s_act)
                    p.w("tensor", s_act)
                    gh = g * 8 + h0
                    p.op("tensor", "matmul", pO[0][0:64, :], self.ones64_f[:],
                         linv2[0:1, 0:512], start=True, stop=True)
                    p.op("tensor", "matmul", pO[0][64:128, :], self.ones64_f[:],
                         linv2[0:1, 512:1024], start=True, stop=True,
                         tile_position=(0, 64))
                    p.op("tensor", "matmul", pO[1][0:64, :],
                         self.mv02[0:1, gh * 64: gh * 64 + 64],
                         sig2[0:1, 0:512], start=True, stop=True)
                    p.op("tensor", "matmul", pO[1][64:128, :],
                         self.mv02[0:1, (gh + 1) * 64: (gh + 1) * 64 + 64],
                         sig2[0:1, 512:1024], start=True, stop=True,
                         tile_position=(0, 64), sem=s_pe)
                    ou_act = s_act.n
                    p.w("vector", s_pe)
                    p.op("vector", "tensor_mul", ouf2[:, 0:512], ouf2[:, 0:512],
                         pO[0][0:64, :])
                    p.op("vector", "tensor_mul", ouf2[:, 512:1024],
                         ouf2[:, 512:1024], pO[0][64:128, :])
                    p.op("vector", "tensor_add",
                         OT[0:64, ci * 512:(ci + 1) * 512],
                         ouf2[:, 0:512], pO[1][0:64, :])
                    p.op("vector", "tensor_add",
                         OT[64:128, ci * 512:(ci + 1) * 512],
                         ouf2[:, 512:1024], pO[1][64:128, :], sem=s_dve)
                    fin_dve = s_dve.n
                if g == 1:
                    p.w("sync", s_st)
                    p.dma("sync", Astage[:],
                          self.park_ap(self.Apark, s, 8), s_ld)
                    p.w("vector", s_ld)
                else:
                    p.w("scalar", s_st)
                evw = {}
                for jc in range(0 if SKIP & 4 else 8):
                    ps = pO[jc % 2]
                    if jc >= 2:
                        p.w("tensor", s_act if g == 0 else s_dve, evw[jc - 2])
                    else:
                        p.w("tensor", s_dve)
                    for ic in range(4):
                        p.op("tensor", "matmul", ps[:],
                             wob[:, ic * D + jc * 128: ic * D + (jc + 1) * 128],
                             OT[:, ic * 512:(ic + 1) * 512],
                             start=(ic == 0), stop=(ic == 3),
                             sem=s_pe if ic == 3 else None)
                    if g == 0:
                        p.w("scalar", s_pe)
                        p.op("scalar", "activation",
                             Astage[:, jc * 512:(jc + 1) * 512], ps[:], AF.Copy,
                             sem=s_act)
                        evw[jc] = s_act.n
                    else:
                        p.w("vector", s_pe)
                        p.op("vector", "scalar_tensor_tensor",
                             Astage[:, jc * 512:(jc + 1) * 512], ps[:],
                             self.bcol(B_BO + jc),
                             Astage[:, jc * 512:(jc + 1) * 512],
                             op0=ALU.add, op1=ALU.add, sem=s_dve)
                        evw[jc] = s_dve.n
                p.w("sync", s_act if g == 0 else s_dve)
                p.dma("sync", self.park_ap(self.Apark, s, 8),
                      Astage[:], s_st)
        self.dump("d_KTg", KTg[:, 0:4096], [128, 4096], BF16)
        self.dump("d_Vg", Vg[:, 0:4160], [128, 4160], BF16)
        self.dump("d_A0", Astage[:], [128, 4096], F32)
        p.w("sync", self.s_st)
        p.barrier("p1_done")
        st.close()

    def phase_p2(self):
        p = self.p
        p.barrier("p2_start")
        st = ExitStack()
        aostage = p.sb("aostage", [128, 6 * QB], F32, stack=st)
        xnT2 = p.sb("xnT2", [128, 8 * T], BF16, stack=st)
        qca = p.sb("qca", [128, 8 * NTO], BF16, stack=st)
        KT2 = p.sb("KT2", [128, 2 * T], BF16, stack=st)
        V2 = p.sb("V2", [128, 32 * 260], BF16, stack=st)
        mslab = p.sb("mslab", [128, 8 * 512], BF16, stack=st)
        qposr = p.sb("qposr", [1, 512], F32, stack=st)
        fchunk = p.sb("p2_fc", [128, 256], F32, stack=st)
        kslab = p.sb("kslab", [128, 8 * 256], BF16, stack=st)
        vslab = p.sb("vslab", [128, 8 * 256], BF16, stack=st)
        fwogb = p.sb("fwogb", [128, 2 * 768], BF16, stack=st)
        OT2 = p.sb("OT2", [128, 2 * 512], BF16, stack=st)
        pT = [p.sb(f"p2_pT{i}", [128, 1024], BF16, stack=st) for i in range(3)]
        sqf = p.sb("p2_sqf", [128, 512], F32, stack=st)
        sqfB = p.sb("p2_sqfB", [128, 512], F32, stack=st)
        ssrow = p.sb("p2_ss", [2, 512], F32, stack=st)
        ssrB = p.sb("p2_ssrB", [2, 512], F32, stack=st)
        ssl = p.sb("p2_ssl", [2, 512], F32, stack=st)
        sslB = p.sb("p2_sslB", [2, 512], F32, stack=st)
        bsb = p.sb("p2_bsb", [128, 512], F32, stack=st)
        bsbB = p.sb("p2_bsbB", [128, 512], F32, stack=st)
        lrow2 = p.sb("p2_lrow", [1, 1024], F32, stack=st)
        linv2 = p.sb("p2_linv", [1, 1024], F32, stack=st)
        ouf2b = p.sb("p2_ou2", [64, 1024], F32, stack=st)
        ouf2 = ouf2b[:, :]
        pS01 = p.psum("p2_pS01", [128, 1024], F32, stack=st)
        pS23 = p.psum("p2_pS23", [128, 1024], F32, stack=st)
        pF = p.psum("p2_pF", [128, 1024], F32, stack=st)
        pSb = [pS01, pS23, pF]
        pO = [p.psum(f"p2_pO{i}", [128, 512], F32, stack=st) for i in range(2)]
        s_ld, s_gld, s_pe, s_act, s_dve, s_st = (self.s_ld, self.s_gld, self.s_pe,
                                                 self.s_act, self.s_dve, self.s_st)

        xldc = {}
        for ic in range(8):
            p.dma("sync", xnT2[:, ic * T:(ic + 1) * T],
                  man_ap(self.xnTpark, ic * T, [[8 * T, 128], [1, T]]), s_ld)
            xldc[ic] = s_ld.n
        p.dma("sync", qca[:], self.qcapark, s_ld)
        for ic in range(8):
            p.w("vector", s_ld, xldc[ic])
            p.op("vector", "tensor_scalar_mul", xnT2[:, ic * T:(ic + 1) * T],
                 xnT2[:, ic * T:(ic + 1) * T], self.bcol(B_N1 + ic),
                 sem=s_dve if ic == 7 else None)

        for g in range(4):
            p.w("gpsimd", s_pe)
            p.dma("gpsimd", kslab[:],
                  man_ap(self.Fwk, g * 256, [[D, 128], [128 * D, 8], [1, 256]]),
                  s_gld)
            p.dma("gpsimd", vslab[:],
                  man_ap(self.Fwv, g * 256, [[D, 128], [128 * D, 8], [1, 256]]),
                  s_gld)
            p.dma("gpsimd", fwogb[:],
                  man_ap(self.Fwo, g * 256 * D + 256, [[D, 128], [128 * D, 2], [1, 768]]), s_gld)
            p.w("tensor", s_ld)
            p.w("tensor", s_gld)
            p.w("tensor", s_dve)
            kps = [pO[0], pO[1], pF[:, 0:512]]
            sqf2 = [sqf, sqfB]
            bsb2 = [bsb, bsbB]
            ssl2 = [ssl, sslB]
            ssrow2 = [ssrow, ssrB]
            evac = {}
            pe_mm, pe_hs, pe_bc = {}, {}, {}
            act_sq, act_ex, act_cp = {}, {}, {}
            for k in range(16 + 2):
                if k < 16:
                    ci, tb = divmod(k, 8)
                    if k >= 3:
                        p.w("tensor", s_dve, evac[k - 3])
                    for ic in range(8):
                        p.op("tensor", "matmul", kps[k % 3][:],
                             kslab[:, ic * 256 + ci * 128: ic * 256 + (ci + 1) * 128],
                             xnT2[:, ic * T + tb * 512: ic * T + tb * 512 + 512],
                             start=(ic == 0), stop=(ic == 7),
                             sem=s_pe if ic == 7 else None)
                    pe_mm[k] = s_pe.n
                if 1 <= k <= 16:
                    j = k - 1
                    p.w("tensor", s_act, act_sq[j])
                    p.op("tensor", "matmul", pSb[j % 2][0:2, 0:512], self.hselb[:],
                         sqf2[j % 2][:], start=True, stop=True, sem=s_pe)
                    pe_hs[j] = s_pe.n
                if k >= 2:
                    i2 = k - 2
                    p.w("tensor", s_act, act_ex[i2])
                    p.op("tensor", "matmul", pSb[i2 % 2][:, 512:1024],
                         self.sel2b[:], ssrow2[i2 % 2][:],
                         start=True, stop=True, sem=s_pe)
                    pe_bc[i2] = s_pe.n
                if k < 16:
                    p.w("scalar", s_pe, pe_mm[k])
                    p.op("scalar", "activation", sqf2[k % 2][:],
                         kps[k % 3][:], AF.Square, sem=s_act)
                    act_sq[k] = s_act.n
                if 1 <= k <= 16:
                    j = k - 1
                    p.w("scalar", s_pe, pe_hs[j])
                    p.op("scalar", "activation", ssl2[j % 2][:],
                         pSb[j % 2][0:2, 0:512], AF.Ln,
                         scale=1.0 / 64.0, bias=EPSR)
                    p.op("scalar", "activation", self.spcr[:], self.spcr[:],
                         AF.Square)
                    p.op("scalar", "activation", ssrow2[j % 2][:],
                         ssl2[j % 2][:], AF.Exp, scale=-0.5, sem=s_act)
                    act_ex[j] = s_act.n
                if k >= 2:
                    i2 = k - 2
                    p.w("scalar", s_pe, pe_bc[i2])
                    if i2 >= 2:
                        p.w("scalar", s_dve, evac[i2 - 2])
                    p.op("scalar", "activation", bsb2[i2 % 2][:],
                         pSb[i2 % 2][:, 512:1024], AF.Copy, sem=s_act)
                    act_cp[i2] = s_act.n
                    ci2, tb2 = divmod(i2, 8)
                    p.w("vector", s_act, act_cp[i2])
                    p.op("vector", "scalar_tensor_tensor",
                         KT2[:, ci2 * T + tb2 * 512: ci2 * T + tb2 * 512 + 512],
                         kps[i2 % 3][:], self.bcol(B_KNW), bsb2[i2 % 2][:],
                         op0=ALU.mult, op1=ALU.mult, sem=s_dve)
                    evac[i2] = s_dve.n
            p.w("tensor", s_dve, evac[15])
            p.op("vector", "memset",
                 man_ap(V2[:, :], 64, [[32 * 260, 128], [260, 32], [65, 4]]), 1.0)
            p.drain("vector", sem=s_dve)
            evac = {}
            for tt in range(32):
                ps = pF
                if tt >= 1:
                    p.w("tensor", s_dve, evac[tt - 1])
                for ic in range(8):
                    p.op("tensor", "matmul", ps[:, 0:256],
                         xnT2[:, ic * T + tt * 128: ic * T + (tt + 1) * 128],
                         vslab[:, ic * 256:(ic + 1) * 256],
                         start=(ic == 0), stop=(ic == 7),
                         sem=s_pe if ic == 7 else None)
                p.w("vector", s_pe)
                p.op("vector", "tensor_single_scalar",
                     man_ap(V2[:, :], tt * 260, [[32 * 260, 128], [65, 4], [1, 64]]),
                     man_ap(pF[:, :], 0, [[1024, 128], [64, 4], [1, 64]]),
                     self.rstd1[:, tt:tt + 1], ALU.mult, sem=s_dve)
                evac[tt] = s_dve.n

            for s in range(4):
                KTN = 8 * s + 8
                p.w("sync", s_pe)
                p.dma("sync", qposr[:], self.qpos[:, s * 512:(s + 1) * 512], s_ld)
                p.w("tensor", s_ld)
                p.w("tensor", s_dve)
                p.op("tensor", "matmul", pO[0][:], self.onesr_f[:], qposr[:],
                     start=True, stop=True, sem=s_pe)
                p.w("vector", s_pe)
                for m in range(8):
                    p.op("vector", "tensor_single_scalar",
                         mslab[:, m * 512:(m + 1) * 512], pO[0][:],
                         self.kcol_s[:, 8 * s + m: 8 * s + m + 1], ALU.is_ge,
                         sem=s_dve if m == 7 else None)
                ou_act = 0
                fin_dve = 0
                prev_expd = None
                for pq in range(2):
                    ci = pq
                    cq = g * 2 + ci
                    h0 = pq * 2
                    expd = {}
                    concnt = {}
                    vcnt = {}

                    def emit_v(kt, pq=pq, h0=h0, vcnt=vcnt, concnt=concnt,
                               KTN=KTN):
                        sm, vv = concnt[kt]
                        if sm is not None:
                            p.w("tensor", sm, vv)
                        if kt == 0:
                            p.w("tensor", s_dve, fin_dve)
                            p.w("tensor", s_act, ou_act)
                            p.w("tensor", s_dve)
                        b3 = kt % 3
                        p.op("tensor", "matmul", pO[0][0:65, :],
                             V2[:, kt * 260 + h0 * 65: kt * 260 + h0 * 65 + 65],
                             pT[b3][:, 0:512],
                             start=(kt == 0), stop=(kt == KTN - 1))
                        p.op("tensor", "matmul", pO[1][0:65, :],
                             V2[:, kt * 260 + (h0 + 1) * 65: kt * 260 + (h0 + 1) * 65 + 65],
                             pT[b3][:, 512:1024],
                             start=(kt == 0), stop=(kt == KTN - 1), sem=s_pe)
                        vcnt[kt] = s_pe.n

                    for kt in range(KTN):
                        masked = kt >= 8 * s
                        pS = pSb[kt % 3]
                        if kt >= 3:
                            p.w("tensor", s_act, expd[kt - 3])
                        elif prev_expd is not None and KTN >= 3:
                            p.w("tensor", s_act, prev_expd[KTN - 3 + kt])
                        else:
                            p.w("tensor", s_act)
                            p.w("tensor", s_dve)
                        p.op("tensor", "matmul", pS[:, 0:512],
                             KT2[0:64, ci * T + kt * 128: ci * T + kt * 128 + 128],
                             qca[0:64, cq * NTO + s * 512: cq * NTO + s * 512 + 512],
                             start=True, stop=True, tile_position=(0, 0))
                        p.op("tensor", "matmul", pS[:, 512:1024],
                             KT2[64:128, ci * T + kt * 128: ci * T + kt * 128 + 128],
                             qca[64:128, cq * NTO + s * 512: cq * NTO + s * 512 + 512],
                             start=True, stop=True, tile_position=(64, 0),
                             sem=s_pe)
                        p.w("scalar", s_pe)
                        if kt >= 3:
                            sm, vv = concnt[kt - 3]
                            if sm is not None:
                                p.w("scalar", sm, vv)
                            p.w("scalar", s_pe, vcnt[kt - 3])
                        p.op("scalar", "activation", pT[kt % 3][:], pS[:],
                             AF.Exp, bias=CSTAB, sem=s_act)
                        expd[kt] = s_act.n
                        if masked:
                            m = kt - 8 * s
                            mt = mslab[:, m * 512:(m + 1) * 512]
                            p.w("vector", s_act)
                            p.op("vector", "tensor_mul", pT[kt % 3][:, 0:512],
                                 pT[kt % 3][:, 0:512], mt)
                            p.op("vector", "tensor_mul", pT[kt % 3][:, 512:1024],
                                 pT[kt % 3][:, 512:1024], mt, sem=s_dve)
                            concnt[kt] = (s_dve, s_dve.n)
                        else:
                            concnt[kt] = (None, None)
                        if kt >= 2:
                            emit_v(kt - 2)
                    if KTN >= 2:
                        emit_v(KTN - 2)
                    emit_v(KTN - 1)
                    prev_expd = expd
                    p.w("scalar", s_pe, vcnt[KTN - 1])
                    p.w("scalar", s_dve, fin_dve)
                    p.op("scalar", "activation", lrow2[0:1, 0:512],
                         pO[0][64:65, :], AF.Copy)
                    p.op("scalar", "activation", lrow2[0:1, 512:1024],
                         pO[1][64:65, :], AF.Copy)
                    p.op("scalar", "activation", ouf2[:, 0:512], pO[0][0:64, :],
                         AF.Copy)
                    p.op("scalar", "activation", lrow2[0:1, 0:1024],
                         lrow2[0:1, 0:1024], AF.Ln)
                    p.op("scalar", "activation", ouf2[:, 512:1024],
                         pO[1][0:64, :], AF.Copy)
                    p.op("scalar", "activation", linv2[0:1, 0:1024],
                         lrow2[0:1, 0:1024], AF.Exp, scale=-1.0, sem=s_act)
                    ou_act = s_act.n
                    p.w("tensor", s_act)
                    p.op("tensor", "matmul", pO[0][0:64, :], self.ones64_f[:],
                         linv2[0:1, 0:512], start=True, stop=True)
                    p.op("tensor", "matmul", pO[0][64:128, :], self.ones64_f[:],
                         linv2[0:1, 512:1024], start=True, stop=True,
                         tile_position=(0, 64), sem=s_pe)
                    p.w("vector", s_act)
                    p.w("vector", s_pe)
                    p.op("vector", "tensor_mul",
                         OT2[0:64, ci * 512:(ci + 1) * 512],
                         ouf2[:, 0:512], pO[0][0:64, :])
                    p.op("vector", "tensor_mul",
                         OT2[64:128, ci * 512:(ci + 1) * 512],
                         ouf2[:, 512:1024], pO[0][64:128, :], sem=s_dve)
                    fin_dve = s_dve.n
                if g > 0:
                    p.w("sync", s_st)
                    p.dma("sync", aostage[:],
                          self.park_ap(self.aopark, s, 6),
                          s_ld)
                    p.w("vector", s_ld)
                else:
                    p.w("scalar", s_st)
                evw = {}
                for jx in range(6):
                    ps = pO[jx % 2]
                    if jx >= 2:
                        p.w("tensor", s_act if g == 0 else s_dve, evw[jx - 2])
                    else:
                        p.w("tensor", s_dve)
                    for ic in range(2):
                        p.op("tensor", "matmul", ps[:],
                             fwogb[:, ic * 768 + jx * 128: ic * 768 + (jx + 1) * 128],
                             OT2[:, ic * 512:(ic + 1) * 512],
                             start=(ic == 0), stop=(ic == 1),
                             sem=s_pe if ic == 1 else None)
                    dst = aostage[:, jx * 512:(jx + 1) * 512]
                    if g == 0:
                        p.w("scalar", s_pe)
                        p.op("scalar", "activation", dst, ps[:], AF.Copy,
                             sem=s_act)
                        evw[jx] = s_act.n
                    else:
                        p.w("vector", s_pe)
                        p.op("vector", "tensor_add", dst, ps[:], dst, sem=s_dve)
                        evw[jx] = s_dve.n
                p.w("sync", s_act if g == 0 else s_dve)
                p.dma("sync", self.park_ap(self.aopark, s, 6),
                      aostage[:], s_st)
        self.dump("d_KT2", KT2[:, 0:4096], [128, 4096], BF16)
        self.dump("d_ao", aostage[:], [128, 3072], F32)
        p.barrier("p2_done")
        st.close()

    def phase_p3(self):
        p = self.p
        s_ld, s_gld, s_pe, s_act, s_dve, s_st = (self.s_ld, self.s_gld, self.s_pe,
                                                 self.s_act, self.s_dve, self.s_st)
        st = ExitStack()
        ptrA = [p.psum(f"p3_tr{i}", [128, 512], F32, stack=st) for i in range(2)]
        xfT = p.sb("xfT", [128, 8 * NTO], BF16, stack=st)
        wtsT = p.sb("wtsT", [4, NTO], F32, stack=st)
        moe = p.sb("moe", [128, 6 * NTO], F32, stack=st)

        stA = ExitStack()
        tok48 = p.sb("tok48", [128, 16 * 768], F32, stack=stA)
        aobuf = p.sb("p3_aob", [128, 6 * QB], F32, stack=stA)
        xnb = p.sb("p3_xnb", [128, D], F32, stack=stA)
        x1b = p.sb("p3_x1b", [128, D], F32, stack=stA)
        xfb = p.sb("p3_xfb", [128, D], BF16, stack=stA)
        sqd = p.sb("p3_sqd", [128, D], F32, stack=stA)
        rws = p.sb("p3_rws", [128, 4], F32, stack=stA)
        ptrB = [p.psum(f"p3_trb{i}", [128, 128], BF16, stack=stA) for i in range(2)]
        evac = {}
        k = 0
        for s in range(4):
            p.w("sync", s_pe)
            p.dma("sync", aobuf[:],
                  self.park_ap(self.aopark, s, 6), s_ld)
            p.w("tensor", s_ld)
            for u in range(4):
                tt = s * 4 + u
                for jx in range(6):
                    b = k % 2
                    if k >= 2:
                        p.w("tensor", s_dve, evac[k - 2])
                    p.op("tensor", "transpose", ptrA[b][:, 0:128],
                         aobuf[:, jx * 512 + u * 128: jx * 512 + u * 128 + 128],
                         self.id32[:], sem=s_pe)
                    p.w("vector", s_pe)
                    p.op("vector", "tensor_copy",
                         tok48[:, tt * 768 + jx * 128: tt * 768 + jx * 128 + 128],
                         ptrA[b][:, 0:128], sem=s_dve)
                    evac[k] = s_dve.n
                    k += 1
        evacX = {}
        evacT = {}
        kT = 0
        for tt in range(16):
            if tt >= 1:
                p.w("sync", s_dve, evacX[tt - 1])
            p.dma("sync", xnb[:], self.xnopark[tt * 128:(tt + 1) * 128, :], s_ld)
            p.w("vector", s_ld)
            p.w("vector", s_st)
            p.op("vector", "tensor_copy", x1b[:], xnb[:])
            p.op("vector", "tensor_add", x1b[:, 256:D], x1b[:, 256:D],
                 tok48[:, tt * 768:(tt + 1) * 768], sem=s_dve)
            evacX[tt] = s_dve.n
            p.w("sync", s_dve)
            p.dma("sync", self.x1park[tt * 128:(tt + 1) * 128, :], x1b[:], s_st)
            p.w("scalar", s_dve)
            p.op("scalar", "activation", sqd[:], x1b[:], AF.Square,
                 accum_out=rws[:, 0:1])
            p.op("scalar", "activation", rws[:, 1:2], rws[:, 0:1], AF.Ln,
                 scale=1.0 / 1024.0, bias=EPSR)
            p.op("scalar", "activation", sqd[:, 0:64], sqd[:, 0:64], AF.Square)
            p.op("scalar", "activation", rws[:, 2:3], rws[:, 1:2], AF.Exp,
                 scale=-0.5, sem=s_act)
            p.w("vector", s_act)
            p.op("vector", "scalar_tensor_tensor", xfb[:], x1b[:], rws[:, 2:3],
                 self.n2bc[:], op0=ALU.mult, op1=ALU.mult, sem=s_dve)
            p.w("tensor", s_dve)
            for c in range(8):
                b = kT % 2
                if kT >= 2:
                    p.w("tensor", s_dve, evacT[kT - 2])
                p.op("tensor", "transpose", ptrB[b][:],
                     xfb[:, c * 128:(c + 1) * 128], self.id16[:], sem=s_pe)
                p.w("vector", s_pe)
                p.op("vector", "tensor_copy",
                     xfT[:, c * NTO + tt * 128: c * NTO + tt * 128 + 128],
                     ptrB[b][:], sem=s_dve)
                evacT[kT] = s_dve.n
                kT += 1
        self.dump("d_xfT", xfT[:, 0:2048], [128, 2048], BF16)
        self.dump("d_x1", x1b[:], [128, 1024], F32)

        gslab = p.sb("gslab", [128, 8 * 4], BF16, stack=stA)
        lg = p.sb("p3_lg", [4, 512], F32, stack=stA)
        g4 = p.sb("p3_g4", [128, 4], F32, stack=stA)
        e4 = p.sb("p3_e4", [128, 4], F32, stack=stA)
        cbuf = p.sb("p3_cb", [128, 16], F32, stack=stA)
        ebuf = p.sb("p3_eb", [128, 16], F32, stack=stA)
        rk = p.sb("p3_rk", [128, 4], F32, stack=stA)
        rk2 = p.sb("p3_rk2", [128, 4], F32, stack=stA)
        w4 = p.sb("p3_w4", [128, 4], F32, stack=stA)
        grow_ = p.sb("p3_gr", [128, 6], F32, stack=stA)
        p.dma("gpsimd", gslab[:],
              self.wsl(self.gwT, 4, 0, 4, 8), s_gld)
        p.w("tensor", s_gld)
        GR_MAX, GR_NM, GR_S, GR_SI, GR_W, GR_WI = range(6)
        for s in range(4):
            p.w("tensor", s_dve)
            for ic in range(8):
                p.op("tensor", "matmul", ptrA[0][0:4, :],
                     gslab[:, ic * 4:(ic + 1) * 4],
                     xfT[:, ic * NTO + s * 512: ic * NTO + s * 512 + 512],
                     start=(ic == 0), stop=(ic == 7),
                     sem=s_pe if ic == 7 else None)
            p.w("scalar", s_pe)
            p.op("scalar", "activation", lg[:], ptrA[0][0:4, :], AF.Copy,
                 sem=s_act)
            for u in range(4):
                tt = s * 4 + u
                p.w("tensor", s_act)
                p.w("tensor", s_dve)
                p.op("tensor", "transpose", ptrA[1][:, 0:4],
                     lg[0:4, u * 128:(u + 1) * 128], self.id32[0:4, 0:4],
                     sem=s_pe)
                p.w("vector", s_pe)
                p.op("vector", "tensor_copy", g4[:], ptrA[1][:, 0:4])
                p.drain("vector")
                p.op("vector", "reduce_max", grow_[:, GR_MAX:GR_MAX + 1], g4[:],
                     AX.X)
                p.drain("vector")
                p.op("vector", "tensor_scalar_mul", grow_[:, GR_NM:GR_NM + 1],
                     grow_[:, GR_MAX:GR_MAX + 1], -1.0)
                p.drain("vector", sem=s_dve)
                p.w("scalar", s_dve)
                p.op("scalar", "activation", e4[:], g4[:], AF.Exp,
                     bias=grow_[:, GR_NM:GR_NM + 1], sem=s_act)
                p.w("vector", s_act)
                p.op("vector", "reduce_sum", grow_[:, GR_S:GR_S + 1], e4[:], AX.X)
                p.drain("vector", sem=s_dve)
                p.w("scalar", s_dve)
                p.op("scalar", "activation", grow_[:, GR_SI:GR_SI + 1],
                     grow_[:, GR_S:GR_S + 1], AF.Ln)
                p.op("scalar", "activation", grow_[:, GR_NM:GR_NM + 1],
                     grow_[:, GR_NM:GR_NM + 1], AF.Copy)
                p.op("scalar", "activation", grow_[:, GR_SI:GR_SI + 1],
                     grow_[:, GR_SI:GR_SI + 1], AF.Exp, scale=-1.0, sem=s_act)
                p.w("vector", s_act)
                p.op("vector", "tensor_scalar_mul", g4[:], e4[:],
                     grow_[:, GR_SI:GR_SI + 1])
                p.drain("vector")
                pa = g4[:].unsqueeze(1).broadcast_to([128, 4, 4])
                pb = g4[:].unsqueeze(2).broadcast_to([128, 4, 4])
                c3 = cbuf[:].rearrange("p (a b) -> p a b", b=4)
                e3_ = ebuf[:].rearrange("p (a b) -> p a b", b=4)
                p.op("vector", "tensor_tensor", c3, pa, pb, ALU.is_gt)
                p.op("vector", "tensor_tensor", e3_, pa, pb, ALU.is_equal)
                p.drain("vector")
                p.op("vector", "tensor_mul", ebuf[:], ebuf[:], self.ltm_s[:])
                p.drain("vector")
                for e in range(4):
                    p.op("vector", "reduce_sum", rk[:, e:e + 1],
                         cbuf[:, 4 * e:4 * e + 4], AX.X)
                    p.op("vector", "reduce_sum", rk2[:, e:e + 1],
                         ebuf[:, 4 * e:4 * e + 4], AX.X)
                p.drain("vector")
                p.op("vector", "tensor_add", rk[:], rk[:], rk2[:])
                p.drain("vector")
                p.op("vector", "tensor_single_scalar", rk[:], rk[:], 1.5,
                     ALU.is_le)
                p.drain("vector")
                p.op("vector", "tensor_mul", w4[:], g4[:], rk[:])
                p.drain("vector")
                p.op("vector", "reduce_sum", grow_[:, GR_W:GR_W + 1], w4[:], AX.X)
                p.drain("vector")
                p.op("vector", "tensor_scalar_add", grow_[:, GR_W:GR_W + 1],
                     grow_[:, GR_W:GR_W + 1], self.cc(C_1E8))
                p.drain("vector", sem=s_dve)
                p.w("scalar", s_dve)
                p.op("scalar", "activation", grow_[:, GR_WI:GR_WI + 1],
                     grow_[:, GR_W:GR_W + 1], AF.Ln)
                p.op("scalar", "activation", grow_[:, GR_NM:GR_NM + 1],
                     grow_[:, GR_NM:GR_NM + 1], AF.Copy)
                p.op("scalar", "activation", grow_[:, GR_WI:GR_WI + 1],
                     grow_[:, GR_WI:GR_WI + 1], AF.Exp, scale=-1.0, sem=s_act)
                p.w("vector", s_act)
                p.op("vector", "tensor_scalar_mul", w4[:], w4[:],
                     grow_[:, GR_WI:GR_WI + 1])
                p.drain("vector", sem=s_dve)
                p.w("tensor", s_dve)
                p.op("tensor", "transpose", ptrA[1][0:4, 0:128], w4[:],
                     self.id32[:], sem=s_pe)
                p.w("vector", s_pe)
                p.op("vector", "tensor_copy", wtsT[0:4, tt * 128:(tt + 1) * 128],
                     ptrA[1][0:4, 0:128], sem=s_dve)
        self.dump("d_wtsT", wtsT[:], [4, NTO], F32)
        p.barrier("p3_gate_done")
        stA.close()

        stB = ExitStack()
        Ht = p.sb("Ht", [128, 16 * 1024], BF16, stack=stB)
        Ht2 = p.sb("Ht2", [128, 16 * 1024], BF16, stack=stB)
        upsl = [p.sb(f"upsl{i}", [128, 16 * 256], BF16, stack=stB) for i in range(2)]
        dnsl = [p.sb(f"dnsl{i}", [128, 16 * 128], BF16, stack=stB) for i in range(2)]
        wbc = p.sb("wbc", [128, NTO], F32, stack=stB)
        tmpd = p.sb("p3_tmpd", [128, 512], F32, stack=stB)
        sel4 = p.sb("sel4s", [4, 512], F32, stack=stB)
        pH = [p.psum(f"p3_pH{i}", [128, 512], F32, stack=stB) for i in range(2)]
        p.dma("sync", sel4[:], self.sel4i, s_ld)
        upcnt = [0, 0]
        dncnt = [0, 0]

        def up_layer(tb, srcT, spitch, sbase, Wd, dst, act, nin, nout=HID):
            nic = nin // 128
            p.w("tensor", s_act)
            p.w("tensor", s_dve)
            ev = {}
            kk = 0
            for jb in range(nout // 256):
                p.w("gpsimd", s_pe, upcnt[jb % 2])
                p.dma("gpsimd", upsl[jb % 2][:, 0:nic * 256],
                      self.wsl(Wd, nout, jb * 256, 256, nic), s_gld)
                nw = s_gld.n
                for jc2 in range(2):
                    jc = jb * 2 + jc2
                    for ts in range(2):
                        b = kk % 2
                        if kk >= 2:
                            p.w("tensor", s_act, ev[kk - 2])
                        if jc2 == 0 and ts == 0:
                            p.w("tensor", s_gld, nw)
                        for ic in range(nic):
                            p.op("tensor", "matmul", pH[b][:],
                                 upsl[jb % 2][:, ic * 256 + jc2 * 128:
                                              ic * 256 + jc2 * 128 + 128],
                                 srcT[:, ic * spitch + sbase + ts * 512:
                                      ic * spitch + sbase + ts * 512 + 512],
                                 start=(ic == 0), stop=(ic == nic - 1),
                                 sem=s_pe if ic == nic - 1 else None)
                        if jc2 == 1 and ts == 1:
                            upcnt[jb % 2] = s_pe.n
                        p.w("scalar", s_pe)
                        p.op("scalar", "activation",
                             dst[:, jc * 1024 + ts * 512: jc * 1024 + ts * 512 + 512],
                             pH[b][:], act, sem=s_act)
                        ev[kk] = s_act.n
                        kk += 1

        def down_layer(tb, e, srcT, Wd, nin):
            nic = nin // 128
            ev = {}
            kk = 0
            for jx in range(6):
                p.w("gpsimd", s_pe, dncnt[jx % 2])
                p.dma("gpsimd", dnsl[jx % 2][:, 0:nic * 128],
                      self.wsl(Wd, D, (jx + 2) * 128, 128, nic), s_gld)
                nw = s_gld.n
                for ts in range(2):
                    b = kk % 2
                    if kk >= 2:
                        p.w("tensor", s_dve, ev[kk - 2])
                    else:
                        p.w("tensor", s_act)
                        p.w("tensor", s_dve)
                    if ts == 0:
                        p.w("tensor", s_gld, nw)
                    for ic in range(nic):
                        p.op("tensor", "matmul", pH[b][:],
                             dnsl[jx % 2][:, ic * 128:(ic + 1) * 128],
                             srcT[:, ic * 1024 + ts * 512:
                                  ic * 1024 + ts * 512 + 512],
                             start=(ic == 0), stop=(ic == nic - 1),
                             sem=s_pe if ic == nic - 1 else None)
                    if ts == 1:
                        dncnt[jx % 2] = s_pe.n
                    p.w("vector", s_pe)
                    col = jx * NTO + tb * 1024 + ts * 512
                    wcol = tb * 1024 + ts * 512
                    dstm = moe[:, col:col + 512]
                    if e == 0:
                        p.op("vector", "tensor_mul", dstm, pH[b][:],
                             wbc[:, wcol:wcol + 512], sem=s_dve)
                    else:
                        p.op("vector", "tensor_mul", tmpd[:], pH[b][:],
                             wbc[:, wcol:wcol + 512])
                        p.op("vector", "tensor_add", dstm, dstm, tmpd[:],
                             sem=s_dve)
                    ev[kk] = s_dve.n
                    kk += 1

        for e, layers in enumerate((None,) * 4):
            for ts4 in range(4):
                p.w("tensor", s_act)
                p.w("tensor", s_dve)
                p.w("tensor", s_ld)
                p.op("tensor", "matmul", pH[0][:], sel4[:, e * 128:(e + 1) * 128],
                     wtsT[0:4, ts4 * 512:(ts4 + 1) * 512], start=True, stop=True,
                     sem=s_pe)
                p.w("scalar", s_pe)
                p.op("scalar", "activation", wbc[:, ts4 * 512:(ts4 + 1) * 512],
                     pH[0][:], AF.Copy, sem=s_act)
            for tb in range(2):
                base = tb * 1024
                if e == 0:
                    up_layer(tb, xfT, NTO, base, self.E0U, Ht, AF.Gelu, D)
                    down_layer(tb, 0, Ht, self.E0D, HID)
                elif e == 1:
                    up_layer(tb, xfT, NTO, base, self.E1A, Ht, AF.Silu, D)
                    up_layer(tb, xfT, NTO, base, self.E1B, Ht2, AF.Copy, D)
                    p.w("vector", s_act)
                    for jc in range(16):
                        for ts in range(2):
                            sl = slice(jc * 1024 + ts * 512,
                                       jc * 1024 + ts * 512 + 512)
                            p.op("vector", "tensor_mul", Ht[:, sl], Ht[:, sl],
                                 Ht2[:, sl],
                                 sem=s_dve if (jc == 15 and ts == 1) else None)
                    down_layer(tb, 1, Ht, self.E1D, HID)
                elif e == 2:
                    up_layer(tb, xfT, NTO, base, self.E2A, Ht, AF.Gelu, D)
                    up_layer(tb, Ht, 1024, 0, self.E2B, Ht2, AF.Gelu, HID)
                    up_layer(tb, Ht2, 1024, 0, self.E2C, Ht, AF.Gelu, HID)
                    down_layer(tb, 2, Ht, self.E2D, HID)
                else:
                    up_layer(tb, xfT, NTO, base, self.E3A, Ht, AF.Gelu, D, nout=D)
                    up_layer(tb, Ht, 1024, 0, self.E3B, Ht2, AF.Gelu, D)
                    down_layer(tb, 3, Ht2, self.E3C, HID)
        self.dump("d_moe", moe[:, 0:2048], [128, 2048], F32)
        p.barrier("p3_moe_done")
        stB.close()

        stC = ExitStack()
        tokm = p.sb("tokm", [128, 16 * 768], F32, stack=stC)
        Abuf = p.sb("p3_Ab", [128, 8 * 512], F32, stack=stC)
        Atok = p.sb("p3_At", [128, 4 * 1024], F32, stack=stC)
        xob = p.sb("p3_xo", [128, D], F32, stack=stC)
        x1b2 = p.sb("p3_x1r", [128, D], F32, stack=stC)
        outb = p.sb("p3_out", [128, D], F32, stack=stC)
        evac = {}
        k = 0
        for jx in range(6):
            for tt in range(16):
                b = k % 2
                if k >= 2:
                    p.w("tensor", s_dve, evac[k - 2])
                p.op("tensor", "transpose", ptrA[b][:, 0:128],
                     moe[:, jx * NTO + tt * 128: jx * NTO + tt * 128 + 128],
                     self.id32[:], sem=s_pe)
                p.w("vector", s_pe)
                p.op("vector", "tensor_copy",
                     tokm[:, tt * 768 + jx * 128: tt * 768 + jx * 128 + 128],
                     ptrA[b][:, 0:128], sem=s_dve)
                evac[k] = s_dve.n
                k += 1
        for s in range(4):
            p.w("sync", s_dve)
            p.dma("sync", Abuf[:],
                  self.park_ap(self.Apark, s, 8), s_ld)
            p.w("tensor", s_ld)
            evac = {}
            k = 0
            for u in range(4):
                for c in range(8):
                    b = k % 2
                    if k >= 2:
                        p.w("tensor", s_dve, evac[k - 2])
                    p.op("tensor", "transpose", ptrA[b][:, 0:128],
                         Abuf[:, c * 512 + u * 128: c * 512 + u * 128 + 128],
                         self.id32[:], sem=s_pe)
                    p.w("vector", s_pe)
                    p.op("vector", "tensor_copy",
                         Atok[:, u * 1024 + c * 128: u * 1024 + c * 128 + 128],
                         ptrA[b][:, 0:128], sem=s_dve)
                    evac[k] = s_dve.n
                    k += 1
            for u in range(4):
                tt = s * 4 + u
                p.w("sync", s_dve)
                p.dma("sync", xob[:], self.xown[tt * 128:(tt + 1) * 128, :], s_ld)
                p.dma("sync", x1b2[:], self.x1park[tt * 128:(tt + 1) * 128, :],
                      s_ld)
                p.w("vector", s_ld)
                p.w("vector", s_st)
                p.op("vector", "tensor_add", outb[:], xob[:], x1b2[:])
                p.op("vector", "tensor_add", outb[:], outb[:],
                     Atok[:, u * 1024:(u + 1) * 1024])
                p.op("vector", "tensor_add", outb[:, 256:D], outb[:, 256:D],
                     tokm[:, tt * 768:(tt + 1) * 768], sem=s_dve)
                p.w("sync", s_dve)
                p.dma("sync", self.yout[tt * 128:(tt + 1) * 128, :], outb[:],
                      s_st)
        p.w("sync", self.s_st)
        stC.close()
        st.close()

    def build_all(self, phases=99):
        p = self.p
        import os
        phases = int(os.environ.get("MK_PHASES", phases))
        st01 = ExitStack()
        self.latT = p.sb("latT", [128, 2 * T], BF16, stack=st01)
        self.qsp = p.sb("qsp", [128, 8 * NTO], BF16, stack=st01)
        self.CS = p.sb("CS", [128, 2 * T], BF16, stack=st01)
        self.bvbc = p.sb("bvbc", [128, D], F32, stack=st01)
        self.mk8 = p.sb("mk8", [128, 8], BF16, stack=st01)
        self.mv02 = p.sb("mv02", [1, NH * 64], F32, stack=st01)
        stCSo = ExitStack()
        CSo = p.sb("CSo", [128, 2 * NTO], BF16, stack=stCSo)
        self.phase_consts()
        if phases >= 1:
            self.phase_big2()
        if phases >= 2:
            self.phase_cs(CSo)
            self.phase_bcasts()
        if phases >= 3:
            st0 = ExitStack()
            stXNT = ExitStack()
            self.phase_p0(st0, stXNT)
        if phases >= 4:
            self.phase_latent()
            p.barrier("xnt_close")
            stXNT.close()
        if phases >= 5:
            self.phase_qprod(CSo, st01)
        if phases >= 3:
            p.barrier("p0_close")
            st0.close()
        stCSo.close()
        if phases >= 6:
            self.phase_p1()
        p.barrier("p01_close")
        st01.close()
        if phases >= 7:
            self.phase_p2()
        if phases >= 8:
            self.phase_p3()
        p.w("sync", self.s_st)
        p.build()
        return self.nc



def _prep_shared(inputs):
    f32 = np.float32
    g = lambda k: np.asarray(inputs[k], dtype=f32)
    out = {}
    out["Wq"] = np.ascontiguousarray(g("L0_attn_Wq"))
    out["Wkd"] = np.ascontiguousarray(g("L0_attn_Wkv_down"))
    kvperm = np.concatenate([np.arange(16) * 128 + j for j in [0]])
    kvperm = np.concatenate(
        [np.concatenate([np.arange(h * 128, h * 128 + 64) for h in range(16)]),
         np.concatenate([np.arange(h * 128 + 64, h * 128 + 128) for h in range(16)])])
    out["Wku"] = np.ascontiguousarray(g("L0_attn_Wkv_up")[:, kvperm])
    out["Wo"] = np.ascontiguousarray(g("L0_attn_Wo"))
    out["bq_pc"] = np.ascontiguousarray(g("L0_attn_bq").reshape(8, 128).T)
    out["bkd_pc"] = np.ascontiguousarray(g("L0_attn_bkv_down").reshape(2, 128).T)
    bku = g("L0_attn_bkv_up")[kvperm]
    out["bkuk_pc"] = np.ascontiguousarray(bku[0:1024].reshape(8, 128).T)
    out["bkuv_row"] = np.ascontiguousarray(bku[1024:2048].reshape(1, 1024))
    out["bo_pc"] = np.ascontiguousarray(g("L0_attn_bo").reshape(8, 128).T)
    out["Fwq"] = np.ascontiguousarray(g("L0_ffn_wq").T)
    out["Fwk"] = np.ascontiguousarray(g("L0_ffn_wk").T)
    out["Fwv"] = np.ascontiguousarray(g("L0_ffn_wv").T)
    out["Fwo"] = np.ascontiguousarray(g("L0_ffn_wo").T)
    out["n1w_pc"] = np.ascontiguousarray(g("L0_ffn_n1_w").reshape(8, 128).T)
    out["n2w_row"] = np.ascontiguousarray(g("L0_ffn_n2_w").reshape(1, 1024))
    out["qnw_pc"] = np.ascontiguousarray(np.tile(g("L0_ffn_qn_w"), 2).reshape(128, 1))
    out["knw_pc"] = np.ascontiguousarray(np.tile(g("L0_ffn_kn_w"), 2).reshape(128, 1))
    out["gwT"] = np.ascontiguousarray(g("L0_ffn_gate_w").T)
    out["E0U"] = np.ascontiguousarray(g("L0_ffn_e0_up").T)
    out["E0D"] = np.ascontiguousarray(g("L0_ffn_e0_down").T)
    out["E1A"] = np.ascontiguousarray(g("L0_ffn_e1_w1").T)
    out["E1B"] = np.ascontiguousarray(g("L0_ffn_e1_w2").T)
    out["E1D"] = np.ascontiguousarray(g("L0_ffn_e1_down").T)
    out["E2A"] = np.ascontiguousarray(g("L0_ffn_e2_l1").T)
    out["E2B"] = np.ascontiguousarray(g("L0_ffn_e2_l2").T)
    out["E2C"] = np.ascontiguousarray(g("L0_ffn_e2_l3").T)
    out["E2D"] = np.ascontiguousarray(g("L0_ffn_e2_l4").T)
    out["E3A"] = np.ascontiguousarray(g("L0_ffn_e3_down").T)
    out["E3B"] = np.ascontiguousarray(g("L0_ffn_e3_up").T)
    out["E3C"] = np.ascontiguousarray(g("L0_ffn_e3_out").T)
    t = np.arange(T, dtype=np.float64)
    out["thi"] = (t // 64).astype(f32).reshape(1, T)
    out["tlo"] = (t % 64).astype(f32).reshape(1, T)
    out["i4r"] = (np.arange(128) % 32).astype(f32).reshape(1, 128)
    out["kcol"] = (np.arange(32)[None, :] * 128
                   + np.arange(128)[:, None]).astype(f32)
    out["I16b"] = np.eye(128, dtype=BF)
    out["I32f"] = np.eye(128, dtype=f32)
    pi = np.zeros((128, 128), dtype=BF)
    for p_ in range(128):
        q = p_ + 32 if (p_ % 64) < 32 else p_ - 32
        pi[p_, q] = 1.0
    out["PIb"] = pi
    sgn = np.where((np.arange(128) % 64) < 32, -1.0, 1.0).astype(f32)
    out["sgn"] = sgn.reshape(128, 1)
    sel2 = np.zeros((2, 128), dtype=f32)
    sel2[0, 0:64] = 1.0
    sel2[1, 64:128] = 1.0
    out["sel2"] = sel2
    out["hsel"] = np.ascontiguousarray(sel2.T)
    ltm = np.zeros((128, 16), dtype=f32)
    for e in range(4):
        for ff in range(4):
            if ff < e:
                ltm[:, 4 * e + ff] = 1.0
    out["ltm16"] = ltm
    sel4 = np.zeros((4, 512), dtype=f32)
    for e in range(4):
        sel4[e, e * 128:(e + 1) * 128] = 1.0
    out["sel4"] = sel4
    return out


def make_in_maps(inputs, cores=8):
    shared = _prep_shared(inputs)
    x = np.asarray(inputs["L0_attn_x"], dtype=np.float32)
    in_maps = []
    for c in range(cores):
        b, half = c // 2, c % 2
        qs = [1024 * s + 512 * half for s in range(NS)]
        own = np.concatenate([np.arange(q, q + QB) for q in qs])
        m = dict(shared)
        m["xfull"] = np.ascontiguousarray(x[b])
        m["xown"] = np.ascontiguousarray(x[b][own])
        tposo = own.astype(np.float64)
        m["thi_o"] = (tposo // 64).astype(np.float32).reshape(1, NTO)
        m["tlo_o"] = (tposo % 64).astype(np.float32).reshape(1, NTO)
        m["qpos"] = tposo.astype(np.float32).reshape(1, NTO)
        m["hsh"] = np.full((128, 1), 512.0 * half, dtype=np.float32)
        in_maps.append(m)
    return in_maps


def kernel(**inputs):
    x = np.asarray(inputs["L0_attn_x"], dtype=np.float32)
    B = x.shape[0]
    mk = MK()
    nc = mk.build_all()
    in_maps = make_in_maps(inputs)
    res = run_bass_kernel_spmd(nc, in_maps, core_ids=list(range(8)))
    out = np.empty((B, T, D), dtype=np.float32)
    for c in range(8):
        b, half = c // 2, c % 2
        y = np.asarray(res.results[c]["yout"])
        for s in range(NS):
            q = 1024 * s + 512 * half
            out[b, q:q + QB] = y[s * QB:(s + 1) * QB]
    return out


# version 17 (final)
# speedup vs baseline: 1.0484x; 1.0005x over previous
import math
import numpy as np
import ml_dtypes
import concourse.bass as bass
import concourse.mybir as mybir
import bass_rust
from concourse.bass_utils import run_bass_kernel_spmd
from contextlib import ExitStack

F32 = mybir.dt.float32
BF16 = mybir.dt.bfloat16
AF = mybir.ActivationFunctionType
ALU = mybir.AluOpType
AX = mybir.AxisListType
BF = ml_dtypes.bfloat16

T = 4096
D = 1024
NH = 16
HD = 64
DL = 256
HID = 2048
NS = 4
QB = 512
NTO = NS * QB
CSTAB = -30.0
A_OFF = 3968
W2 = 8192
EPSR = 1.1920929e-07
ENGS = ("sync", "tensor", "vector", "scalar", "gpsimd")

_CW1 = 6.28125
_CW23 = float(np.float32(2.0 * math.pi - 6.28125))


class Sem:
    __slots__ = ("h", "n", "name")

    def __init__(self, h, name=""):
        self.h = h
        self.n = 0
        self.name = name


class P:
    def __init__(self, nc):
        self.nc = nc
        self.q = {e: [] for e in ENGS}
        self.es = ExitStack()

    def sb(self, name, shape, dt=F32, stack=None, side=None):
        kw = {"side": side} if side else {}
        return (stack or self.es).enter_context(
            self.nc.sbuf_tensor(name, shape, dt, **kw))

    def psum(self, name, shape, dt=F32, stack=None):
        return (stack or self.es).enter_context(self.nc.psum_tensor(name, shape, dt))

    def sem(self, name):
        return Sem(self.es.enter_context(self.nc.semaphore(name)), name)

    def op(self, eng, meth, *args, sem=None, d=1, **kw):
        if sem is not None:
            sem.n += d
            h, dd = sem.h, d

            def f(e, meth=meth, args=args, kw=kw, h=h, dd=dd):
                getattr(e, meth)(*args, **kw).then_inc(h, dd)
        else:
            def f(e, meth=meth, args=args, kw=kw):
                getattr(e, meth)(*args, **kw)
        self.q[eng].append(f)

    def w(self, eng, sem, val=None):
        v = sem.n if val is None else val
        h = sem.h
        self.q[eng].append(lambda e, h=h, v=v: e.wait_ge(h, v))

    def dma(self, eng, out, in_, sem=None, **kw):
        self.op(eng, "dma_start", sem=sem, d=16, out=out, in_=in_, **kw)

    def drain(self, eng="vector", sem=None):
        self.op(eng, "drain", sem=sem)

    def barrier(self, name, engs=ENGS):
        s = self.sem(name)
        for eng in engs:
            self.op(eng, "nop", sem=s)
        for eng in engs:
            self.w(eng, s)

    def build(self):
        with self.nc.Block() as block:
            for eng in ENGS:
                def body(e, eng=eng):
                    for fn in self.q[eng]:
                        fn(e)
                getattr(block, eng)(body)


def man_ap(base_ap, offset, pattern):
    return bass_rust.AP(tensor=base_ap.tensor, offset=offset, ap=pattern)


C_P, C_PI, C_PI2, C_NP, C_001, C_N001, C_N01, C_1E9, C_02, C_1E8, C_LN4 = range(11)
B_BQ, B_BKD, B_BKUK, B_BO, B_N1, B_QNW, B_KNW = 0, 8, 10, 18, 26, 34, 35


class MK:
    def __init__(self, debug=()):
        self.nc = bass.Bass("TRN2", target_bir_lowering=False)
        self.p = P(self.nc)
        self.debug = set(debug)
        self.DBG = {}
        self._declare_dram()
        self._declare_global_sbuf()
        p = self.p
        self.s_ld = p.sem("s_ld")
        self.s_gld = p.sem("s_gld")
        self.s_pe = p.sem("s_pe")
        self.s_act = p.sem("s_act")
        self.s_dve = p.sem("s_dve")
        self.s_pool = p.sem("s_pool")
        self.s_st = p.sem("s_st")

    def _declare_dram(self):
        nc = self.nc
        self.DI = {}

        def din(name, shape, dt=F32):
            ap = nc.dram_tensor(name, list(shape), dt, kind="ExternalInput").ap()
            self.DI[name] = ap
            return ap

        self.xfull = din("xfull", [T, D])
        self.xown = din("xown", [NTO, D])
        self.Wq = din("Wq", [D, D])
        self.Wkd = din("Wkd", [D, DL])
        self.Wku = din("Wku", [DL, 2 * D])
        self.Wo = din("Wo", [D, D])
        self.bq_pc = din("bq_pc", [128, 8])
        self.bkd_pc = din("bkd_pc", [128, 2])
        self.bkuk_pc = din("bkuk_pc", [128, 8])
        self.bkuv_row = din("bkuv_row", [1, D])
        self.bo_pc = din("bo_pc", [128, 8])
        self.Fwq = din("Fwq", [D, D])
        self.Fwk = din("Fwk", [D, D])
        self.Fwv = din("Fwv", [D, D])
        self.Fwo = din("Fwo", [D, D])
        self.n1w_pc = din("n1w_pc", [128, 8])
        self.n2w_row = din("n2w_row", [1, D])
        self.qnw_pc = din("qnw_pc", [128, 1])
        self.knw_pc = din("knw_pc", [128, 1])
        self.gwT = din("gwT", [D, 4])
        self.E0U = din("E0U", [D, HID])
        self.E0D = din("E0D", [HID, D])
        self.E1A = din("E1A", [D, HID])
        self.E1B = din("E1B", [D, HID])
        self.E1D = din("E1D", [HID, D])
        self.E2A = din("E2A", [D, HID])
        self.E2B = din("E2B", [HID, HID])
        self.E2C = din("E2C", [HID, HID])
        self.E2D = din("E2D", [HID, D])
        self.E3A = din("E3A", [D, D])
        self.E3B = din("E3B", [D, HID])
        self.E3C = din("E3C", [HID, D])
        self.thi = din("thi", [1, T])
        self.tlo = din("tlo", [1, T])
        self.thi_o = din("thi_o", [1, NTO])
        self.tlo_o = din("tlo_o", [1, NTO])
        self.i4r = din("i4r", [1, 128])
        self.kcol = din("kcol", [128, 32])
        self.qpos = din("qpos", [1, NTO])
        self.hsh = din("hsh", [128, 1])
        self.I16b = din("I16b", [128, 128], BF16)
        self.I32f = din("I32f", [128, 128], F32)
        self.PIb = din("PIb", [128, 128], BF16)
        self.sgn = din("sgn", [128, 1])
        self.sel2 = din("sel2", [2, 128])
        self.hsel = din("hsel", [128, 2])
        self.ltm16 = din("ltm16", [128, 16])
        self.sel4i = din("sel4", [4, 512])

        self.yout = nc.dram_tensor("yout", [NTO, D], F32, kind="ExternalOutput").ap()
        self.big2 = nc.dram_tensor("big2", [128, W2], BF16, kind="Internal").ap()
        self.Apark = nc.dram_tensor("Apark", [NS, 8, 128, QB], F32, kind="Internal").ap()
        self.qcapark = nc.dram_tensor("qcapark", [128, 8 * NTO], BF16, kind="Internal").ap()
        self.xnTpark = nc.dram_tensor("xnTpark", [128, 8 * T], BF16, kind="Internal").ap()
        self.xnopark = nc.dram_tensor("xnopark", [NTO, D], F32, kind="Internal").ap()
        self.x1park = nc.dram_tensor("x1park", [NTO, D], F32, kind="Internal").ap()
        self.aopark = nc.dram_tensor("aopark", [NS, 6, 128, QB], F32, kind="Internal").ap()

    def dbg(self, name, shape, dt=F32):
        if name not in self.debug:
            return None
        if name not in self.DBG:
            self.DBG[name] = self.nc.dram_tensor(
                name, list(shape), dt, kind="ExternalOutput").ap()
        return self.DBG[name]

    def dump(self, name, src_ap, shape, dt=F32):
        ap = self.dbg(name, shape, dt)
        if ap is not None:
            for s in (self.s_act, self.s_dve, self.s_pe, self.s_pool):
                self.p.w("sync", s)
            self.p.dma("sync", ap, src_ap, self.s_st)
            self.p.w("sync", self.s_st)

    def _declare_global_sbuf(self):
        p = self.p
        self.co = p.sb("co", [128, 16], F32)
        self.rstd1 = p.sb("rstd1", [128, 32], F32)
        self.ones1b = p.sb("ones1b", [128, 1], BF16)
        self.onesr_f = p.sb("onesr_f", [1, 128], F32)
        self.ones64_f = p.sb("ones64_f", [1, 64], F32)
        self.onesc_f = p.sb("onesc_f", [128, 1], F32)
        self.sel2b = p.sb("sel2b", [2, 128], F32)
        self.hselb = p.sb("hselb", [128, 2], F32)
        self.kcol_s = p.sb("kcol_s", [128, 32], F32)
        self.id16 = p.sb("id16", [128, 128], BF16)
        self.id32 = p.sb("id32", [128, 128], F32)
        self.pib = p.sb("pib", [128, 128], BF16)
        self.sgn_s = p.sb("sgn_s", [128, 1], F32)
        self.ltm_s = p.sb("ltm_s", [128, 16], F32)
        self.biases = p.sb("biases", [128, 40], F32)
        self.n2bc = p.sb("n2bc", [128, D], F32)
        self.i4row = p.sb("i4row", [1, 128], F32)
        self.ctmp1 = p.sb("c_tmp1", [1, 128], F32)
        self.ACTC_VALS = [1.0e-5, EPSR, 1.0, CSTAB, -1.0, 1.0 / 64.0,
                          1.0 / 1024.0, 0.5]
        self.actc = p.sb("actc", [128, len(self.ACTC_VALS)], F32)
        for i, v in enumerate(self.ACTC_VALS):
            self.nc.const_aps.register(F32, v, self.actc[:, i:i + 1])
        self.spcr = p.sb("spcr", [2, 64], F32)
        self.frow = p.sb("frow", [1, 128], F32)
        self.grow = p.sb("grow", [1, 128], F32)

    def cc(self, i):
        return self.co[:, i:i + 1]

    def cc1(self, i):
        return self.co[0:1, i:i + 1]

    def bcol(self, i):
        return self.biases[:, i:i + 1]

    def wsl(self, Wd, nout_total, j0, nj, nic):
        return man_ap(Wd, j0, [[nout_total, 128], [128 * nout_total, nic],
                               [1, nj]])

    def park_ap(self, base, s, C):
        return man_ap(base, s * C * 128 * QB,
                      [[QB, 128], [128 * QB, C], [1, QB]])

    def phase_consts(self):
        p, s_ld, s_dve = self.p, self.s_ld, self.s_dve
        co = self.co

        def consts(v):
            tp1 = float(np.float32(2 * np.pi))
            tp2 = float(np.float32(np.float64(2 * np.pi) - tp1))
            tp3 = float(np.float32(np.float64(2 * np.pi) - tp1 - tp2))
            v.memset(co[:, C_P:C_P + 1], tp1)
            v.drain()
            v.tensor_scalar_add(co[:, C_P:C_P + 1], co[:, C_P:C_P + 1], tp2)
            v.drain()
            v.tensor_scalar_add(co[:, C_P:C_P + 1], co[:, C_P:C_P + 1], tp3)
            v.drain()
            v.tensor_scalar_mul(co[:, C_PI:C_PI + 1], co[:, C_P:C_P + 1], 0.5)
            v.tensor_scalar_mul(co[:, C_PI2:C_PI2 + 1], co[:, C_P:C_P + 1], 0.25)
            v.tensor_scalar_mul(co[:, C_NP:C_NP + 1], co[:, C_P:C_P + 1], -1.0)
            v.memset(co[:, C_001:C_001 + 1], 100.0)
            v.memset(co[:, C_N01:C_N01 + 1], 10.0)
            v.memset(co[:, C_1E9:C_1E9 + 1], 1.0e9)
            v.memset(co[:, C_02:C_02 + 1], 5.0)
            v.memset(co[:, C_1E8:C_1E8 + 1], 1.0e8)
            v.drain()
            v.reciprocal(co[:, C_001:C_001 + 1], co[:, C_001:C_001 + 1])
            v.drain()
            v.reciprocal(co[:, C_N01:C_N01 + 1], co[:, C_N01:C_N01 + 1])
            v.drain()
            v.reciprocal(co[:, C_1E9:C_1E9 + 1], co[:, C_1E9:C_1E9 + 1])
            v.drain()
            v.reciprocal(co[:, C_02:C_02 + 1], co[:, C_02:C_02 + 1])
            v.drain()
            v.reciprocal(co[:, C_1E8:C_1E8 + 1], co[:, C_1E8:C_1E8 + 1])
            v.drain()
            v.tensor_scalar_mul(co[:, C_N001:C_N001 + 1], co[:, C_001:C_001 + 1], -1.0)
            v.tensor_scalar_mul(co[:, C_N01:C_N01 + 1], co[:, C_N01:C_N01 + 1], -1.0)
            L = np.float64(math.log(10000.0))
            L1 = float(np.float32(L))
            L2 = float(np.float32(L - np.float64(np.float32(L))))
            v.memset(co[:, C_LN4:C_LN4 + 1], L1)
            v.drain()
            v.tensor_scalar_add(co[:, C_LN4:C_LN4 + 1], co[:, C_LN4:C_LN4 + 1], L2)
            for i, val in enumerate(self.ACTC_VALS):
                v.memset(self.actc[:, i:i + 1], val)
            v.memset(self.spcr[:], 1.0)
            v.memset(self.ones1b[:], 1.0)
            v.memset(self.onesr_f[:], 1.0)
            v.memset(self.ones64_f[:], 1.0)
            v.memset(self.onesc_f[:], 1.0)
            v.drain()

        p.q["vector"].append(consts)
        p.op("vector", "nop", sem=s_dve)

        for dst, src in [(self.id16[:], self.I16b), (self.id32[:], self.I32f),
                         (self.pib[:], self.PIb), (self.sgn_s[:], self.sgn),
                         (self.sel2b[:], self.sel2), (self.hselb[:], self.hsel),
                         (self.kcol_s[:], self.kcol),
                         (self.ltm_s[:], self.ltm16), (self.i4row[:], self.i4r)]:
            p.dma("sync", dst, src, self.s_ld)
        b = self.biases
        p.dma("sync", b[:, 0:8], self.bq_pc, self.s_ld)
        p.dma("sync", b[:, 8:10], self.bkd_pc, self.s_ld)
        p.dma("sync", b[:, 10:18], self.bkuk_pc, self.s_ld)
        p.dma("sync", b[:, 18:26], self.bo_pc, self.s_ld)
        p.dma("sync", b[:, 26:34], self.n1w_pc, self.s_ld)
        p.dma("sync", b[:, 34:35], self.qnw_pc, self.s_ld)
        p.dma("sync", b[:, 35:36], self.knw_pc, self.s_ld)
        p.w("vector", self.s_ld)
        p.op("vector", "tensor_scalar_mul", self.bcol(B_QNW), self.bcol(B_QNW), 0.125)
        p.drain("vector", sem=s_dve)

        p.w("vector", s_dve)
        p.op("vector", "tensor_scalar",
             self.frow[:], self.i4row[:], self.cc1(C_LN4), -0.03125,
             op0=ALU.mult, op1=ALU.mult)
        p.drain("vector", sem=s_dve)
        p.w("scalar", s_dve)
        p.op("scalar", "activation", self.frow[:], self.frow[:], AF.Exp,
             sem=self.s_act)
        p.w("vector", self.s_act)
        p.op("vector", "tensor_scalar_mul", self.grow[:], self.frow[:], 64.0)
        p.drain("vector")
        tmp1 = self.ctmp1
        for j in range(3, -1, -1):
            thr = float(np.float32(2.0 * math.pi) * (1 << j))
            c1j = _CW1 * (1 << j)
            c23j = _CW23 * (1 << j)
            p.op("vector", "tensor_single_scalar", tmp1[:], self.grow[:], thr, ALU.is_ge)
            p.drain("vector")
            p.op("vector", "scalar_tensor_tensor", self.grow[:], tmp1[:], -c1j,
                 self.grow[:], op0=ALU.mult, op1=ALU.add)
            p.drain("vector")
            p.op("vector", "scalar_tensor_tensor", self.grow[:], tmp1[:], -c23j,
                 self.grow[:], op0=ALU.mult, op1=ALU.add)
            p.drain("vector")
        p.op("vector", "nop", sem=s_dve)
        self.dump("d_frow", self.frow[:], [1, 128], F32)
        self.dump("d_grow", self.grow[:], [1, 128], F32)
        self.dump("d_co", self.co[:], [128, 16], F32)

    def phase_big2(self):
        p = self.p
        st = ExitStack()
        dgb = p.sb("b2_dg", [128, 2048], F32, stack=st)
        t1b = p.sb("b2_t1", [128, 2048], F32, stack=st)
        t2b = p.sb("b2_t2", [128, 2048], F32, stack=st)
        wb = p.sb("b2_w", [128, 2048], BF16, stack=st)
        hshs = p.sb("b2_hsh", [128, 1], F32, stack=st)
        p.dma("sync", hshs[:], self.hsh, self.s_ld)
        for ch in range(4):
            base = -A_OFF + ch * 2048
            p.w("gpsimd", self.s_act)
            p.w("gpsimd", self.s_dve)
            p.op("gpsimd", "iota", dgb[:], [[1, 2048]], base=base,
                 channel_multiplier=-1, allow_small_or_imprecise_dtypes=True,
                 sem=self.s_pool)
            p.w("vector", self.s_pool)
            p.w("vector", self.s_ld)
            p.op("vector", "tensor_scalar_add", dgb[:], dgb[:], hshs[:], sem=self.s_dve)
            p.w("scalar", self.s_dve)
            p.op("scalar", "activation", t1b[:], dgb[:], AF.Relu)
            p.op("scalar", "activation", t2b[:], t1b[:], AF.Exp,
                 scale=self.cc(C_N001), sem=self.s_act)
            p.op("scalar", "activation", t1b[:], dgb[:], AF.Abs)
            p.op("scalar", "activation", t1b[:], t1b[:], AF.Ln, bias=1.0)
            p.op("scalar", "activation", t1b[:], t1b[:], AF.Exp,
                 scale=self.cc(C_N01), sem=self.s_act)
            p.w("vector", self.s_act)
            p.w("vector", self.s_st)
            p.op("vector", "tensor_scalar_add", t2b[:], t2b[:], self.cc(C_1E9))
            p.op("vector", "tensor_mul", wb[:], t1b[:], t2b[:], sem=self.s_dve)
            p.w("sync", self.s_dve)
            p.dma("sync", self.big2[:, ch * 2048:(ch + 1) * 2048], wb[:], self.s_st)
        self.dump("d_big2d", wb[:], [128, 2048], BF16)
        p.w("sync", self.s_st)
        st.close()

    def _cs_chunk(self, st_bufs, dst_cos, dst_sin, thi_src, tlo_src):
        p = self.p
        rbuf, rc, tmp, thr_, tlr_, psA = st_bufs
        p.w("sync", self.s_pe)
        p.dma("sync", thr_[:], thi_src, self.s_ld)
        p.dma("sync", tlr_[:], tlo_src, self.s_ld)
        p.w("tensor", self.s_ld)
        p.w("tensor", self.s_dve)
        p.op("tensor", "matmul", psA[:], self.grow[:], thr_[:], start=True, stop=False)
        p.op("tensor", "matmul", psA[:], self.frow[:], tlr_[:], start=False, stop=True,
             sem=self.s_pe)
        p.w("vector", self.s_pe)
        p.w("vector", self.s_act)
        p.op("vector", "tensor_copy", rbuf[:], psA[:], sem=self.s_dve)
        for j in range(6, -1, -1):
            thr = float(np.float32(2.0 * math.pi) * (1 << j))
            c1j = _CW1 * (1 << j)
            c23j = _CW23 * (1 << j)
            p.op("vector", "tensor_single_scalar", tmp[:], rbuf[:], thr, ALU.is_ge)
            p.op("vector", "scalar_tensor_tensor", rbuf[:], tmp[:], -c1j,
                 rbuf[:], op0=ALU.mult, op1=ALU.add)
            p.op("vector", "scalar_tensor_tensor", rbuf[:], tmp[:], -c23j,
                 rbuf[:], op0=ALU.mult, op1=ALU.add)
        p.op("vector", "tensor_single_scalar", tmp[:], rbuf[:], self.cc(C_PI), ALU.is_gt)
        p.op("vector", "scalar_tensor_tensor", rbuf[:], tmp[:], self.cc(C_NP),
             rbuf[:], op0=ALU.mult, op1=ALU.add)
        p.op("vector", "tensor_scalar_add", rc[:], rbuf[:], self.cc(C_PI2))
        p.op("vector", "tensor_single_scalar", tmp[:], rc[:], self.cc(C_PI), ALU.is_gt)
        p.op("vector", "scalar_tensor_tensor", rc[:], tmp[:], self.cc(C_NP),
             rc[:], op0=ALU.mult, op1=ALU.add, sem=self.s_dve)
        p.w("scalar", self.s_dve)
        p.op("scalar", "activation", dst_sin, rbuf[:], AF.Sin)
        p.op("scalar", "activation", dst_cos, rc[:], AF.Sin, sem=self.s_act)
        if getattr(self, "_cs_first", True):
            self._cs_first = False
            self.dump("d_rb0", rbuf[:], [128, 512], F32)
            self.dump("d_rc0", rc[:], [128, 512], F32)

    def phase_cs(self, CSo):
        p = self.p
        p.barrier("cs_start")
        st = ExitStack()
        rbuf = p.sb("cs_r", [128, 512], F32, stack=st)
        rc = p.sb("cs_rc", [128, 512], F32, stack=st)
        tmp = p.sb("cs_tmp", [128, 512], F32, stack=st)
        thr_ = p.sb("cs_thi", [1, 512], F32, stack=st)
        tlr_ = p.sb("cs_tlo", [1, 512], F32, stack=st)
        psA = p.psum("cs_ps", [128, 512], F32, stack=st)
        bufs = (rbuf, rc, tmp, thr_, tlr_, psA)
        for ch in range(8):
            c0 = ch * 512
            self._cs_chunk(bufs,
                           self.CS[:, c0:c0 + 512],
                           self.CS[:, T + c0:T + c0 + 512],
                           self.thi[:, c0:c0 + 512],
                           self.tlo[:, c0:c0 + 512])
        for ch in range(4):
            c0 = ch * 512
            self._cs_chunk(bufs,
                           CSo[:, c0:c0 + 512],
                           CSo[:, NTO + c0:NTO + c0 + 512],
                           self.thi_o[:, c0:c0 + 512],
                           self.tlo_o[:, c0:c0 + 512])
        self.dump("d_cs", self.CS[:, 0:4096], [128, 4096], BF16)
        self.dump("d_css", self.CS[:, T:T + 4096], [128, 4096], BF16)
        self.dump("d_rbuf", rbuf[:], [128, 512], F32)
        st.close()

    def phase_p0(self, st0, stXNT):
        p = self.p
        p.barrier("p0_start")
        self.xnT = p.sb("xnT", [128, 8 * T], BF16, stack=stXNT, side="right")
        self.xnTo = p.sb("xnTo", [128, 8 * NTO], BF16, stack=st0)
        st = ExitStack()
        xb = [p.sb(f"p0_x{i}", [128, D], F32, stack=st) for i in range(2)]
        sqt = p.sb("p0_sq", [128, D], F32, stack=st)
        xnf = p.sb("p0_xnf", [128, D], F32, stack=st)
        xnb = p.sb("p0_xnb", [128, D], BF16, stack=st)
        rows = p.sb("p0_rows", [128, 16], F32, stack=st)
        ptr = [p.psum(f"p0_pt{i}", [128, 128], BF16, stack=st) for i in range(2)]
        R_S, R_Q, R_M, R_V, R_NM, R_SD, R_RR, R_T1, R_S2 = range(9)

        apply_cnt = {}
        evac_cnt = {}

        def rcol(i):
            return rows[:, i:i + 1]

        def ln_tile(src_rows, gi, own, idx):
            bi = gi % 2
            if gi - 2 in apply_cnt:
                p.w("sync", self.s_dve, apply_cnt[gi - 2])
            p.dma("sync", xb[bi][:], src_rows, self.s_ld)
            p.w("scalar", self.s_ld)
            p.op("scalar", "activation", sqt[:], xb[bi][:], AF.Copy,
                 accum_out=rcol(R_S))
            p.op("scalar", "activation", sqt[:], xb[bi][:], AF.Square,
                 accum_out=rcol(R_Q), sem=self.s_act)
            p.w("vector", self.s_act)
            p.op("vector", "tensor_single_scalar", rcol(R_M), rcol(R_S),
                 1.0 / 1024.0, ALU.mult)
            p.op("vector", "tensor_single_scalar", rcol(R_V), rcol(R_Q),
                 1.0 / 1024.0, ALU.mult)
            p.drain("vector")
            p.op("vector", "tensor_scalar", rcol(R_NM), rcol(R_M), rcol(R_M),
                 -1.0, op0=ALU.mult, op1=ALU.mult)
            p.drain("vector")
            p.op("vector", "tensor_add", rcol(R_V), rcol(R_V), rcol(R_NM))
            p.drain("vector", sem=self.s_dve)
            p.w("scalar", self.s_dve)
            p.op("scalar", "activation", rcol(R_SD), rcol(R_V), AF.Ln,
                 bias=1.0e-5)
            p.op("scalar", "activation", rcol(R_T1), rcol(R_V), AF.Square)
            p.op("scalar", "activation", rcol(R_RR), rcol(R_SD), AF.Exp,
                 scale=-0.5, sem=self.s_act)
            p.w("vector", self.s_act)
            if not own:
                p.op("vector", "tensor_mul", rcol(R_T1), rcol(R_V), rcol(R_RR))
                p.drain("vector")
                p.op("vector", "tensor_mul", rcol(R_T1), rcol(R_T1), rcol(R_RR))
                p.drain("vector", sem=self.s_dve)
                p.w("scalar", self.s_dve)
                p.op("scalar", "activation", rcol(R_S2), rcol(R_T1), AF.Ln,
                     bias=EPSR)
                p.op("scalar", "activation", rcol(R_NM), rcol(R_T1), AF.Square)
                p.op("scalar", "activation", self.rstd1[:, idx:idx + 1],
                     rcol(R_S2), AF.Exp, scale=-0.5, sem=self.s_act)
                p.w("vector", self.s_act)
            if own:
                p.w("vector", self.s_st)
            p.op("vector", "tensor_scalar", xnf[:], xb[bi][:], rcol(R_M),
                 rcol(R_RR), op0=ALU.subtract, op1=ALU.mult, sem=self.s_dve)
            apply_cnt[gi] = self.s_dve.n
            p.op("vector", "tensor_copy", xnb[:], xnf[:], sem=self.s_dve)
            if own:
                p.w("sync", self.s_dve, apply_cnt[gi])
                p.dma("sync", self.xnopark[idx * 128:(idx + 1) * 128, :], xnf[:],
                      self.s_st)
            dstT = self.xnTo if own else self.xnT
            span = NTO if own else T
            p.w("tensor", self.s_dve)
            for c in range(8):
                k = gi * 8 + c
                if k >= 2:
                    p.w("tensor", self.s_dve, evac_cnt[k - 2])
                p.op("tensor", "transpose", ptr[k % 2][:],
                     xnb[:, c * 128:(c + 1) * 128], self.id16[:], sem=self.s_pe)
                p.w("vector", self.s_pe)
                p.op("vector", "tensor_copy",
                     dstT[:, c * span + idx * 128: c * span + idx * 128 + 128],
                     ptr[k % 2][:], sem=self.s_dve)
                evac_cnt[k] = self.s_dve.n

        gi = 0
        for i in range(32):
            ln_tile(self.xfull[i * 128:(i + 1) * 128, :], gi, False, i)
            gi += 1
        for i in range(16):
            ln_tile(self.xown[i * 128:(i + 1) * 128, :], gi, True, i)
            gi += 1
        self.dump("d_xnT", self.xnT[:, 0:4096], [128, 4096], BF16)
        self.dump("d_xnTo", self.xnTo[:, 0:2048], [128, 2048], BF16)
        self.dump("d_rstd1", self.rstd1[:], [128, 32], F32)
        st.close()

    def phase_bcasts(self):
        p = self.p
        p.barrier("bcast_start")
        st = ExitStack()
        row = p.sb("bc_row", [1, D], F32, stack=st)
        psb = p.psum("bc_ps", [128, 512], F32, stack=st)
        for nm, srow, dst in [("bv", self.bkuv_row, self.bvbc),
                              ("n2", self.n2w_row, self.n2bc)]:
            p.w("sync", self.s_pe)
            p.dma("sync", row[:], srow, self.s_ld)
            for h in range(2):
                p.w("tensor", self.s_ld)
                p.w("tensor", self.s_act)
                p.op("tensor", "matmul", psb[:], self.onesr_f[:],
                     row[:, h * 512:(h + 1) * 512], start=True, stop=True,
                     sem=self.s_pe)
                p.w("scalar", self.s_pe)
                p.op("scalar", "activation", dst[:, h * 512:(h + 1) * 512],
                     psb[:], AF.Copy, sem=self.s_act)
        p.barrier("bc_done")
        st.close()

    def phase_latent(self):
        p = self.p
        p.barrier("lat_start")
        st = ExitStack()
        wkdb = p.sb("wkdb", [128, 8 * DL], BF16, stack=st)
        pp = [p.psum(f"lat_ps{i}", [128, 512], F32, stack=st) for i in range(2)]
        p.dma("gpsimd", wkdb[:],
              self.wsl(self.Wkd, DL, 0, DL, 8), self.s_gld)
        p.w("tensor", self.s_gld)
        p.w("tensor", self.s_dve)
        evac = {}
        k = 0
        for jc in range(2):
            for tb in range(8):
                b = k % 2
                if k >= 2:
                    p.w("tensor", self.s_dve, evac[k - 2])
                for ic in range(8):
                    p.op("tensor", "matmul", pp[b][:],
                         wkdb[:, ic * DL + jc * 128: ic * DL + (jc + 1) * 128],
                         self.xnT[:, ic * T + tb * 512: ic * T + tb * 512 + 512],
                         start=(ic == 0), stop=(ic == 7),
                         sem=self.s_pe if ic == 7 else None)
                p.w("vector", self.s_pe)
                p.op("vector", "tensor_scalar_add",
                     self.latT[:, jc * T + tb * 512: jc * T + tb * 512 + 512],
                     pp[b][:], self.bcol(B_BKD + jc), sem=self.s_dve)
                evac[k] = self.s_dve.n
                k += 1
        self.dump("d_latT", self.latT[:, 0:4096], [128, 4096], BF16)
        self.dump("d_wkdb", wkdb[:], [128, 2048], BF16)
        self.dump("d_xnT2", self.xnT[:, 0:4096], [128, 4096], BF16)
        p.w("sync", self.s_dve)
        p.dma("sync", self.xnTpark, self.xnT[:], self.s_st)
        p.w("sync", self.s_st)
        p.barrier("lat_done")
        st.close()

    def phase_qprod(self, CSo, st01):
        p = self.p
        p.barrier("qprod_start")
        st = ExitStack()
        wqb = p.sb("wqb", [128, 8 * D], BF16, stack=st)
        fchunk = p.sb("fwq_c", [128, D], F32, stack=st)
        fwqb = p.sb("fwqb", [128, 8 * D], BF16, stack=st)
        qtile = [p.sb(f"qtile{i}", [128, 512], BF16, stack=st) for i in range(2)]
        q1f = p.sb("q1f", [128, 512], F32, stack=st)
        qb1 = p.sb("qb1", [128, 512], BF16, stack=st)
        t1f = p.sb("t1f", [128, 512], F32, stack=st)
        t2f = p.sb("t2f", [128, 512], F32, stack=st)
        sqf = p.sb("sqf", [128, 512], F32, stack=st)
        sqfB = p.sb("sqfB", [128, 512], F32, stack=st)
        ssrow = p.sb("ssrow", [2, 512], F32, stack=st)
        ssrB = p.sb("qp_ssrB", [2, 512], F32, stack=st)
        ssl = p.sb("qp_ssl", [2, 512], F32, stack=st)
        sslB = p.sb("qp_sslB", [2, 512], F32, stack=st)
        bsb = p.sb("bsb", [128, 512], F32, stack=st)
        bsbB = p.sb("bsbB", [128, 512], F32, stack=st)
        q1f2 = [q1f, p.sb("q1fB", [128, 512], F32, stack=st)]
        qb12 = [qb1, p.sb("qb1B", [128, 512], BF16, stack=st)]
        pbank = [p.psum(f"qp_pb{i}", [128, 512], F32, stack=st) for i in range(7)]
        pa = pbank[0:2]
        pPq = pbank[3:5]

        p.dma("gpsimd", wqb[:],
              self.wsl(self.Wq, D, 0, D, 8), self.s_gld)
        for ic in range(8):
            p.w("sync", self.s_dve)
            p.dma("sync", fchunk[:], self.Fwq[ic * 128:(ic + 1) * 128, :], self.s_ld)
            p.w("vector", self.s_ld)
            p.op("vector", "tensor_scalar_mul", fwqb[:, ic * D:(ic + 1) * D],
                 fchunk[:], self.bcol(B_N1 + ic), sem=self.s_dve)
        p.w("tensor", self.s_gld)
        p.w("tensor", self.s_dve)

        evacq, dve_cp, pe_mm, pe_pi = {}, {}, {}, {}
        for k in range(33):
            if k >= 1:
                j = k - 1
                p.w("tensor", self.s_dve, dve_cp[j])
                p.op("tensor", "matmul", pPq[j % 2][:], self.pib[:], qb12[j % 2][:],
                     start=True, stop=True, sem=self.s_pe)
                pe_pi[j] = self.s_pe.n
            if k < 32:
                jc, tb = divmod(k, 4)
                if k >= 2:
                    p.w("tensor", self.s_dve, evacq[k - 2])
                for ic in range(8):
                    p.op("tensor", "matmul", pa[k % 2][:],
                         wqb[:, ic * D + jc * 128: ic * D + (jc + 1) * 128],
                         self.xnTo[:, ic * NTO + tb * 512: ic * NTO + tb * 512 + 512],
                         start=(ic == 0), stop=(ic == 7),
                         sem=self.s_pe if ic == 7 else None)
                pe_mm[k] = self.s_pe.n
            if k >= 1:
                j = k - 1
                jcj, tbj = divmod(j, 4)
                p.op("vector", "tensor_mul", t1f[:], q1f2[j % 2][:],
                     CSo[:, tbj * 512: tbj * 512 + 512])
            if k < 32:
                jc, tb = divmod(k, 4)
                p.w("vector", self.s_pe, pe_mm[k])
                p.op("vector", "tensor_scalar", q1f2[k % 2][:], pa[k % 2][:],
                     self.bcol(B_BQ + jc), 0.125, op0=ALU.add, op1=ALU.mult,
                     sem=self.s_dve)
                evacq[k] = self.s_dve.n
                p.op("vector", "tensor_copy", qb12[k % 2][:], q1f2[k % 2][:],
                     sem=self.s_dve)
                dve_cp[k] = self.s_dve.n
            if k >= 1:
                j = k - 1
                jcj, tbj = divmod(j, 4)
                p.w("vector", self.s_pe, pe_pi[j])
                p.op("vector", "scalar_tensor_tensor", t2f[:], pPq[j % 2][:],
                     self.sgn_s[:], CSo[:, NTO + tbj * 512: NTO + tbj * 512 + 512],
                     op0=ALU.mult, op1=ALU.mult)
                p.op("vector", "tensor_add",
                     self.qsp[:, jcj * NTO + tbj * 512: jcj * NTO + tbj * 512 + 512],
                     t1f[:], t2f[:], sem=self.s_dve)

        ring = [pbank[0], pbank[1], pbank[2]]
        sqf2 = [sqf, sqfB]
        bsb2 = [bsb, bsbB]
        ssl2 = [ssl, sslB]
        ssrow2 = [ssrow, ssrB]
        evac, stcnt = {}, {}
        pe_mm, pe_hs, pe_bc = {}, {}, {}
        act_sq, act_ex, act_cp = {}, {}, {}
        p.w("tensor", self.s_dve)
        for k in range(32 + 2):
            if k < 32:
                jc, tb = divmod(k, 4)
                if k >= 3:
                    p.w("tensor", self.s_dve, evac[k - 3])
                for ic in range(8):
                    p.op("tensor", "matmul", ring[k % 3][:],
                         fwqb[:, ic * D + jc * 128: ic * D + (jc + 1) * 128],
                         self.xnTo[:, ic * NTO + tb * 512: ic * NTO + tb * 512 + 512],
                         start=(ic == 0), stop=(ic == 7),
                         sem=self.s_pe if ic == 7 else None)
                pe_mm[k] = self.s_pe.n
            if 1 <= k <= 32:
                j = k - 1
                p.w("tensor", self.s_act, act_sq[j])
                p.op("tensor", "matmul", pbank[3 + j % 2][0:2, :], self.hselb[:],
                     sqf2[j % 2][:], start=True, stop=True, sem=self.s_pe)
                pe_hs[j] = self.s_pe.n
            if k >= 2:
                i2 = k - 2
                p.w("tensor", self.s_act, act_ex[i2])
                p.op("tensor", "matmul", pbank[5 + i2 % 2][:],
                     self.sel2b[:], ssrow2[i2 % 2][:],
                     start=True, stop=True, sem=self.s_pe)
                pe_bc[i2] = self.s_pe.n
            if k < 32:
                p.w("scalar", self.s_pe, pe_mm[k])
                p.op("scalar", "activation", sqf2[k % 2][:],
                     ring[k % 3][:], AF.Square, sem=self.s_act)
                act_sq[k] = self.s_act.n
            if 1 <= k <= 32:
                j = k - 1
                p.w("scalar", self.s_pe, pe_hs[j])
                p.op("scalar", "activation", ssl2[j % 2][:],
                     pbank[3 + j % 2][0:2, :], AF.Ln,
                     scale=1.0 / 64.0, bias=EPSR)
                p.op("scalar", "activation", self.spcr[:], self.spcr[:],
                     AF.Square)
                p.op("scalar", "activation", ssrow2[j % 2][:],
                     ssl2[j % 2][:], AF.Exp, scale=-0.5, sem=self.s_act)
                act_ex[j] = self.s_act.n
            if k >= 2:
                i2 = k - 2
                jc2, tb2 = divmod(i2, 4)
                p.w("scalar", self.s_pe, pe_bc[i2])
                if i2 >= 2:
                    p.w("scalar", self.s_dve, evac[i2 - 2])
                p.op("scalar", "activation", bsb2[i2 % 2][:],
                     pbank[5 + i2 % 2][:], AF.Copy, sem=self.s_act)
                act_cp[i2] = self.s_act.n
                p.w("vector", self.s_act, act_cp[i2])
                if i2 >= 2:
                    p.w("vector", self.s_st, stcnt[i2 - 2])
                p.op("vector", "scalar_tensor_tensor", qtile[i2 % 2][:],
                     ring[i2 % 3][:], self.bcol(B_QNW), bsb2[i2 % 2][:],
                     op0=ALU.mult, op1=ALU.mult, sem=self.s_dve)
                evac[i2] = self.s_dve.n
                p.w("sync", self.s_dve, evac[i2])
                p.dma("sync",
                      self.qcapark[:, jc2 * NTO + tb2 * 512: jc2 * NTO + tb2 * 512 + 512],
                      qtile[i2 % 2][:], self.s_st)
                stcnt[i2] = self.s_st.n
        self.dump("d_qsp", self.qsp[:, 0:2048], [128, 2048], BF16)
        p.w("sync", self.s_st)
        p.barrier("qprod_done")
        st.close()

    def phase_p1(self):
        import os
        NG = int(os.environ.get("MK_P1_G", "2"))
        NSL = int(os.environ.get("MK_P1_S", "4"))
        SKIP = int(os.environ.get("MK_P1_SKIP", "0"))
        p = self.p
        p.barrier("p1_start")
        st = ExitStack()
        KTg = p.sb("KTg", [128, 4 * T], BF16, stack=st)
        Vg = p.sb("Vg", [128, 32 * 520], BF16, stack=st)
        wq_ = [p.sb(f"wq_{i}", [128, 2 * 512], BF16, stack=st) for i in range(2)]
        Astage = p.sb("Astage", [128, 8 * 512], F32, stack=st)
        OT = p.sb("OT", [128, 4 * 512], BF16, stack=st)
        pT = [p.sb(f"pT{i}", [128, 1024], BF16, stack=st) for i in range(3)]
        ktmp = pT[0][:, 0:512]
        scr = p.sb("p1_scr", [128, 1024], F32, stack=st)
        t1f = scr[:, 0:512]
        t2f = scr[:, 512:1024]
        mkf4 = p.sb("p1_mkf", [128, 4], F32, stack=st)
        linv2 = p.sb("p1_linv", [1, 1024], F32, stack=st)
        sig2 = p.sb("p1_sig", [1, 1024], F32, stack=st)
        lrow2 = p.sb("p1_lrow", [1, 1024], F32, stack=st)
        sig2s = lrow2
        ouf2 = scr[0:64, :]
        wkuk2 = [p.sb(f"wkuk{i}", [128, 2 * 512], BF16, stack=st) for i in range(2)]
        wkuv2 = [p.sb(f"wkuv{i}", [128, 2 * 512], BF16, stack=st) for i in range(2)]
        wob = p.sb("wob", [128, 4 * D], BF16, stack=st)
        pS01 = p.psum("pS01", [128, 1024], F32, stack=st)
        pS23 = p.psum("pS23", [128, 1024], F32, stack=st)
        pF = p.psum("pF", [128, 1024], F32, stack=st)
        pSb = [pS01, pS23, pF]
        pO = [p.psum(f"pO{i}", [128, 512], F32, stack=st) for i in range(2)]

        s_ld, s_gld, s_pe, s_act, s_dve, s_st = (self.s_ld, self.s_gld, self.s_pe,
                                                 self.s_act, self.s_dve, self.s_st)
        CSc = self.CS[:, 0:T]
        CSs = self.CS[:, T:2 * T]

        gld_cnt = {}
        for gg in range(NG):
            p.dma("gpsimd", wkuk2[gg][:],
                  self.wsl(self.Wku, 2 * D, gg * 512, 512, 2), s_gld)
            p.dma("gpsimd", wkuv2[gg][:],
                  self.wsl(self.Wku, 2 * D, D + gg * 512, 512, 2), s_gld)
            gld_cnt[gg] = s_gld.n
        for g in range(NG):
            wkuk, wkuv = wkuk2[g], wkuv2[g]
            p.w("gpsimd", s_pe)
            p.dma("gpsimd", wob[:],
                  man_ap(self.Wo, g * 512 * D, [[D, 128], [128 * D, 4], [1, D]]),
                  s_gld)
            wobc = s_gld.n
            p.w("tensor", s_gld, gld_cnt[g])
            ktmp2 = [pT[0][:, 0:512], pT[0][:, 512:1024]]
            pPi2 = [pF[:, 0:512], pF[:, 512:1024]]
            evac, dve_kt, pe_mm, pe_pi = {}, {}, {}, {}
            for k in range(33):
                if k >= 1:
                    j = k - 1
                    p.w("tensor", s_dve, dve_kt[j])
                    p.op("tensor", "matmul", pPi2[j % 2], self.pib[:],
                         ktmp2[j % 2], start=True, stop=True, sem=s_pe)
                    pe_pi[j] = s_pe.n
                if k < 32:
                    ci, tb = divmod(k, 8)
                    if k >= 2:
                        p.w("tensor", s_dve, evac[k - 2])
                    for lc in range(2):
                        p.op("tensor", "matmul", pO[k % 2][:],
                             wkuk[:, lc * 512 + ci * 128: lc * 512 + (ci + 1) * 128],
                             self.latT[:, lc * T + tb * 512: lc * T + tb * 512 + 512],
                             start=(lc == 0), stop=(lc == 1),
                             sem=s_pe if lc == 1 else None)
                    pe_mm[k] = s_pe.n
                if k >= 1:
                    j = k - 1
                    cj, tbj = divmod(j, 8)
                    bcolj = self.bcol(B_BKUK + g * 4 + cj)
                    p.op("vector", "scalar_tensor_tensor", t1f, pO[j % 2][:],
                         bcolj, CSc[:, tbj * 512: tbj * 512 + 512],
                         op0=ALU.add, op1=ALU.mult)
                if k < 32:
                    ci, tb = divmod(k, 8)
                    p.w("vector", s_pe, pe_mm[k])
                    p.op("vector", "tensor_scalar_add", ktmp2[k % 2], pO[k % 2][:],
                         self.bcol(B_BKUK + g * 4 + ci), sem=s_dve)
                    dve_kt[k] = s_dve.n
                if k >= 1:
                    j = k - 1
                    cj, tbj = divmod(j, 8)
                    p.w("vector", s_pe, pe_pi[j])
                    p.op("vector", "scalar_tensor_tensor", t2f, pPi2[j % 2],
                         self.sgn_s[:], CSs[:, tbj * 512: tbj * 512 + 512],
                         op0=ALU.mult, op1=ALU.mult)
                    p.op("vector", "tensor_add",
                         KTg[:, cj * T + tbj * 512: cj * T + tbj * 512 + 512],
                         t1f, t2f, sem=s_dve)
                    evac[j] = s_dve.n
            for ci in range(4):
                p.op("vector", "reduce_sum", mkf4[:, ci:ci + 1],
                     KTg[:, ci * T:(ci + 1) * T], AX.X)
            p.drain("vector")
            p.op("vector", "tensor_single_scalar", self.mk8[:, 0:4],
                 mkf4[:], 8.0 / 4096.0, ALU.mult)
            p.drain("vector", sem=s_dve)
            p.op("vector", "memset",
                 man_ap(Vg[:, :], 64, [[32 * 520, 128], [520, 32], [65, 8]]), 1.0)
            p.drain("vector", sem=s_dve)
            p.w("tensor", s_dve)
            evac, pe_mm = {}, {}
            for tt in range(33):
                if tt < 32:
                    if tt >= 2:
                        p.w("tensor", s_dve, evac[tt - 2])
                    for lc in range(2):
                        p.op("tensor", "matmul", pPi2[tt % 2],
                             self.latT[:, lc * T + tt * 128: lc * T + (tt + 1) * 128],
                             wkuv[:, lc * 512:(lc + 1) * 512],
                             start=(lc == 0), stop=(lc == 1),
                             sem=s_pe if lc == 1 else None)
                    pe_mm[tt] = s_pe.n
                if tt >= 1:
                    j = tt - 1
                    p.w("vector", s_pe, pe_mm[j])
                    p.op("vector", "tensor_add",
                         man_ap(Vg[:, :], j * 520, [[32 * 520, 128], [65, 8], [1, 64]]),
                         man_ap(pF[:, :], (j % 2) * 512, [[1024, 128], [64, 8], [1, 64]]),
                         man_ap(self.bvbc[:, :], g * 512, [[D, 128], [64, 8], [1, 64]]),
                         sem=s_dve)
                    evac[j] = s_dve.n
            for hl in range(8):
                p.w("tensor", s_dve)
                p.w("tensor", s_act)
                for tt in range(32):
                    p.op("tensor", "matmul", pS01[0:1, 0:65], self.ones1b[:],
                         Vg[:, tt * 520 + hl * 65: tt * 520 + hl * 65 + 65],
                         start=(tt == 0), stop=(tt == 31),
                         sem=s_pe if tt == 31 else None)
                p.w("scalar", s_pe)
                p.op("scalar", "activation",
                     self.mv02[0:1, (g * 8 + hl) * 64:(g * 8 + hl) * 64 + 64],
                     pS01[0:1, 0:64], AF.Copy, sem=s_act)
            p.w("vector", s_act)
            p.op("vector", "tensor_scalar",
                 self.mv02[0:1, g * 512:(g + 1) * 512],
                 self.mv02[0:1, g * 512:(g + 1) * 512],
                 self.co[0:1, C_02:C_02 + 1], 1.0 / 4096.0,
                 op0=ALU.mult, op1=ALU.mult)
            p.drain("vector", sem=s_dve)

            for s in range(NSL):
                segq = list(range(16)) * 4
                ldcnt = {}
                segwm = {}

                def load_seg(seg, s=s, segq=segq, ldcnt=ldcnt, segwm=segwm):
                    qtr = segq[seg]
                    base = A_OFF + 1024 * s - 128 * (2 * qtr + 1)
                    if seg >= 2:
                        p.w("sync", s_dve, segwm[seg - 2])
                    else:
                        p.w("sync", s_dve)
                    p.dma("sync", wq_[seg % 2][:],
                          man_ap(self.big2, base, [[W2, 128], [128, 2], [1, 512]]),
                          s_ld)
                    ldcnt[seg] = s_ld.n

                load_seg(0)
                ou_act = 0
                fin_dve = 0
                prev_expd = None
                for pq in range(4):
                    if SKIP & 1:
                        break
                    ci = pq
                    cq = g * 4 + ci
                    h0 = pq * 2
                    expd = {}
                    wmulcnt = {}
                    vcnt = {}

                    def emit_v(kt, pq=pq, h0=h0, vcnt=vcnt, wmulcnt=wmulcnt):
                        p.w("tensor", s_dve, wmulcnt[kt])
                        if kt == 0:
                            p.w("tensor", s_dve, fin_dve)
                            p.w("tensor", s_act, ou_act)
                        b3 = kt % 3
                        p.op("tensor", "matmul", pO[0][0:65, :],
                             Vg[:, kt * 520 + h0 * 65: kt * 520 + h0 * 65 + 65],
                             pT[b3][:, 0:512],
                             start=(kt == 0), stop=(kt == 31))
                        p.op("tensor", "matmul", pO[1][0:65, :],
                             Vg[:, kt * 520 + (h0 + 1) * 65: kt * 520 + (h0 + 1) * 65 + 65],
                             pT[b3][:, 512:1024],
                             start=(kt == 0), stop=(kt == 31), sem=s_pe)
                        vcnt[kt] = s_pe.n

                    for kt in range(32):
                        seg = pq * 16 + kt // 2
                        if kt % 2 == 0 and seg + 1 < 64:
                            load_seg(seg + 1)
                        r = 1 - (kt % 2)
                        wtile = wq_[seg % 2][:, r * 512:(r + 1) * 512]
                        pS = pSb[kt % 3]
                        if kt >= 3:
                            p.w("tensor", s_act, expd[kt - 3])
                        elif pq == 0:
                            p.w("tensor", s_act)
                            p.w("tensor", s_dve)
                        else:
                            p.w("tensor", s_act, ou_act)
                        p.op("tensor", "matmul", pS[:, 0:512],
                             KTg[0:64, ci * T + kt * 128: ci * T + kt * 128 + 128],
                             self.qsp[0:64, cq * NTO + s * 512: cq * NTO + s * 512 + 512],
                             start=True, stop=True, tile_position=(0, 0))
                        p.op("tensor", "matmul", pS[:, 512:1024],
                             KTg[64:128, ci * T + kt * 128: ci * T + kt * 128 + 128],
                             self.qsp[64:128, cq * NTO + s * 512: cq * NTO + s * 512 + 512],
                             start=True, stop=True, tile_position=(64, 0),
                             sem=s_pe)
                        p.w("scalar", s_pe)
                        if kt >= 3:
                            p.w("scalar", s_pe, vcnt[kt - 3])
                        p.op("scalar", "activation", pT[kt % 3][:], pS[:],
                             AF.Exp, bias=CSTAB, sem=s_act)
                        expd[kt] = s_act.n
                        p.w("vector", s_act)
                        p.w("vector", s_ld, ldcnt[seg])
                        p.op("vector", "tensor_mul", pT[kt % 3][:, 0:512],
                             pT[kt % 3][:, 0:512], wtile)
                        p.op("vector", "tensor_mul", pT[kt % 3][:, 512:1024],
                             pT[kt % 3][:, 512:1024], wtile, sem=s_dve)
                        wmulcnt[kt] = s_dve.n
                        if kt % 2 == 1:
                            segwm[seg] = s_dve.n
                        if kt >= 2:
                            emit_v(kt - 2)
                    emit_v(30)
                    emit_v(31)
                    prev_expd = expd
                    if SKIP & 2:
                        continue
                    p.w("scalar", s_pe, vcnt[31])
                    p.w("scalar", s_dve, fin_dve)
                    p.op("scalar", "activation", lrow2[0:1, 0:512],
                         pO[0][64:65, :], AF.Copy)
                    p.op("scalar", "activation", lrow2[0:1, 512:1024],
                         pO[1][64:65, :], AF.Copy)
                    p.op("scalar", "activation", ouf2[:, 0:512], pO[0][0:64, :],
                         AF.Copy)
                    p.op("scalar", "activation", lrow2[0:1, 0:1024],
                         lrow2[0:1, 0:1024], AF.Ln)
                    p.op("scalar", "activation", ouf2[:, 512:1024],
                         pO[1][0:64, :], AF.Copy)
                    p.op("scalar", "activation", linv2[0:1, 0:1024],
                         lrow2[0:1, 0:1024], AF.Exp, scale=-1.0, sem=s_act)
                    p.w("tensor", s_act)
                    p.op("tensor", "matmul", pO[0][0:1, :],
                         self.mk8[0:64, ci:ci + 1],
                         self.qsp[0:64, cq * NTO + s * 512: cq * NTO + s * 512 + 512],
                         start=True, stop=True, tile_position=(0, 0))
                    p.op("tensor", "matmul", pO[1][0:1, :],
                         self.mk8[64:128, ci:ci + 1],
                         self.qsp[64:128, cq * NTO + s * 512: cq * NTO + s * 512 + 512],
                         start=True, stop=True, tile_position=(64, 0), sem=s_pe)
                    p.w("scalar", s_pe)
                    p.op("scalar", "activation", sig2[0:1, 0:512],
                         pO[0][0:1, :], AF.Exp, scale=-1.0)
                    p.op("scalar", "activation", sig2[0:1, 512:1024],
                         pO[1][0:1, :], AF.Exp, scale=-1.0)
                    p.op("scalar", "activation", sig2s[0:1, 0:512],
                         sig2[0:1, 0:512], AF.Ln, bias=1.0)
                    p.op("scalar", "activation", sig2s[0:1, 512:1024],
                         sig2[0:1, 512:1024], AF.Ln, bias=1.0)
                    p.op("scalar", "activation", sig2[0:1, 0:512],
                         sig2s[0:1, 0:512], AF.Exp, scale=-1.0)
                    p.op("scalar", "activation", sig2[0:1, 512:1024],
                         sig2s[0:1, 512:1024], AF.Exp, scale=-1.0, sem=s_act)
                    p.w("tensor", s_act)
                    gh = g * 8 + h0
                    p.op("tensor", "matmul", pO[0][0:64, :], self.ones64_f[:],
                         linv2[0:1, 0:512], start=True, stop=True)
                    p.op("tensor", "matmul", pO[0][64:128, :], self.ones64_f[:],
                         linv2[0:1, 512:1024], start=True, stop=True,
                         tile_position=(0, 64))
                    p.op("tensor", "matmul", pO[1][0:64, :],
                         self.mv02[0:1, gh * 64: gh * 64 + 64],
                         sig2[0:1, 0:512], start=True, stop=True)
                    p.op("tensor", "matmul", pO[1][64:128, :],
                         self.mv02[0:1, (gh + 1) * 64: (gh + 1) * 64 + 64],
                         sig2[0:1, 512:1024], start=True, stop=True,
                         tile_position=(0, 64), sem=s_pe)
                    ou_act = s_act.n
                    p.w("vector", s_pe)
                    p.op("vector", "tensor_mul", ouf2[:, 0:512], ouf2[:, 0:512],
                         pO[0][0:64, :])
                    p.op("vector", "tensor_mul", ouf2[:, 512:1024],
                         ouf2[:, 512:1024], pO[0][64:128, :])
                    p.op("vector", "tensor_add",
                         OT[0:64, ci * 512:(ci + 1) * 512],
                         ouf2[:, 0:512], pO[1][0:64, :])
                    p.op("vector", "tensor_add",
                         OT[64:128, ci * 512:(ci + 1) * 512],
                         ouf2[:, 512:1024], pO[1][64:128, :], sem=s_dve)
                    fin_dve = s_dve.n
                if g == 1:
                    p.w("sync", s_st)
                    p.dma("sync", Astage[:],
                          self.park_ap(self.Apark, s, 8), s_ld)
                    p.w("vector", s_ld)
                else:
                    p.w("scalar", s_st)
                evw = {}
                p.w("tensor", s_gld, wobc)
                for jc in range(0 if SKIP & 4 else 8):
                    ps = pO[jc % 2]
                    if jc >= 2:
                        p.w("tensor", s_act if g == 0 else s_dve, evw[jc - 2])
                    else:
                        p.w("tensor", s_dve)
                    for ic in range(4):
                        p.op("tensor", "matmul", ps[:],
                             wob[:, ic * D + jc * 128: ic * D + (jc + 1) * 128],
                             OT[:, ic * 512:(ic + 1) * 512],
                             start=(ic == 0), stop=(ic == 3),
                             sem=s_pe if ic == 3 else None)
                    if g == 0:
                        p.w("scalar", s_pe)
                        p.op("scalar", "activation",
                             Astage[:, jc * 512:(jc + 1) * 512], ps[:], AF.Copy,
                             sem=s_act)
                        evw[jc] = s_act.n
                    else:
                        p.w("vector", s_pe)
                        p.op("vector", "scalar_tensor_tensor",
                             Astage[:, jc * 512:(jc + 1) * 512], ps[:],
                             self.bcol(B_BO + jc),
                             Astage[:, jc * 512:(jc + 1) * 512],
                             op0=ALU.add, op1=ALU.add, sem=s_dve)
                        evw[jc] = s_dve.n
                p.w("sync", s_act if g == 0 else s_dve)
                p.dma("sync", self.park_ap(self.Apark, s, 8),
                      Astage[:], s_st)
        self.dump("d_KTg", KTg[:, 0:4096], [128, 4096], BF16)
        self.dump("d_Vg", Vg[:, 0:4160], [128, 4160], BF16)
        self.dump("d_A0", Astage[:], [128, 4096], F32)
        p.w("sync", self.s_st)
        p.barrier("p1_done")
        st.close()

    def phase_p2(self):
        p = self.p
        p.barrier("p2_start")
        st = ExitStack()
        aostage = p.sb("aostage", [128, 6 * QB], F32, stack=st)
        xnT2 = p.sb("xnT2", [128, 8 * T], BF16, stack=st)
        qca = p.sb("qca", [128, 8 * NTO], BF16, stack=st)
        KT2 = p.sb("KT2", [128, 2 * T], BF16, stack=st)
        V2 = p.sb("V2", [128, 32 * 260], BF16, stack=st)
        mslab = p.sb("mslab", [128, 8 * 512], BF16, stack=st)
        qposr = p.sb("qposr", [1, 512], F32, stack=st)
        fchunk = p.sb("p2_fc", [128, 256], F32, stack=st)
        kslab = p.sb("kslab", [128, 8 * 256], BF16, stack=st)
        vslab = p.sb("vslab", [128, 8 * 256], BF16, stack=st)
        fwogb = p.sb("fwogb", [128, 2 * 768], BF16, stack=st)
        OT2 = p.sb("OT2", [128, 2 * 512], BF16, stack=st)
        pT = [p.sb(f"p2_pT{i}", [128, 1024], BF16, stack=st) for i in range(3)]
        sqf = p.sb("p2_sqf", [128, 512], F32, stack=st)
        sqfB = p.sb("p2_sqfB", [128, 512], F32, stack=st)
        ssrow = p.sb("p2_ss", [2, 512], F32, stack=st)
        ssrB = p.sb("p2_ssrB", [2, 512], F32, stack=st)
        ssl = p.sb("p2_ssl", [2, 512], F32, stack=st)
        sslB = p.sb("p2_sslB", [2, 512], F32, stack=st)
        bsb = p.sb("p2_bsb", [128, 512], F32, stack=st)
        bsbB = p.sb("p2_bsbB", [128, 512], F32, stack=st)
        lrow2 = p.sb("p2_lrow", [1, 1024], F32, stack=st)
        linv2 = p.sb("p2_linv", [1, 1024], F32, stack=st)
        ouf2b = p.sb("p2_ou2", [64, 1024], F32, stack=st)
        ouf2 = ouf2b[:, :]
        pS01 = p.psum("p2_pS01", [128, 1024], F32, stack=st)
        pS23 = p.psum("p2_pS23", [128, 1024], F32, stack=st)
        pF = p.psum("p2_pF", [128, 1024], F32, stack=st)
        pSb = [pS01, pS23, pF]
        pO = [p.psum(f"p2_pO{i}", [128, 512], F32, stack=st) for i in range(2)]
        s_ld, s_gld, s_pe, s_act, s_dve, s_st = (self.s_ld, self.s_gld, self.s_pe,
                                                 self.s_act, self.s_dve, self.s_st)

        xldc = {}
        for ic in range(8):
            p.dma("sync", xnT2[:, ic * T:(ic + 1) * T],
                  man_ap(self.xnTpark, ic * T, [[8 * T, 128], [1, T]]), s_ld)
            xldc[ic] = s_ld.n
        p.dma("sync", qca[:], self.qcapark, s_ld)
        for ic in range(8):
            p.w("vector", s_ld, xldc[ic])
            p.op("vector", "tensor_scalar_mul", xnT2[:, ic * T:(ic + 1) * T],
                 xnT2[:, ic * T:(ic + 1) * T], self.bcol(B_N1 + ic),
                 sem=s_dve if ic == 7 else None)

        for g in range(4):
            p.w("gpsimd", s_pe)
            p.dma("gpsimd", kslab[:],
                  man_ap(self.Fwk, g * 256, [[D, 128], [128 * D, 8], [1, 256]]),
                  s_gld)
            p.dma("gpsimd", vslab[:],
                  man_ap(self.Fwv, g * 256, [[D, 128], [128 * D, 8], [1, 256]]),
                  s_gld)
            p.dma("gpsimd", fwogb[:],
                  man_ap(self.Fwo, g * 256 * D + 256, [[D, 128], [128 * D, 2], [1, 768]]), s_gld)
            p.w("tensor", s_ld)
            p.w("tensor", s_gld)
            p.w("tensor", s_dve)
            kps = [pO[0], pO[1], pF[:, 0:512]]
            sqf2 = [sqf, sqfB]
            bsb2 = [bsb, bsbB]
            ssl2 = [ssl, sslB]
            ssrow2 = [ssrow, ssrB]
            evac = {}
            pe_mm, pe_hs, pe_bc = {}, {}, {}
            act_sq, act_ex, act_cp = {}, {}, {}
            for k in range(16 + 2):
                if k < 16:
                    ci, tb = divmod(k, 8)
                    if k >= 3:
                        p.w("tensor", s_dve, evac[k - 3])
                    for ic in range(8):
                        p.op("tensor", "matmul", kps[k % 3][:],
                             kslab[:, ic * 256 + ci * 128: ic * 256 + (ci + 1) * 128],
                             xnT2[:, ic * T + tb * 512: ic * T + tb * 512 + 512],
                             start=(ic == 0), stop=(ic == 7),
                             sem=s_pe if ic == 7 else None)
                    pe_mm[k] = s_pe.n
                if 1 <= k <= 16:
                    j = k - 1
                    p.w("tensor", s_act, act_sq[j])
                    p.op("tensor", "matmul", pSb[j % 2][0:2, 0:512], self.hselb[:],
                         sqf2[j % 2][:], start=True, stop=True, sem=s_pe)
                    pe_hs[j] = s_pe.n
                if k >= 2:
                    i2 = k - 2
                    p.w("tensor", s_act, act_ex[i2])
                    p.op("tensor", "matmul", pSb[i2 % 2][:, 512:1024],
                         self.sel2b[:], ssrow2[i2 % 2][:],
                         start=True, stop=True, sem=s_pe)
                    pe_bc[i2] = s_pe.n
                if k < 16:
                    p.w("scalar", s_pe, pe_mm[k])
                    p.op("scalar", "activation", sqf2[k % 2][:],
                         kps[k % 3][:], AF.Square, sem=s_act)
                    act_sq[k] = s_act.n
                if 1 <= k <= 16:
                    j = k - 1
                    p.w("scalar", s_pe, pe_hs[j])
                    p.op("scalar", "activation", ssl2[j % 2][:],
                         pSb[j % 2][0:2, 0:512], AF.Ln,
                         scale=1.0 / 64.0, bias=EPSR)
                    p.op("scalar", "activation", self.spcr[:], self.spcr[:],
                         AF.Square)
                    p.op("scalar", "activation", ssrow2[j % 2][:],
                         ssl2[j % 2][:], AF.Exp, scale=-0.5, sem=s_act)
                    act_ex[j] = s_act.n
                if k >= 2:
                    i2 = k - 2
                    p.w("scalar", s_pe, pe_bc[i2])
                    if i2 >= 2:
                        p.w("scalar", s_dve, evac[i2 - 2])
                    p.op("scalar", "activation", bsb2[i2 % 2][:],
                         pSb[i2 % 2][:, 512:1024], AF.Copy, sem=s_act)
                    act_cp[i2] = s_act.n
                    ci2, tb2 = divmod(i2, 8)
                    p.w("vector", s_act, act_cp[i2])
                    p.op("vector", "scalar_tensor_tensor",
                         KT2[:, ci2 * T + tb2 * 512: ci2 * T + tb2 * 512 + 512],
                         kps[i2 % 3][:], self.bcol(B_KNW), bsb2[i2 % 2][:],
                         op0=ALU.mult, op1=ALU.mult, sem=s_dve)
                    evac[i2] = s_dve.n
            p.w("tensor", s_dve, evac[15])
            p.op("vector", "memset",
                 man_ap(V2[:, :], 64, [[32 * 260, 128], [260, 32], [65, 4]]), 1.0)
            p.drain("vector", sem=s_dve)
            evac = {}
            for tt in range(32):
                ps = pF
                if tt >= 1:
                    p.w("tensor", s_dve, evac[tt - 1])
                for ic in range(8):
                    p.op("tensor", "matmul", ps[:, 0:256],
                         xnT2[:, ic * T + tt * 128: ic * T + (tt + 1) * 128],
                         vslab[:, ic * 256:(ic + 1) * 256],
                         start=(ic == 0), stop=(ic == 7),
                         sem=s_pe if ic == 7 else None)
                p.w("vector", s_pe)
                p.op("vector", "tensor_single_scalar",
                     man_ap(V2[:, :], tt * 260, [[32 * 260, 128], [65, 4], [1, 64]]),
                     man_ap(pF[:, :], 0, [[1024, 128], [64, 4], [1, 64]]),
                     self.rstd1[:, tt:tt + 1], ALU.mult, sem=s_dve)
                evac[tt] = s_dve.n

            for s in range(4):
                KTN = 8 * s + 8
                p.w("sync", s_pe)
                p.dma("sync", qposr[:], self.qpos[:, s * 512:(s + 1) * 512], s_ld)
                p.w("tensor", s_ld)
                p.w("tensor", s_dve)
                p.op("tensor", "matmul", pO[0][:], self.onesr_f[:], qposr[:],
                     start=True, stop=True, sem=s_pe)
                p.w("vector", s_pe)
                for m in range(8):
                    p.op("vector", "tensor_single_scalar",
                         mslab[:, m * 512:(m + 1) * 512], pO[0][:],
                         self.kcol_s[:, 8 * s + m: 8 * s + m + 1], ALU.is_ge,
                         sem=s_dve if m == 7 else None)
                ou_act = 0
                fin_dve = 0
                prev_expd = None
                for pq in range(2):
                    ci = pq
                    cq = g * 2 + ci
                    h0 = pq * 2
                    expd = {}
                    concnt = {}
                    vcnt = {}

                    def emit_v(kt, pq=pq, h0=h0, vcnt=vcnt, concnt=concnt,
                               KTN=KTN):
                        sm, vv = concnt[kt]
                        if sm is not None:
                            p.w("tensor", sm, vv)
                        if kt == 0:
                            p.w("tensor", s_dve, fin_dve)
                            p.w("tensor", s_act, ou_act)
                            p.w("tensor", s_dve)
                        b3 = kt % 3
                        p.op("tensor", "matmul", pO[0][0:65, :],
                             V2[:, kt * 260 + h0 * 65: kt * 260 + h0 * 65 + 65],
                             pT[b3][:, 0:512],
                             start=(kt == 0), stop=(kt == KTN - 1))
                        p.op("tensor", "matmul", pO[1][0:65, :],
                             V2[:, kt * 260 + (h0 + 1) * 65: kt * 260 + (h0 + 1) * 65 + 65],
                             pT[b3][:, 512:1024],
                             start=(kt == 0), stop=(kt == KTN - 1), sem=s_pe)
                        vcnt[kt] = s_pe.n

                    for kt in range(KTN):
                        masked = kt >= 8 * s
                        pS = pSb[kt % 3]
                        if kt >= 3:
                            p.w("tensor", s_act, expd[kt - 3])
                        elif prev_expd is not None and KTN >= 3:
                            p.w("tensor", s_act, prev_expd[KTN - 3 + kt])
                        else:
                            p.w("tensor", s_act)
                            p.w("tensor", s_dve)
                        p.op("tensor", "matmul", pS[:, 0:512],
                             KT2[0:64, ci * T + kt * 128: ci * T + kt * 128 + 128],
                             qca[0:64, cq * NTO + s * 512: cq * NTO + s * 512 + 512],
                             start=True, stop=True, tile_position=(0, 0))
                        p.op("tensor", "matmul", pS[:, 512:1024],
                             KT2[64:128, ci * T + kt * 128: ci * T + kt * 128 + 128],
                             qca[64:128, cq * NTO + s * 512: cq * NTO + s * 512 + 512],
                             start=True, stop=True, tile_position=(64, 0),
                             sem=s_pe)
                        p.w("scalar", s_pe)
                        if kt >= 3:
                            sm, vv = concnt[kt - 3]
                            if sm is not None:
                                p.w("scalar", sm, vv)
                            p.w("scalar", s_pe, vcnt[kt - 3])
                        p.op("scalar", "activation", pT[kt % 3][:], pS[:],
                             AF.Exp, bias=CSTAB, sem=s_act)
                        expd[kt] = s_act.n
                        if masked:
                            m = kt - 8 * s
                            mt = mslab[:, m * 512:(m + 1) * 512]
                            p.w("vector", s_act)
                            p.op("vector", "tensor_mul", pT[kt % 3][:, 0:512],
                                 pT[kt % 3][:, 0:512], mt)
                            p.op("vector", "tensor_mul", pT[kt % 3][:, 512:1024],
                                 pT[kt % 3][:, 512:1024], mt, sem=s_dve)
                            concnt[kt] = (s_dve, s_dve.n)
                        else:
                            concnt[kt] = (None, None)
                        if kt >= 2:
                            emit_v(kt - 2)
                    if KTN >= 2:
                        emit_v(KTN - 2)
                    emit_v(KTN - 1)
                    prev_expd = expd
                    p.w("scalar", s_pe, vcnt[KTN - 1])
                    p.w("scalar", s_dve, fin_dve)
                    p.op("scalar", "activation", lrow2[0:1, 0:512],
                         pO[0][64:65, :], AF.Copy)
                    p.op("scalar", "activation", lrow2[0:1, 512:1024],
                         pO[1][64:65, :], AF.Copy)
                    p.op("scalar", "activation", ouf2[:, 0:512], pO[0][0:64, :],
                         AF.Copy)
                    p.op("scalar", "activation", lrow2[0:1, 0:1024],
                         lrow2[0:1, 0:1024], AF.Ln)
                    p.op("scalar", "activation", ouf2[:, 512:1024],
                         pO[1][0:64, :], AF.Copy)
                    p.op("scalar", "activation", linv2[0:1, 0:1024],
                         lrow2[0:1, 0:1024], AF.Exp, scale=-1.0, sem=s_act)
                    ou_act = s_act.n
                    p.w("tensor", s_act)
                    p.op("tensor", "matmul", pO[0][0:64, :], self.ones64_f[:],
                         linv2[0:1, 0:512], start=True, stop=True)
                    p.op("tensor", "matmul", pO[0][64:128, :], self.ones64_f[:],
                         linv2[0:1, 512:1024], start=True, stop=True,
                         tile_position=(0, 64), sem=s_pe)
                    p.w("vector", s_act)
                    p.w("vector", s_pe)
                    p.op("vector", "tensor_mul",
                         OT2[0:64, ci * 512:(ci + 1) * 512],
                         ouf2[:, 0:512], pO[0][0:64, :])
                    p.op("vector", "tensor_mul",
                         OT2[64:128, ci * 512:(ci + 1) * 512],
                         ouf2[:, 512:1024], pO[0][64:128, :], sem=s_dve)
                    fin_dve = s_dve.n
                if g > 0:
                    p.w("sync", s_st)
                    p.dma("sync", aostage[:],
                          self.park_ap(self.aopark, s, 6),
                          s_ld)
                    p.w("vector", s_ld)
                else:
                    p.w("scalar", s_st)
                evw = {}
                for jx in range(6):
                    ps = pO[jx % 2]
                    if jx >= 2:
                        p.w("tensor", s_act if g == 0 else s_dve, evw[jx - 2])
                    else:
                        p.w("tensor", s_dve)
                    for ic in range(2):
                        p.op("tensor", "matmul", ps[:],
                             fwogb[:, ic * 768 + jx * 128: ic * 768 + (jx + 1) * 128],
                             OT2[:, ic * 512:(ic + 1) * 512],
                             start=(ic == 0), stop=(ic == 1),
                             sem=s_pe if ic == 1 else None)
                    dst = aostage[:, jx * 512:(jx + 1) * 512]
                    if g == 0:
                        p.w("scalar", s_pe)
                        p.op("scalar", "activation", dst, ps[:], AF.Copy,
                             sem=s_act)
                        evw[jx] = s_act.n
                    else:
                        p.w("vector", s_pe)
                        p.op("vector", "tensor_add", dst, ps[:], dst, sem=s_dve)
                        evw[jx] = s_dve.n
                p.w("sync", s_act if g == 0 else s_dve)
                p.dma("sync", self.park_ap(self.aopark, s, 6),
                      aostage[:], s_st)
        self.dump("d_KT2", KT2[:, 0:4096], [128, 4096], BF16)
        self.dump("d_ao", aostage[:], [128, 3072], F32)
        p.barrier("p2_done")
        st.close()

    def phase_p3(self):
        p = self.p
        s_ld, s_gld, s_pe, s_act, s_dve, s_st = (self.s_ld, self.s_gld, self.s_pe,
                                                 self.s_act, self.s_dve, self.s_st)
        st = ExitStack()
        ptrA = [p.psum(f"p3_tr{i}", [128, 512], F32, stack=st) for i in range(2)]
        xfT = p.sb("xfT", [128, 8 * NTO], BF16, stack=st)
        wtsT = p.sb("wtsT", [4, NTO], F32, stack=st)
        moe = p.sb("moe", [128, 6 * NTO], F32, stack=st)

        stA = ExitStack()
        tok48 = p.sb("tok48", [128, 16 * 768], F32, stack=stA)
        aobuf = p.sb("p3_aob", [128, 6 * QB], F32, stack=stA)
        xnb = p.sb("p3_xnb", [128, D], F32, stack=stA)
        x1b = p.sb("p3_x1b", [128, D], F32, stack=stA)
        xfb = p.sb("p3_xfb", [128, D], BF16, stack=stA)
        sqd = p.sb("p3_sqd", [128, D], F32, stack=stA)
        rws = p.sb("p3_rws", [128, 4], F32, stack=stA)
        ptrB = [p.psum(f"p3_trb{i}", [128, 128], BF16, stack=stA) for i in range(2)]
        evac = {}
        k = 0
        for s in range(4):
            p.w("sync", s_pe)
            p.dma("sync", aobuf[:],
                  self.park_ap(self.aopark, s, 6), s_ld)
            p.w("tensor", s_ld)
            for u in range(4):
                tt = s * 4 + u
                for jx in range(6):
                    b = k % 2
                    if k >= 2:
                        p.w("tensor", s_dve, evac[k - 2])
                    p.op("tensor", "transpose", ptrA[b][:, 0:128],
                         aobuf[:, jx * 512 + u * 128: jx * 512 + u * 128 + 128],
                         self.id32[:], sem=s_pe)
                    p.w("vector", s_pe)
                    p.op("vector", "tensor_copy",
                         tok48[:, tt * 768 + jx * 128: tt * 768 + jx * 128 + 128],
                         ptrA[b][:, 0:128], sem=s_dve)
                    evac[k] = s_dve.n
                    k += 1
        evacX = {}
        evacT = {}
        kT = 0
        for tt in range(16):
            if tt >= 1:
                p.w("sync", s_dve, evacX[tt - 1])
            p.dma("sync", xnb[:], self.xnopark[tt * 128:(tt + 1) * 128, :], s_ld)
            p.w("vector", s_ld)
            p.w("vector", s_st)
            p.op("vector", "tensor_copy", x1b[:], xnb[:])
            p.op("vector", "tensor_add", x1b[:, 256:D], x1b[:, 256:D],
                 tok48[:, tt * 768:(tt + 1) * 768], sem=s_dve)
            evacX[tt] = s_dve.n
            p.w("sync", s_dve)
            p.dma("sync", self.x1park[tt * 128:(tt + 1) * 128, :], x1b[:], s_st)
            p.w("scalar", s_dve)
            p.op("scalar", "activation", sqd[:], x1b[:], AF.Square,
                 accum_out=rws[:, 0:1])
            p.op("scalar", "activation", rws[:, 1:2], rws[:, 0:1], AF.Ln,
                 scale=1.0 / 1024.0, bias=EPSR)
            p.op("scalar", "activation", sqd[:, 0:64], sqd[:, 0:64], AF.Square)
            p.op("scalar", "activation", rws[:, 2:3], rws[:, 1:2], AF.Exp,
                 scale=-0.5, sem=s_act)
            p.w("vector", s_act)
            p.op("vector", "scalar_tensor_tensor", xfb[:], x1b[:], rws[:, 2:3],
                 self.n2bc[:], op0=ALU.mult, op1=ALU.mult, sem=s_dve)
            p.w("tensor", s_dve)
            for c in range(8):
                b = kT % 2
                if kT >= 2:
                    p.w("tensor", s_dve, evacT[kT - 2])
                p.op("tensor", "transpose", ptrB[b][:],
                     xfb[:, c * 128:(c + 1) * 128], self.id16[:], sem=s_pe)
                p.w("vector", s_pe)
                p.op("vector", "tensor_copy",
                     xfT[:, c * NTO + tt * 128: c * NTO + tt * 128 + 128],
                     ptrB[b][:], sem=s_dve)
                evacT[kT] = s_dve.n
                kT += 1
        self.dump("d_xfT", xfT[:, 0:2048], [128, 2048], BF16)
        self.dump("d_x1", x1b[:], [128, 1024], F32)

        gslab = p.sb("gslab", [128, 8 * 4], BF16, stack=stA)
        lg = p.sb("p3_lg", [4, 512], F32, stack=stA)
        g4 = p.sb("p3_g4", [128, 4], F32, stack=stA)
        e4 = p.sb("p3_e4", [128, 4], F32, stack=stA)
        cbuf = p.sb("p3_cb", [128, 16], F32, stack=stA)
        ebuf = p.sb("p3_eb", [128, 16], F32, stack=stA)
        rk = p.sb("p3_rk", [128, 4], F32, stack=stA)
        rk2 = p.sb("p3_rk2", [128, 4], F32, stack=stA)
        w4 = p.sb("p3_w4", [128, 4], F32, stack=stA)
        grow_ = p.sb("p3_gr", [128, 6], F32, stack=stA)
        p.dma("gpsimd", gslab[:],
              self.wsl(self.gwT, 4, 0, 4, 8), s_gld)
        p.w("tensor", s_gld)
        GR_MAX, GR_NM, GR_S, GR_SI, GR_W, GR_WI = range(6)
        for s in range(4):
            p.w("tensor", s_dve)
            for ic in range(8):
                p.op("tensor", "matmul", ptrA[0][0:4, :],
                     gslab[:, ic * 4:(ic + 1) * 4],
                     xfT[:, ic * NTO + s * 512: ic * NTO + s * 512 + 512],
                     start=(ic == 0), stop=(ic == 7),
                     sem=s_pe if ic == 7 else None)
            p.w("scalar", s_pe)
            p.op("scalar", "activation", lg[:], ptrA[0][0:4, :], AF.Copy,
                 sem=s_act)
            for u in range(4):
                tt = s * 4 + u
                p.w("tensor", s_act)
                p.w("tensor", s_dve)
                p.op("tensor", "transpose", ptrA[1][:, 0:4],
                     lg[0:4, u * 128:(u + 1) * 128], self.id32[0:4, 0:4],
                     sem=s_pe)
                p.w("vector", s_pe)
                p.op("vector", "tensor_copy", g4[:], ptrA[1][:, 0:4])
                p.drain("vector")
                p.op("vector", "reduce_max", grow_[:, GR_MAX:GR_MAX + 1], g4[:],
                     AX.X)
                p.drain("vector")
                p.op("vector", "tensor_scalar_mul", grow_[:, GR_NM:GR_NM + 1],
                     grow_[:, GR_MAX:GR_MAX + 1], -1.0)
                p.drain("vector", sem=s_dve)
                p.w("scalar", s_dve)
                p.op("scalar", "activation", e4[:], g4[:], AF.Exp,
                     bias=grow_[:, GR_NM:GR_NM + 1], sem=s_act)
                p.w("vector", s_act)
                p.op("vector", "reduce_sum", grow_[:, GR_S:GR_S + 1], e4[:], AX.X)
                p.drain("vector", sem=s_dve)
                p.w("scalar", s_dve)
                p.op("scalar", "activation", grow_[:, GR_SI:GR_SI + 1],
                     grow_[:, GR_S:GR_S + 1], AF.Ln)
                p.op("scalar", "activation", grow_[:, GR_NM:GR_NM + 1],
                     grow_[:, GR_NM:GR_NM + 1], AF.Copy)
                p.op("scalar", "activation", grow_[:, GR_SI:GR_SI + 1],
                     grow_[:, GR_SI:GR_SI + 1], AF.Exp, scale=-1.0, sem=s_act)
                p.w("vector", s_act)
                p.op("vector", "tensor_scalar_mul", g4[:], e4[:],
                     grow_[:, GR_SI:GR_SI + 1])
                p.drain("vector")
                pa = g4[:].unsqueeze(1).broadcast_to([128, 4, 4])
                pb = g4[:].unsqueeze(2).broadcast_to([128, 4, 4])
                c3 = cbuf[:].rearrange("p (a b) -> p a b", b=4)
                e3_ = ebuf[:].rearrange("p (a b) -> p a b", b=4)
                p.op("vector", "tensor_tensor", c3, pa, pb, ALU.is_gt)
                p.op("vector", "tensor_tensor", e3_, pa, pb, ALU.is_equal)
                p.drain("vector")
                p.op("vector", "tensor_mul", ebuf[:], ebuf[:], self.ltm_s[:])
                p.drain("vector")
                for e in range(4):
                    p.op("vector", "reduce_sum", rk[:, e:e + 1],
                         cbuf[:, 4 * e:4 * e + 4], AX.X)
                    p.op("vector", "reduce_sum", rk2[:, e:e + 1],
                         ebuf[:, 4 * e:4 * e + 4], AX.X)
                p.drain("vector")
                p.op("vector", "tensor_add", rk[:], rk[:], rk2[:])
                p.drain("vector")
                p.op("vector", "tensor_single_scalar", rk[:], rk[:], 1.5,
                     ALU.is_le)
                p.drain("vector")
                p.op("vector", "tensor_mul", w4[:], g4[:], rk[:])
                p.drain("vector")
                p.op("vector", "reduce_sum", grow_[:, GR_W:GR_W + 1], w4[:], AX.X)
                p.drain("vector")
                p.op("vector", "tensor_scalar_add", grow_[:, GR_W:GR_W + 1],
                     grow_[:, GR_W:GR_W + 1], self.cc(C_1E8))
                p.drain("vector", sem=s_dve)
                p.w("scalar", s_dve)
                p.op("scalar", "activation", grow_[:, GR_WI:GR_WI + 1],
                     grow_[:, GR_W:GR_W + 1], AF.Ln)
                p.op("scalar", "activation", grow_[:, GR_NM:GR_NM + 1],
                     grow_[:, GR_NM:GR_NM + 1], AF.Copy)
                p.op("scalar", "activation", grow_[:, GR_WI:GR_WI + 1],
                     grow_[:, GR_WI:GR_WI + 1], AF.Exp, scale=-1.0, sem=s_act)
                p.w("vector", s_act)
                p.op("vector", "tensor_scalar_mul", w4[:], w4[:],
                     grow_[:, GR_WI:GR_WI + 1])
                p.drain("vector", sem=s_dve)
                p.w("tensor", s_dve)
                p.op("tensor", "transpose", ptrA[1][0:4, 0:128], w4[:],
                     self.id32[:], sem=s_pe)
                p.w("vector", s_pe)
                p.op("vector", "tensor_copy", wtsT[0:4, tt * 128:(tt + 1) * 128],
                     ptrA[1][0:4, 0:128], sem=s_dve)
        self.dump("d_wtsT", wtsT[:], [4, NTO], F32)
        p.barrier("p3_gate_done")
        stA.close()

        stB = ExitStack()
        Ht = p.sb("Ht", [128, 16 * 1024], BF16, stack=stB)
        Ht2 = p.sb("Ht2", [128, 16 * 1024], BF16, stack=stB)
        upsl = [p.sb(f"upsl{i}", [128, 16 * 256], BF16, stack=stB) for i in range(2)]
        dnsl = [p.sb(f"dnsl{i}", [128, 16 * 128], BF16, stack=stB) for i in range(2)]
        wbc = p.sb("wbc", [128, NTO], F32, stack=stB)
        tmpd = p.sb("p3_tmpd", [128, 512], F32, stack=stB)
        sel4 = p.sb("sel4s", [4, 512], F32, stack=stB)
        pH = [p.psum(f"p3_pH{i}", [128, 512], F32, stack=stB) for i in range(2)]
        p.dma("sync", sel4[:], self.sel4i, s_ld)
        upcnt = [0, 0]
        dncnt = [0, 0]

        def up_layer(tb, srcT, spitch, sbase, Wd, dst, act, nin, nout=HID):
            nic = nin // 128
            p.w("tensor", s_act)
            p.w("tensor", s_dve)
            ev = {}
            kk = 0
            for jb in range(nout // 256):
                p.w("gpsimd", s_pe, upcnt[jb % 2])
                p.dma("gpsimd", upsl[jb % 2][:, 0:nic * 256],
                      self.wsl(Wd, nout, jb * 256, 256, nic), s_gld)
                nw = s_gld.n
                for jc2 in range(2):
                    jc = jb * 2 + jc2
                    for ts in range(2):
                        b = kk % 2
                        if kk >= 2:
                            p.w("tensor", s_act, ev[kk - 2])
                        if jc2 == 0 and ts == 0:
                            p.w("tensor", s_gld, nw)
                        for ic in range(nic):
                            p.op("tensor", "matmul", pH[b][:],
                                 upsl[jb % 2][:, ic * 256 + jc2 * 128:
                                              ic * 256 + jc2 * 128 + 128],
                                 srcT[:, ic * spitch + sbase + ts * 512:
                                      ic * spitch + sbase + ts * 512 + 512],
                                 start=(ic == 0), stop=(ic == nic - 1),
                                 sem=s_pe if ic == nic - 1 else None)
                        if jc2 == 1 and ts == 1:
                            upcnt[jb % 2] = s_pe.n
                        p.w("scalar", s_pe)
                        p.op("scalar", "activation",
                             dst[:, jc * 1024 + ts * 512: jc * 1024 + ts * 512 + 512],
                             pH[b][:], act, sem=s_act)
                        ev[kk] = s_act.n
                        kk += 1

        def down_layer(tb, e, srcT, Wd, nin):
            nic = nin // 128
            ev = {}
            kk = 0
            for jx in range(6):
                p.w("gpsimd", s_pe, dncnt[jx % 2])
                p.dma("gpsimd", dnsl[jx % 2][:, 0:nic * 128],
                      self.wsl(Wd, D, (jx + 2) * 128, 128, nic), s_gld)
                nw = s_gld.n
                for ts in range(2):
                    b = kk % 2
                    if kk >= 2:
                        p.w("tensor", s_dve, ev[kk - 2])
                    else:
                        p.w("tensor", s_act)
                        p.w("tensor", s_dve)
                    if ts == 0:
                        p.w("tensor", s_gld, nw)
                    for ic in range(nic):
                        p.op("tensor", "matmul", pH[b][:],
                             dnsl[jx % 2][:, ic * 128:(ic + 1) * 128],
                             srcT[:, ic * 1024 + ts * 512:
                                  ic * 1024 + ts * 512 + 512],
                             start=(ic == 0), stop=(ic == nic - 1),
                             sem=s_pe if ic == nic - 1 else None)
                    if ts == 1:
                        dncnt[jx % 2] = s_pe.n
                    p.w("vector", s_pe)
                    col = jx * NTO + tb * 1024 + ts * 512
                    wcol = tb * 1024 + ts * 512
                    dstm = moe[:, col:col + 512]
                    if e == 0:
                        p.op("vector", "tensor_mul", dstm, pH[b][:],
                             wbc[:, wcol:wcol + 512], sem=s_dve)
                    else:
                        p.op("vector", "tensor_mul", tmpd[:], pH[b][:],
                             wbc[:, wcol:wcol + 512])
                        p.op("vector", "tensor_add", dstm, dstm, tmpd[:],
                             sem=s_dve)
                    ev[kk] = s_dve.n
                    kk += 1

        for e, layers in enumerate((None,) * 4):
            for ts4 in range(4):
                p.w("tensor", s_act)
                p.w("tensor", s_dve)
                p.w("tensor", s_ld)
                p.op("tensor", "matmul", pH[0][:], sel4[:, e * 128:(e + 1) * 128],
                     wtsT[0:4, ts4 * 512:(ts4 + 1) * 512], start=True, stop=True,
                     sem=s_pe)
                p.w("scalar", s_pe)
                p.op("scalar", "activation", wbc[:, ts4 * 512:(ts4 + 1) * 512],
                     pH[0][:], AF.Copy, sem=s_act)
            for tb in range(2):
                base = tb * 1024
                if e == 0:
                    up_layer(tb, xfT, NTO, base, self.E0U, Ht, AF.Gelu, D)
                    down_layer(tb, 0, Ht, self.E0D, HID)
                elif e == 1:
                    up_layer(tb, xfT, NTO, base, self.E1A, Ht, AF.Silu, D)
                    up_layer(tb, xfT, NTO, base, self.E1B, Ht2, AF.Copy, D)
                    p.w("vector", s_act)
                    for jc in range(16):
                        for ts in range(2):
                            sl = slice(jc * 1024 + ts * 512,
                                       jc * 1024 + ts * 512 + 512)
                            p.op("vector", "tensor_mul", Ht[:, sl], Ht[:, sl],
                                 Ht2[:, sl],
                                 sem=s_dve if (jc == 15 and ts == 1) else None)
                    down_layer(tb, 1, Ht, self.E1D, HID)
                elif e == 2:
                    up_layer(tb, xfT, NTO, base, self.E2A, Ht, AF.Gelu, D)
                    up_layer(tb, Ht, 1024, 0, self.E2B, Ht2, AF.Gelu, HID)
                    up_layer(tb, Ht2, 1024, 0, self.E2C, Ht, AF.Gelu, HID)
                    down_layer(tb, 2, Ht, self.E2D, HID)
                else:
                    up_layer(tb, xfT, NTO, base, self.E3A, Ht, AF.Gelu, D, nout=D)
                    up_layer(tb, Ht, 1024, 0, self.E3B, Ht2, AF.Gelu, D)
                    down_layer(tb, 3, Ht2, self.E3C, HID)
        self.dump("d_moe", moe[:, 0:2048], [128, 2048], F32)
        p.barrier("p3_moe_done")
        stB.close()

        stC = ExitStack()
        tokm = p.sb("tokm", [128, 16 * 768], F32, stack=stC)
        Abuf = p.sb("p3_Ab", [128, 8 * 512], F32, stack=stC)
        Atok = p.sb("p3_At", [128, 4 * 1024], F32, stack=stC)
        xob = p.sb("p3_xo", [128, D], F32, stack=stC)
        x1b2 = p.sb("p3_x1r", [128, D], F32, stack=stC)
        outb = p.sb("p3_out", [128, D], F32, stack=stC)
        evac = {}
        k = 0
        for jx in range(6):
            for tt in range(16):
                b = k % 2
                if k >= 2:
                    p.w("tensor", s_dve, evac[k - 2])
                p.op("tensor", "transpose", ptrA[b][:, 0:128],
                     moe[:, jx * NTO + tt * 128: jx * NTO + tt * 128 + 128],
                     self.id32[:], sem=s_pe)
                p.w("vector", s_pe)
                p.op("vector", "tensor_copy",
                     tokm[:, tt * 768 + jx * 128: tt * 768 + jx * 128 + 128],
                     ptrA[b][:, 0:128], sem=s_dve)
                evac[k] = s_dve.n
                k += 1
        for s in range(4):
            p.w("sync", s_dve)
            p.dma("sync", Abuf[:],
                  self.park_ap(self.Apark, s, 8), s_ld)
            p.w("tensor", s_ld)
            evac = {}
            k = 0
            for u in range(4):
                for c in range(8):
                    b = k % 2
                    if k >= 2:
                        p.w("tensor", s_dve, evac[k - 2])
                    p.op("tensor", "transpose", ptrA[b][:, 0:128],
                         Abuf[:, c * 512 + u * 128: c * 512 + u * 128 + 128],
                         self.id32[:], sem=s_pe)
                    p.w("vector", s_pe)
                    p.op("vector", "tensor_copy",
                         Atok[:, u * 1024 + c * 128: u * 1024 + c * 128 + 128],
                         ptrA[b][:, 0:128], sem=s_dve)
                    evac[k] = s_dve.n
                    k += 1
            for u in range(4):
                tt = s * 4 + u
                p.w("sync", s_dve)
                p.dma("sync", xob[:], self.xown[tt * 128:(tt + 1) * 128, :], s_ld)
                p.dma("sync", x1b2[:], self.x1park[tt * 128:(tt + 1) * 128, :],
                      s_ld)
                p.w("vector", s_ld)
                p.w("vector", s_st)
                p.op("vector", "tensor_add", outb[:], xob[:], x1b2[:])
                p.op("vector", "tensor_add", outb[:], outb[:],
                     Atok[:, u * 1024:(u + 1) * 1024])
                p.op("vector", "tensor_add", outb[:, 256:D], outb[:, 256:D],
                     tokm[:, tt * 768:(tt + 1) * 768], sem=s_dve)
                p.w("sync", s_dve)
                p.dma("sync", self.yout[tt * 128:(tt + 1) * 128, :], outb[:],
                      s_st)
        p.w("sync", self.s_st)
        stC.close()
        st.close()

    def build_all(self, phases=99):
        p = self.p
        import os
        phases = int(os.environ.get("MK_PHASES", phases))
        st01 = ExitStack()
        self.latT = p.sb("latT", [128, 2 * T], BF16, stack=st01)
        self.qsp = p.sb("qsp", [128, 8 * NTO], BF16, stack=st01)
        self.CS = p.sb("CS", [128, 2 * T], BF16, stack=st01)
        self.bvbc = p.sb("bvbc", [128, D], F32, stack=st01)
        self.mk8 = p.sb("mk8", [128, 8], BF16, stack=st01)
        self.mv02 = p.sb("mv02", [1, NH * 64], F32, stack=st01)
        stCSo = ExitStack()
        CSo = p.sb("CSo", [128, 2 * NTO], BF16, stack=stCSo)
        self.phase_consts()
        if phases >= 1:
            self.phase_big2()
        if phases >= 2:
            self.phase_cs(CSo)
            self.phase_bcasts()
        if phases >= 3:
            st0 = ExitStack()
            stXNT = ExitStack()
            self.phase_p0(st0, stXNT)
        if phases >= 4:
            self.phase_latent()
            p.barrier("xnt_close")
            stXNT.close()
        if phases >= 5:
            self.phase_qprod(CSo, st01)
        if phases >= 3:
            p.barrier("p0_close")
            st0.close()
        stCSo.close()
        if phases >= 6:
            self.phase_p1()
        p.barrier("p01_close")
        st01.close()
        if phases >= 7:
            self.phase_p2()
        if phases >= 8:
            self.phase_p3()
        p.w("sync", self.s_st)
        p.build()
        return self.nc



def _prep_shared(inputs):
    f32 = np.float32
    g = lambda k: np.asarray(inputs[k], dtype=f32)
    out = {}
    out["Wq"] = np.ascontiguousarray(g("L0_attn_Wq"))
    out["Wkd"] = np.ascontiguousarray(g("L0_attn_Wkv_down"))
    kvperm = np.concatenate([np.arange(16) * 128 + j for j in [0]])
    kvperm = np.concatenate(
        [np.concatenate([np.arange(h * 128, h * 128 + 64) for h in range(16)]),
         np.concatenate([np.arange(h * 128 + 64, h * 128 + 128) for h in range(16)])])
    out["Wku"] = np.ascontiguousarray(g("L0_attn_Wkv_up")[:, kvperm])
    out["Wo"] = np.ascontiguousarray(g("L0_attn_Wo"))
    out["bq_pc"] = np.ascontiguousarray(g("L0_attn_bq").reshape(8, 128).T)
    out["bkd_pc"] = np.ascontiguousarray(g("L0_attn_bkv_down").reshape(2, 128).T)
    bku = g("L0_attn_bkv_up")[kvperm]
    out["bkuk_pc"] = np.ascontiguousarray(bku[0:1024].reshape(8, 128).T)
    out["bkuv_row"] = np.ascontiguousarray(bku[1024:2048].reshape(1, 1024))
    out["bo_pc"] = np.ascontiguousarray(g("L0_attn_bo").reshape(8, 128).T)
    out["Fwq"] = np.ascontiguousarray(g("L0_ffn_wq").T)
    out["Fwk"] = np.ascontiguousarray(g("L0_ffn_wk").T)
    out["Fwv"] = np.ascontiguousarray(g("L0_ffn_wv").T)
    out["Fwo"] = np.ascontiguousarray(g("L0_ffn_wo").T)
    out["n1w_pc"] = np.ascontiguousarray(g("L0_ffn_n1_w").reshape(8, 128).T)
    out["n2w_row"] = np.ascontiguousarray(g("L0_ffn_n2_w").reshape(1, 1024))
    out["qnw_pc"] = np.ascontiguousarray(np.tile(g("L0_ffn_qn_w"), 2).reshape(128, 1))
    out["knw_pc"] = np.ascontiguousarray(np.tile(g("L0_ffn_kn_w"), 2).reshape(128, 1))
    out["gwT"] = np.ascontiguousarray(g("L0_ffn_gate_w").T)
    out["E0U"] = np.ascontiguousarray(g("L0_ffn_e0_up").T)
    out["E0D"] = np.ascontiguousarray(g("L0_ffn_e0_down").T)
    out["E1A"] = np.ascontiguousarray(g("L0_ffn_e1_w1").T)
    out["E1B"] = np.ascontiguousarray(g("L0_ffn_e1_w2").T)
    out["E1D"] = np.ascontiguousarray(g("L0_ffn_e1_down").T)
    out["E2A"] = np.ascontiguousarray(g("L0_ffn_e2_l1").T)
    out["E2B"] = np.ascontiguousarray(g("L0_ffn_e2_l2").T)
    out["E2C"] = np.ascontiguousarray(g("L0_ffn_e2_l3").T)
    out["E2D"] = np.ascontiguousarray(g("L0_ffn_e2_l4").T)
    out["E3A"] = np.ascontiguousarray(g("L0_ffn_e3_down").T)
    out["E3B"] = np.ascontiguousarray(g("L0_ffn_e3_up").T)
    out["E3C"] = np.ascontiguousarray(g("L0_ffn_e3_out").T)
    t = np.arange(T, dtype=np.float64)
    out["thi"] = (t // 64).astype(f32).reshape(1, T)
    out["tlo"] = (t % 64).astype(f32).reshape(1, T)
    out["i4r"] = (np.arange(128) % 32).astype(f32).reshape(1, 128)
    out["kcol"] = (np.arange(32)[None, :] * 128
                   + np.arange(128)[:, None]).astype(f32)
    out["I16b"] = np.eye(128, dtype=BF)
    out["I32f"] = np.eye(128, dtype=f32)
    pi = np.zeros((128, 128), dtype=BF)
    for p_ in range(128):
        q = p_ + 32 if (p_ % 64) < 32 else p_ - 32
        pi[p_, q] = 1.0
    out["PIb"] = pi
    sgn = np.where((np.arange(128) % 64) < 32, -1.0, 1.0).astype(f32)
    out["sgn"] = sgn.reshape(128, 1)
    sel2 = np.zeros((2, 128), dtype=f32)
    sel2[0, 0:64] = 1.0
    sel2[1, 64:128] = 1.0
    out["sel2"] = sel2
    out["hsel"] = np.ascontiguousarray(sel2.T)
    ltm = np.zeros((128, 16), dtype=f32)
    for e in range(4):
        for ff in range(4):
            if ff < e:
                ltm[:, 4 * e + ff] = 1.0
    out["ltm16"] = ltm
    sel4 = np.zeros((4, 512), dtype=f32)
    for e in range(4):
        sel4[e, e * 128:(e + 1) * 128] = 1.0
    out["sel4"] = sel4
    return out


def make_in_maps(inputs, cores=8):
    shared = _prep_shared(inputs)
    x = np.asarray(inputs["L0_attn_x"], dtype=np.float32)
    in_maps = []
    for c in range(cores):
        b, half = c // 2, c % 2
        qs = [1024 * s + 512 * half for s in range(NS)]
        own = np.concatenate([np.arange(q, q + QB) for q in qs])
        m = dict(shared)
        m["xfull"] = np.ascontiguousarray(x[b])
        m["xown"] = np.ascontiguousarray(x[b][own])
        tposo = own.astype(np.float64)
        m["thi_o"] = (tposo // 64).astype(np.float32).reshape(1, NTO)
        m["tlo_o"] = (tposo % 64).astype(np.float32).reshape(1, NTO)
        m["qpos"] = tposo.astype(np.float32).reshape(1, NTO)
        m["hsh"] = np.full((128, 1), 512.0 * half, dtype=np.float32)
        in_maps.append(m)
    return in_maps


def kernel(**inputs):
    x = np.asarray(inputs["L0_attn_x"], dtype=np.float32)
    B = x.shape[0]
    mk = MK()
    nc = mk.build_all()
    in_maps = make_in_maps(inputs)
    res = run_bass_kernel_spmd(nc, in_maps, core_ids=list(range(8)))
    out = np.empty((B, T, D), dtype=np.float32)
    for c in range(8):
        b, half = c // 2, c % 2
        y = np.asarray(res.results[c]["yout"])
        for s in range(NS):
            q = 1024 * s + 512 * half
            out[b, q:q + QB] = y[s * QB:(s + 1) * QB]
    return out
